# version 35
# baseline (speedup 1.0000x reference)
import contextlib
import math
import numpy as np
import ml_dtypes
import concourse.bass as bass
import concourse.mybir as mybir
from concourse.bass_utils import run_bass_kernel_spmd
from concourse.bass import IndirectOffsetOnAxis

F32 = mybir.dt.float32
BF16 = mybir.dt.bfloat16
U32 = mybir.dt.uint32
AF = mybir.ActivationFunctionType
ALU = mybir.AluOpType
AX = mybir.AxisListType
BF = ml_dtypes.bfloat16

NCORES = 8
TOK = 4096
TALL = 32768
D = 1024
SDIM = 64
HD = 4096
NE = 8
CAP = 8602.0
L = 1152
QROWS = NE * L
C2 = 2.0 * math.sqrt(2.0 / math.pi)
GC = 0.044715
EPS = 1e-5

ENGS = ("sync", "scalar", "vector", "gpsimd", "tensor")
_NEXP = NE
_NTIL = 3


class Sem:
    def __init__(self, h, name):
        self.h = h
        self.name = name
        self.n = 0


class Prog:
    def __init__(self):
        self.q = {e: [] for e in ENGS}

    def wait(self, eng, sem, val):
        if val and val > 0:
            self.q[eng].append(("wait", sem, int(val)))

    def op(self, eng, fn, inc=None):
        tgt = None
        if inc is not None:
            inc[0].n += inc[1]
            tgt = inc[0].n
        self.q[eng].append(("op", fn, inc))
        return tgt

    def dma(self, eng, sem, out, in_, **kw):
        return self.op(eng, lambda e, o=out, i=in_, k=dict(kw): e.dma_start(out=o, in_=i, **k),
                       inc=(sem, 16))

    def tdma(self, eng, sem, out, in_):
        return self.op(eng, lambda e, o=out, i=in_: e.dma_start(out=o, in_=i, transpose=True),
                       inc=(sem, 16))

    def drain(self, eng):
        self.op(eng, lambda e: e.drain())

    def mark(self, eng, sem):
        return self.op(eng, lambda e: e.drain(), inc=(sem, 1))

    def emit(self, eng_name, eng):
        for item in self.q[eng_name]:
            if item[0] == "wait":
                eng.wait_ge(item[1].h, item[2])
            else:
                inst = item[1](eng)
                if item[2] is not None:
                    assert inst is not None
                    inst.then_inc(item[2][0].h, item[2][1])


def simulate(progs):
    qs = {e: [] for e in ENGS}
    for p in progs:
        for e in ENGS:
            qs[e].extend(p.q[e])
    counts = {}
    ptr = {e: 0 for e in ENGS}
    progress = True
    while progress:
        progress = False
        for e in ENGS:
            while ptr[e] < len(qs[e]):
                item = qs[e][ptr[e]]
                if item[0] == "wait":
                    if counts.get(item[1], 0) < item[2]:
                        break
                elif item[2] is not None:
                    counts[item[2][0]] = counts.get(item[2][0], 0) + item[2][1]
                ptr[e] += 1
                progress = True
    stalled = {e: ptr[e] for e in ENGS if ptr[e] < len(qs[e])}
    if stalled:
        det = []
        for e, p in stalled.items():
            it = qs[e][p]
            det.append(f"{e}@{p}/{len(qs[e])} waits {it[1].name}>={it[2]}"
                       f" (have {counts.get(it[1], 0)})")
        raise RuntimeError("planned-program deadlock: " + "; ".join(det))


def emit_block(nc, progs):
    with nc.Block() as blk:
        @blk.sync
        def _(eng):
            for P in progs:
                P.emit("sync", eng)

        @blk.scalar
        def _(eng):
            for P in progs:
                P.emit("scalar", eng)

        @blk.vector
        def _(eng):
            for P in progs:
                P.emit("vector", eng)

        @blk.gpsimd
        def _(eng):
            for P in progs:
                P.emit("gpsimd", eng)

        @blk.tensor
        def _(eng):
            for P in progs:
                P.emit("tensor", eng)


def build_nc():
    nc = bass.Bass("TRN2", target_bir_lowering=False)
    dt = nc.dram_tensor
    d_x = dt("x", [TALL, D], F32, kind="ExternalInput").ap()
    d_xp = dt("xpair", [TOK, D], F32, kind="ExternalInput").ap()
    d_Bw = dt("Bw", [SDIM, D], F32, kind="ExternalInput").ap()
    d_Cw = dt("Cw", [128, NE * SDIM], F32, kind="ExternalInput").ap()
    d_GW = dt("GW", [D, D], F32, kind="ExternalInput").ap()
    d_gb = dt("gb", [128, NE], F32, kind="ExternalInput").ap()
    d_ld = dt("ldec", [128, NE], F32, kind="ExternalInput").ap()
    d_Wg = dt("Wg", [128, NE * NE], F32, kind="ExternalInput").ap()
    d_W1 = dt("W1", [NE * 128, NE * HD], F32, kind="ExternalInput").ap()
    d_W2 = dt("W2", [NE * 128, 32 * D], F32, kind="ExternalInput").ap()
    d_b1 = dt("b1", [NE * 128, 32], F32, kind="ExternalInput").ap()
    d_b2 = dt("b2", [NE, D], F32, kind="ExternalInput").ap()
    d_Lx = dt("Lx", [128, 128], F32, kind="ExternalInput").ap()
    d_id = dt("ident", [128, 128], F32, kind="ExternalInput").ap()
    d_onr = dt("onesr", [1, 128], F32, kind="ExternalInput").ap()
    d_io8 = dt("iota8", [128, NE], F32, kind="ExternalInput").ap()
    d_ioT = dt("iotaT", [128, TOK], F32, kind="ExternalInput").ap()
    d_ioP = dt("iotaP", [128, TOK], F32, kind="ExternalInput").ap()
    d_big = dt("bigio", [128, 1], U32, kind="ExternalInput").ap()
    d_pvm = dt("pvm", [NE, 256], F32, kind="ExternalInput").ap()
    d_out = dt("out", [TOK, D], F32, kind="ExternalOutput").ap()
    d_xT = dt("xT", [D, TALL], F32, kind="ExternalInput").ap()
    d_xnb = dt("xnb", [TOK, D], BF16, kind="Internal").ap()
    d_xnpb = dt("xnpb", [TOK, D], BF16, kind="Internal").ap()
    d_gwb = dt("gwb", [D, D], BF16, kind="Internal").ap()
    d_wfm = dt("wfm", [D, TOK], BF16, kind="Internal").ap()
    d_gfm = dt("gfm", [D, TOK], BF16, kind="Internal").ap()
    d_sfm = dt("sfm", [D, TOK], BF16, kind="Internal").ap()
    d_xq = dt("xq", [QROWS, D], BF16, kind="Internal").ap()
    d_yq = dt("yq", [QROWS, D], BF16, kind="Internal").ap()
    d_rtm = dt("rtm", [256, NE], F32, kind="Internal").ap()
    d_b2b = dt("b2b", [NE, D], BF16, kind="Internal").ap()
    d_w1b = dt("w1b", [NE * 128, NE * HD], BF16, kind="Internal").ap()
    d_w2b = dt("w2b", [NE * 128, 32 * D], BF16, kind="Internal").ap()

    progs = []
    holders = {}
    with contextlib.ExitStack() as top:
        def mksem(name):
            return Sem(top.enter_context(nc.semaphore(name)), name)

        s_ld = mksem("s_ld")
        s_st = mksem("s_st")
        s_gq = mksem("s_gq")
        s_pe = mksem("s_pe")
        s_dv = mksem("s_dv")
        s_ac = mksem("s_ac")
        s_gp = mksem("s_gp")
        s_sc = mksem("s_sc")

        def sb(stack, name, shape, dtp, **kw):
            return stack.enter_context(nc.sbuf_tensor(name, shape, dtp, **kw))

        c_id = sb(top, "c_id", [128, 128], F32)
        c_onr = sb(top, "c_onr", [1, 128], F32)
        c_onrb = sb(top, "c_onrb", [1, 128], BF16)
        c_io8 = sb(top, "c_io8", [128, NE], F32)
        c_bigf = sb(top, "c_bigf", [128, 1], F32)
        pk_i1 = sb(top, "pk_i1", [128, 32], F32)
        pk_i2 = sb(top, "pk_i2", [128, 32], F32)
        pk_c1 = sb(top, "pk_c1", [128, 32], F32)
        pk_c2 = sb(top, "pk_c2", [128, 32], F32)
        pk_zd = sb(top, "pk_zd", [128, 32], F32)
        pk_rs = sb(top, "pk_rs", [128, 32], F32)
        pk_p1 = sb(top, "pk_p1", [128, 32], F32)
        pk_p2 = sb(top, "pk_p2", [128, 32], F32)
        pk_w1 = sb(top, "pk_w1", [128, 32], F32)
        pk_w2 = sb(top, "pk_w2", [128, 32], F32)
        pk_g1 = sb(top, "pk_g1", [128, 32], U32)
        pk_g2 = sb(top, "pk_g2", [128, 32], U32)
        bc_reg = {}
        c_pvm = sb(top, "c_pvm", [NE, 256], F32)
        r_opref = sb(top, "r_opref", [128, NE], F32)
        t_oh1 = sb(top, "t_oh1", [128, NE * 4], F32)
        t_oh2 = sb(top, "t_oh2", [128, NE * 4], F32)
        t_tmp8 = sb(top, "t_tmp8", [128, NE * 8], F32)
        rt_bin = sb(top, "rt_bin", [NE, 256], F32)
        t_rt128 = sb(top, "t_rt128", [128, NE], F32)
        t_osel = sb(top, "t_osel", [128, 16], F32)
        t_kv = sb(top, "t_kv", [128, 16], F32)
        t_vm = sb(top, "t_vm", [128, 16], U32)
        t_iv = sb(top, "t_iv", [128, 16], F32)
        t_g1c = sb(top, "t_g1c", [128, 32], F32)
        t_g2c = sb(top, "t_g2c", [128, 32], F32)
        v_pref = sb(top, "v_pref", [128, NE * 8], F32)
        t_oprow = sb(top, "t_oprow", [1, NE], F32)
        t_opcol = sb(top, "t_opcol", [NE, 1], F32)
        t_pvmul = sb(top, "t_pvmul", [NE, 256], F32)

        with contextlib.ExitStack() as A:
            c_idb = sb(A, "c_idb", [128, 128], BF16)
            c_Lxb = sb(A, "c_Lxb", [128, 128], BF16)
            c_oncb = sb(A, "c_oncb", [128, 1], BF16)
            rt_base = sb(A, "rt_base", [NE, 32], F32)
            r_tot = sb(A, "r_tot", [1, 2048], F32)
            r_srep = sb(A, "r_srep", [128, NE], F32)
            w_ext = sb(A, "w_ext", [128, 9 * NE], BF16)
            w_BwT = sb(A, "w_BwT", [128, SDIM * NE], BF16)
            w_CwT = sb(A, "w_CwT", [64, 128 * NE], BF16)
            w_GWT = sb(A, "w_GWT", [128, D * NE], BF16)
            t_gwraw = sb(A, "t_gwraw", [128, D * NE], BF16)
            v_lnd = sb(A, "v_lnd", [128, NE], F32)
            v_gb = sb(A, "v_gb", [128, NE], F32)
            t_wgf = sb(A, "t_wgf", [128, NE * NE], F32)
            t_ldf = sb(A, "t_ldf", [128, NE], F32)
            t_dec = sb(A, "t_dec", [128, NE], F32)
            t_bwf = sb(A, "t_bwf", [64, D], F32)
            t_bwb = sb(A, "t_bwb", [64, D], BF16)
            t_cwf = sb(A, "t_cwf", [128, SDIM * NE], F32)
            t_cwb = sb(A, "t_cwb", [128, SDIM * NE], BF16)
            t_lxf = sb(A, "t_lxf", [128, 128], F32)
            t_srow = sb(A, "t_srow", [1, NE], F32)
            t_bigu = sb(A, "t_bigu", [128, 1], U32)
            t_xfm = sb(A, "t_xfm", [128, 512 * NE], BF16)
            t_xff = [sb(A, f"t_xff{p}", [128, 512 * NE], F32)
                     for p in range(2)]
            t_a = sb(A, "t_a", [128, NE * 4], F32)
            t_zsh = sb(A, "t_zsh", [128, NE * 4], F32)
            t_mx = sb(A, "t_mx", [128, NE * 4], F32)
            t_mi = sb(A, "t_mi", [128, NE * 4], U32)
            t_mif = sb(A, "t_mif", [128, 8], F32)
            t_S = sb(A, "t_S", [128, NE * 4], BF16)
            t_x = [sb(A, f"t_x{p}", [128, D * 4], F32) for p in range(2)]
            t_xn = [sb(A, f"t_xn{p}", [128, D * 4], BF16) for p in range(2)]
            t_bnst = sb(A, "t_bnst", [128, 12 * 4], F32)
            t_bnag = sb(A, "t_bnag", [128, 2 * 4], F32)
            t_sd = sb(A, "t_sd", [128, 4], F32)
            t_rst = sb(A, "t_rst", [128, 4], F32)
            t_anfm = [sb(A, f"t_anfm{p}", [128, 512 * NE], BF16, is_transpose_dest=True)
                      for p in range(2)]
            t_bu = sb(A, "t_bu", [64, 512], BF16)
            t_iot = [sb(A, f"t_iot{p}", [128, 512], F32) for p in range(2)]
            t_expo = [sb(A, f"t_expo{p}", [128, 512], F32) for p in range(2)]
            t_pw = [sb(A, f"t_pw{p}", [128, 512], F32) for p in range(2)]
            t_wsb = [sb(A, f"t_wsb{p}", [128, 512], BF16) for p in range(2)]
            t_wpf = [sb(A, f"t_wpf{p}", [128, 512], F32) for p in range(2)]
            t_gsb = [sb(A, f"t_gsb{p}", [128, 512], BF16) for p in range(2)]
            t_scxn = [sb(A, f"t_scxn{p}", [128, D], BF16) for p in range(2)]
            t_idx = [sb(A, f"t_idx{p}", [128, 16], U32) for p in range(2)]
            t_rt32 = sb(A, "t_rt32", [32, NE], F32)
            t_brep = sb(A, "t_brep", [128, 256], F32)
            t_rowb = sb(A, "t_rowb", [1, 256], F32)
            t_eps = sb(A, "t_eps", [128, 1], F32)

            ps_ztm = [A.enter_context(nc.psum_tensor(f"ps_ztm{i}", [128, 36], F32))
                      for i in range(2)]
            ps_tot = A.enter_context(nc.psum_tensor("ps_tot", [1, 256], F32))
            ps_pos = A.enter_context(nc.psum_tensor("ps_pos", [128, 32], F32))
            ps_bu = A.enter_context(nc.psum_tensor("ps_bu", [64, 512], F32))
            prepstack = contextlib.ExitStack()
            ps_tb = prepstack.enter_context(nc.psum_tensor("ps_tb", [128, 128], BF16))

            P = Prog()
            progs.append(P)

            P.dma("sync", s_ld, c_id[:], d_id[:])
            P.dma("sync", s_ld, t_lxf[:], d_Lx[:])
            P.dma("sync", s_ld, c_onr[:], d_onr[:])
            P.dma("sync", s_ld, c_io8[:], d_io8[:])
            P.dma("sync", s_ld, t_bigu[:], d_big[:])
            P.dma("sync", s_ld, c_pvm[:], d_pvm[:])
            v_consts = s_ld.n
            P.dma("scalar", s_st, t_wgf[:], d_Wg[:])
            P.dma("scalar", s_st, t_ldf[:], d_ld[:])
            P.dma("scalar", s_st, v_gb[:], d_gb[:])
            P.dma("scalar", s_st, t_bwf[:], d_Bw[:])
            P.dma("scalar", s_st, t_cwf[:], d_Cw[:])
            v_consts2 = s_st.n

            def _init_bcreg(e):
                bc_reg["r"] = e.alloc_register(name="bcreg")
                return e.reg_mov(bc_reg["r"], QROWS - 1)

            P.op("gpsimd", _init_bcreg)
            P.dma("gpsimd", s_gq, d_gwb.rearrange("(a p) d -> a (p d)", p=128),
                  d_GW.rearrange("(a p) d -> a (p d)", p=128))
            v_gwb = s_gq.n
            P.dma("gpsimd", s_gq, d_b2b[:], d_b2[:])
            v_b2cast = s_gq.n
            wpc = {}
            holders["wpc"] = wpc

            def emit_wprecast(e):
                for q4 in range(4):
                    P.dma("gpsimd", s_gq,
                          d_w1b[e * 128 + q4 * 32:e * 128 + (q4 + 1) * 32, :],
                          d_W1[e * 128 + q4 * 32:e * 128 + (q4 + 1) * 32, :])
                for q4 in range(4):
                    P.dma("gpsimd", s_gq,
                          d_w2b[e * 128 + q4 * 32:e * 128 + (q4 + 1) * 32, :],
                          d_W2[e * 128 + q4 * 32:e * 128 + (q4 + 1) * 32, :])
                wpc[e] = s_gq.n
            holders["emit_wprecast"] = emit_wprecast

            for r in range(NE):
                P.dma("gpsimd", s_gq, t_gwraw[:, r * D:(r + 1) * D],
                      d_gwb[r * 128:(r + 1) * 128, :])
            v_gwraw = s_gq.n
            emit_wprecast(0)

            P.wait("vector", s_ld, v_consts)
            P.wait("vector", s_st, v_consts2)
            P.op("vector", lambda e: e.tensor_copy(out=c_idb[:], in_=c_id[:]))
            P.op("vector", lambda e: e.tensor_copy(out=c_Lxb[:], in_=t_lxf[:]))
            P.op("vector", lambda e: e.memset(c_oncb[:], 1.0))
            P.op("vector", lambda e: e.tensor_copy(out=c_bigf[:], in_=t_bigu[:]))
            P.op("vector", lambda e: e.memset(rt_base[:], 0.0))
            P.op("vector", lambda e: e.memset(t_eps[:], EPS))
            for k in range(NE):
                P.op("vector", lambda e, k=k: e.tensor_copy(
                    out=w_ext[:, k * 9:k * 9 + 8], in_=t_wgf[:, k * NE:(k + 1) * NE]))
                P.op("vector", lambda e, k=k: e.memset(w_ext[:, k * 9 + 8:k * 9 + 9], 1.0))
            P.op("vector", lambda e: e.tensor_copy(out=t_bwb[:], in_=t_bwf[:]))
            P.op("vector", lambda e: e.tensor_copy(out=t_cwb[:], in_=t_cwf[:]))
            v_prep = P.mark("vector", s_dv)

            holders["v_b2cast"] = v_b2cast

            P.wait("scalar", s_st, v_consts2)
            P.op("scalar", lambda e: e.activation(out=t_dec[:], in_=t_ldf[:], func=AF.Sigmoid))
            a_sig = P.mark("scalar", s_ac)
            P.wait("vector", s_ac, a_sig)
            P.op("vector", lambda e: e.tensor_scalar(out=t_dec[:], in0=t_dec[:], scalar1=1e-19,
                                                     scalar2=None, op0=ALU.max))
            v_clamp = P.mark("vector", s_dv)
            P.wait("scalar", s_dv, v_clamp)
            P.op("scalar", lambda e: e.activation(out=v_lnd[:], in_=t_dec[:], func=AF.Ln))
            a_lnd = P.mark("scalar", s_ac)

            P.wait("tensor", s_dv, v_prep)
            for k in range(NE):
                pv = P.op("tensor", lambda e, k=k: e.transpose(
                    out=ps_tb[:, 0:64], in_=t_bwb[:, k * 128:(k + 1) * 128],
                    identity=c_idb[0:64, 0:64]), inc=(s_pe, 1))
                P.wait("vector", s_pe, pv)
                dv = P.op("vector", lambda e, k=k: e.tensor_copy(
                    out=w_BwT[:, k * SDIM:(k + 1) * SDIM], in_=ps_tb[:, 0:64]), inc=(s_dv, 1))
                P.wait("tensor", s_dv, dv)
            for dm in range(NE):
                pv = P.op("tensor", lambda e, dm=dm: e.transpose(
                    out=ps_tb[0:64, 0:128], in_=t_cwb[:, dm * SDIM:(dm + 1) * SDIM],
                    identity=c_idb[:, 0:128]), inc=(s_pe, 1))
                P.wait("vector", s_pe, pv)
                dv = P.op("vector", lambda e, dm=dm: e.tensor_copy(
                    out=w_CwT[:, dm * 128:(dm + 1) * 128], in_=ps_tb[0:64, 0:128]), inc=(s_dv, 1))
                P.wait("tensor", s_dv, dv)
            for k in range(NE):
                P.op("tensor", lambda e, k=k: e.matmul(
                    out=ps_ztm[0][0:1, 0:9], lhsT=c_oncb[:], rhs=w_ext[:, k * 9:(k + 1) * 9],
                    start=(k == 0), stop=(k == NE - 1)))
            pv = P.mark("tensor", s_pe)
            P.wait("vector", s_pe, pv)
            P.op("vector", lambda e: e.tensor_scalar(out=t_srow[:], in0=ps_ztm[0][0:1, 0:NE],
                                                     scalar1=1.0 / D, scalar2=None, op0=ALU.mult))
            dv = P.mark("vector", s_dv)
            P.wait("tensor", s_dv, dv)
            pv = P.op("tensor", lambda e: e.matmul(out=ps_pos[:, 0:NE], lhsT=c_onr[:],
                                                   rhs=t_srow[:], start=True, stop=True),
                      inc=(s_pe, 1))
            P.wait("vector", s_pe, pv)
            v_srep = P.op("vector", lambda e: e.tensor_copy(out=r_srep[:], in_=ps_pos[:, 0:NE]),
                          inc=(s_dv, 1))


            prepstack.close()
            ps_bp = [A.enter_context(nc.psum_tensor(f"ps_bp{i}", [128, 512], F32))
                     for i in range(2)]
            ps_gt = A.enter_context(nc.psum_tensor("ps_gt", [128, 1024], BF16))

            zmm_done = {}
            zfree = {}
            stageF = {}
            totals_pe = {}
            seg_evac = {}
            pos_used = {}
            tot_extra = {}
            xfm_cast = {}
            first_own_mm = [True]

            def routing_chunk(ch):
                par = ch % 2
                own = ch < 8
                seg = ch // 8
                if ch >= 2:
                    P.wait("sync", s_dv, xfm_cast[ch - 2])
                for k in range(NE):
                    P.dma("sync", s_ld, t_xff[par][:, k * 512:(k + 1) * 512],
                          d_xT[k * 128:(k + 1) * 128, ch * 512:(ch + 1) * 512])
                v_x = s_ld.n
                P.wait("vector", s_ld, v_x)
                if ch >= 1:
                    P.wait("vector", s_pe, zmm_done[ch - 1])
                P.op("vector", lambda e, par=par: e.tensor_copy(out=t_xfm[:],
                                                                in_=t_xff[par][:]))
                xfm_cast[ch] = P.mark("vector", s_dv)
                P.wait("tensor", s_dv, xfm_cast[ch])
                if ch == 0:
                    P.wait("tensor", s_dv, v_prep)
                    P.wait("tensor", s_dv, v_srep)
                if ch >= 2:
                    P.wait("tensor", s_dv, zfree[ch - 2])
                for q in range(4):
                    for k in range(NE):
                        P.op("tensor", lambda e, q=q, k=k, par=par: e.matmul(
                            out=ps_ztm[par][:, q * 9:(q + 1) * 9],
                            lhsT=t_xfm[:, k * 512 + q * 128:k * 512 + (q + 1) * 128],
                            rhs=w_ext[:, k * 9:(k + 1) * 9],
                            start=(k == 0), stop=(k == NE - 1)),
                            inc=(s_pe, 1) if (q == 3 and k == NE - 1) else None)
                zmm_done[ch] = s_pe.n
                if ch >= 1:
                    P.wait("vector", s_pe, totals_pe[ch - 1])
                P.wait("vector", s_pe, zmm_done[ch])
                for q in range(4):
                    P.op("vector", lambda e, q=q, par=par: e.tensor_scalar(
                        out=t_a[:, q * 8:(q + 1) * 8], in0=r_srep[:],
                        scalar1=ps_ztm[par][:, q * 9 + 8:q * 9 + 9], scalar2=1.0 / D,
                        op0=ALU.mult, op1=ALU.mult))
                P.drain("vector")
                for q in range(4):
                    P.op("vector", lambda e, q=q, par=par: e.tensor_tensor(
                        out=t_zsh[:, q * 8:(q + 1) * 8], in0=ps_ztm[par][:, q * 9:q * 9 + 8],
                        in1=t_a[:, q * 8:(q + 1) * 8], op=ALU.subtract))
                zfree[ch] = P.mark("vector", s_dv)
                P.drain("vector")
                for q in range(4):
                    P.op("vector", lambda e, q=q: e.max(
                        out=t_mx[:, q * 8:(q + 1) * 8], in_=t_zsh[:, q * 8:(q + 1) * 8]))
                P.drain("vector")
                for q in range(4):
                    P.op("vector", lambda e, q=q: e.max_index(
                        out=t_mi[:, q * 8:(q + 1) * 8], in_max=t_mx[:, q * 8:(q + 1) * 8],
                        in_values=t_zsh[:, q * 8:(q + 1) * 8]))
                P.drain("vector")
                for q in range(4):
                    P.op("vector", lambda e, q=q: e.tensor_copy(
                        out=t_mif[:, q * 2:q * 2 + 1], in_=t_mi[:, q * 8:q * 8 + 1]))
                    P.op("vector", lambda e, q=q: e.tensor_copy(
                        out=t_mif[:, q * 2 + 1:q * 2 + 2], in_=t_mi[:, q * 8 + 1:q * 8 + 2]))
                P.drain("vector")
                for q in range(4):
                    P.op("vector", lambda e, q=q: e.tensor_scalar(
                        out=t_oh1[:, q * 8:(q + 1) * 8], in0=c_io8[:],
                        scalar1=t_mif[:, q * 2:q * 2 + 1], scalar2=None, op0=ALU.is_equal))
                    P.op("vector", lambda e, q=q: e.tensor_scalar(
                        out=t_oh2[:, q * 8:(q + 1) * 8], in0=c_io8[:],
                        scalar1=t_mif[:, q * 2 + 1:q * 2 + 2], scalar2=None, op0=ALU.is_equal))
                P.drain("vector")
                for q in range(4):
                    P.op("vector", lambda e, q=q: e.tensor_tensor(
                        out=t_S[:, q * 8:(q + 1) * 8], in0=t_oh1[:, q * 8:(q + 1) * 8],
                        in1=t_oh2[:, q * 8:(q + 1) * 8], op=ALU.add))
                stageF[ch] = P.mark("vector", s_dv)
                P.wait("tensor", s_dv, stageF[ch])
                if ch % 8 == 0 and seg >= 1:
                    P.wait("tensor", s_dv, seg_evac[seg - 1])
                if ch in tot_extra:
                    P.wait("tensor", s_dv, tot_extra[ch])
                for q in range(4):
                    T = ch * 4 + q
                    P.op("tensor", lambda e, q=q, T=T: e.matmul(
                        out=ps_tot[0:1, (T % 32) * 8:(T % 32) * 8 + 8], lhsT=c_oncb[:],
                        rhs=t_S[:, q * 8:(q + 1) * 8], start=True, stop=True))
                if own:
                    if first_own_mm[0]:
                        P.wait("tensor", s_dv, v_srep)
                        first_own_mm[0] = False
                    if ch >= 1:
                        P.wait("tensor", s_dv, pos_used[ch - 1])
                    for q in range(4):
                        P.op("tensor", lambda e, q=q: e.matmul(
                            out=ps_pos[:, q * 8:(q + 1) * 8], lhsT=c_Lxb[:],
                            rhs=t_S[:, q * 8:(q + 1) * 8], start=True, stop=True))
                totals_pe[ch] = P.mark("tensor", s_pe)
                if own:
                    P.wait("vector", s_pe, totals_pe[ch])
                    for q in range(4):
                        P.op("vector", lambda e, q=q: e.tensor_tensor(
                            out=t_tmp8[:, q * 8:(q + 1) * 8], in0=ps_pos[:, q * 8:(q + 1) * 8],
                            in1=t_oh1[:, q * 8:(q + 1) * 8], op=ALU.mult))
                        P.op("vector", lambda e, q=q: e.tensor_tensor(
                            out=t_tmp8[:, 32 + q * 8:40 + q * 8], in0=ps_pos[:, q * 8:(q + 1) * 8],
                            in1=t_oh2[:, q * 8:(q + 1) * 8], op=ALU.mult))
                    P.drain("vector")
                    for q in range(4):
                        T = ch * 4 + q
                        P.op("vector", lambda e, q=q, T=T: e.reduce_sum(
                            out=pk_c1[:, T:T + 1], in_=t_tmp8[:, q * 8:(q + 1) * 8], axis=AX.X))
                        P.op("vector", lambda e, q=q, T=T: e.reduce_sum(
                            out=pk_c2[:, T:T + 1], in_=t_tmp8[:, 32 + q * 8:40 + q * 8], axis=AX.X))
                        P.op("vector", lambda e, q=q, T=T: e.tensor_tensor(
                            out=pk_zd[:, T:T + 1], in0=t_mx[:, q * 8:q * 8 + 1],
                            in1=t_mx[:, q * 8 + 1:q * 8 + 2], op=ALU.subtract))
                        P.op("vector", lambda e, q=q, T=T: e.tensor_copy(
                            out=pk_i1[:, T:T + 1], in_=t_mi[:, q * 8:q * 8 + 1]))
                        P.op("vector", lambda e, q=q, T=T: e.tensor_copy(
                            out=pk_i2[:, T:T + 1], in_=t_mi[:, q * 8 + 1:q * 8 + 2]))
                    pos_used[ch] = P.mark("vector", s_dv)
                if ch % 8 == 7:
                    P.wait("vector", s_pe, totals_pe[ch])
                    P.op("vector", lambda e, seg=seg: e.tensor_copy(
                        out=r_tot[0:1, seg * 256:(seg + 1) * 256], in_=ps_tot[:]))
                    seg_evac[seg] = P.mark("vector", s_dv)

            ln_apply = {}
            last_sqrt = [0]
            xnb_store = {}

            def ln_chunk(ch, pair):
                par = ch % 2
                src = d_xp if pair else d_x
                dst = d_xnpb if pair else d_xnb
                key = ("p" if pair else "o", ch)
                prev = ("p" if pair else "o", ch - 2)
                if prev in ln_apply:
                    P.wait("scalar", s_dv, ln_apply[prev])
                elif pair and ch < 2:
                    P.wait("scalar", s_dv, ln_apply[("o", 6 + ch)])
                for t in range(4):
                    P.dma("scalar", s_st, t_x[par][:, t * D:(t + 1) * D],
                          src[ch * 512 + t * 128: ch * 512 + (t + 1) * 128, :])
                v_x = s_st.n
                P.wait("vector", s_st, v_x)
                P.wait("vector", s_ac, last_sqrt[0])
                for t in range(4):
                    P.op("vector", lambda e, t=t, par=par: e.bn_stats(
                        out=t_bnst[:, t * 12:t * 12 + 6], in_=t_x[par][:, t * D:t * D + 512]))
                    P.op("vector", lambda e, t=t, par=par: e.bn_stats(
                        out=t_bnst[:, t * 12 + 6:t * 12 + 12],
                        in_=t_x[par][:, t * D + 512:(t + 1) * D]))
                P.drain("vector")
                for t in range(4):
                    P.op("vector", lambda e, t=t: e.bn_aggr(
                        out=t_bnag[:, t * 2:(t + 1) * 2], in_=t_bnst[:, t * 12:(t + 1) * 12]))
                v_agg = P.mark("vector", s_dv)
                P.wait("scalar", s_dv, v_agg)
                for t in range(4):
                    P.op("scalar", lambda e, t=t: e.activation(
                        out=t_sd[:, t:t + 1], in_=t_bnag[:, t * 2 + 1:t * 2 + 2],
                        func=AF.Sqrt, bias=t_eps[:]))
                last_sqrt[0] = P.mark("scalar", s_ac)
                P.wait("vector", s_ac, last_sqrt[0])
                for t in range(4):
                    P.op("vector", lambda e, t=t: e.reciprocal(
                        out=t_rst[:, t:t + 1], in_=t_sd[:, t:t + 1]))
                P.drain("vector")
                for t in range(4):
                    P.op("vector", lambda e, t=t, par=par: e.tensor_scalar(
                        out=t_xn[par][:, t * D:(t + 1) * D], in0=t_x[par][:, t * D:(t + 1) * D],
                        scalar1=t_bnag[:, t * 2:t * 2 + 1], scalar2=t_rst[:, t:t + 1],
                        op0=ALU.subtract, op1=ALU.mult))
                    if not pair:
                        P.op("vector", lambda e, t=t, ch=ch: e.tensor_copy(
                            out=pk_rs[:, ch * 4 + t:ch * 4 + t + 1], in_=t_rst[:, t:t + 1]))
                ln_apply[key] = P.mark("vector", s_dv)
                P.wait("scalar", s_dv, ln_apply[key])
                for t in range(4):
                    P.dma("scalar", s_st, dst[ch * 512 + t * 128: ch * 512 + (t + 1) * 128, :],
                          t_xn[par][:, t * D:(t + 1) * D])
                return s_st.n

            for ch in range(8):
                routing_chunk(ch)
                xnb_store[ch] = ln_chunk(ch, pair=False)

            P.wait("scalar", s_dv, seg_evac[0])
            P.dma("scalar", s_st, d_rtm[0:32, :], r_tot[0:1, 0:256])
            v_rtm = s_st.n
            P.wait("sync", s_st, v_rtm)
            P.dma("sync", s_ld, t_rt32[:], d_rtm[0:32, :])
            v_rt32 = s_ld.n
            P.wait("tensor", s_ld, v_rt32)
            pv = P.op("tensor", lambda e: e.transpose(
                out=ps_bu[0:8, 0:32], in_=t_rt32[:], identity=c_id[0:32, 0:32]), inc=(s_pe, 1))
            P.wait("vector", s_pe, pv)
            P.op("vector", lambda e: e.tensor_copy(out=rt_bin[:, 0:32], in_=ps_bu[0:8, 0:32]))
            P.drain("vector")
            P.op("vector", lambda e: e.tensor_tensor_scan(
                out=rt_base[:, 1:32], data0=rt_bin[:, 0:31], data1=rt_bin[:, 0:31],
                initial=0.0, op0=ALU.add, op1=ALU.bypass))
            v_base = P.mark("vector", s_dv)
            v_bu_free = v_base

            P.wait("tensor", s_dv, v_base)
            P.wait("tensor", s_dv, seg_evac[0])
            for T in range(32):
                P.op("tensor", lambda e, T=T: e.matmul(
                    out=ps_tot[0:1, (T % 32) * 8:(T % 32) * 8 + 8], lhsT=rt_base[:, T:T + 1],
                    rhs=c_id[0:8, 0:8], start=True, stop=True))
            pv = P.mark("tensor", s_pe)
            P.wait("vector", s_pe, pv)
            P.op("vector", lambda e: e.tensor_copy(out=t_rowb[:], in_=ps_tot[:]))
            v_rowb = P.mark("vector", s_dv)
            tot_extra[8] = v_rowb
            P.wait("tensor", s_dv, v_rowb)
            P.wait("tensor", s_dv, pos_used[7])
            v_brep_last = 0
            for half in range(8):
                for T in range(half * 4, (half + 1) * 4):
                    P.op("tensor", lambda e, T=T: e.matmul(
                        out=ps_pos[:, (T % 4) * 8:(T % 4) * 8 + 8], lhsT=c_onr[:],
                        rhs=t_rowb[0:1, T * 8:(T + 1) * 8], start=True, stop=True))
                pv = P.mark("tensor", s_pe)
                P.wait("vector", s_pe, pv)
                for T in range(half * 4, (half + 1) * 4):
                    P.op("vector", lambda e, T=T: e.tensor_copy(
                        out=t_brep[:, T * 8:(T + 1) * 8],
                        in_=ps_pos[:, (T % 4) * 8:(T % 4) * 8 + 8]))
                v_brep_last = P.mark("vector", s_dv)
                if half < 7:
                    P.wait("tensor", s_dv, v_brep_last)

            scxn_used = {}
            scat_batch = {}
            for b8 in range(4):
                tiles = list(range(b8 * 8, (b8 + 1) * 8))
                if b8 >= 2:
                    P.wait("vector", s_gq, scat_batch[b8 - 2])
                for T in tiles:
                    P.op("vector", lambda e, T=T: e.tensor_scalar(
                        out=t_oh1[:, (T % 4) * 8:(T % 4) * 8 + 8], in0=c_io8[:],
                        scalar1=pk_i1[:, T:T + 1], scalar2=None, op0=ALU.is_equal))
                    P.op("vector", lambda e, T=T: e.tensor_scalar(
                        out=t_oh2[:, (T % 4) * 8:(T % 4) * 8 + 8], in0=c_io8[:],
                        scalar1=pk_i2[:, T:T + 1], scalar2=None, op0=ALU.is_equal))
                    if T % 4 == 3:
                        P.drain("vector")
                        for Q in range(T - 3, T + 1):
                            P.op("vector", lambda e, Q=Q: e.tensor_tensor(
                                out=t_tmp8[:, (Q % 4) * 8:(Q % 4) * 8 + 8],
                                in0=t_brep[:, Q * 8:(Q + 1) * 8],
                                in1=t_oh1[:, (Q % 4) * 8:(Q % 4) * 8 + 8], op=ALU.mult))
                            P.op("vector", lambda e, Q=Q: e.tensor_tensor(
                                out=t_tmp8[:, 32 + (Q % 4) * 8:40 + (Q % 4) * 8],
                                in0=t_brep[:, Q * 8:(Q + 1) * 8],
                                in1=t_oh2[:, (Q % 4) * 8:(Q % 4) * 8 + 8], op=ALU.mult))
                        P.drain("vector")
                        for Q in range(T - 3, T + 1):
                            P.op("vector", lambda e, Q=Q: e.reduce_sum(
                                out=t_osel[:, (Q % 8) * 2:(Q % 8) * 2 + 1],
                                in_=t_tmp8[:, (Q % 4) * 8:(Q % 4) * 8 + 8], axis=AX.X))
                            P.op("vector", lambda e, Q=Q: e.reduce_sum(
                                out=t_osel[:, (Q % 8) * 2 + 1:(Q % 8) * 2 + 2],
                                in_=t_tmp8[:, 32 + (Q % 4) * 8:40 + (Q % 4) * 8], axis=AX.X))
                        P.drain("vector")
                        for Q in range(T - 3, T + 1):
                            P.op("vector", lambda e, Q=Q: e.tensor_tensor(
                                out=pk_p1[:, Q:Q + 1], in0=pk_c1[:, Q:Q + 1],
                                in1=t_osel[:, (Q % 8) * 2:(Q % 8) * 2 + 1], op=ALU.add))
                            P.op("vector", lambda e, Q=Q: e.tensor_tensor(
                                out=pk_p2[:, Q:Q + 1], in0=pk_c2[:, Q:Q + 1],
                                in1=t_osel[:, (Q % 8) * 2 + 1:(Q % 8) * 2 + 2], op=ALU.add))
                        P.drain("vector")
                        for Q in range(T - 3, T + 1):
                            for kk, pk_i, pk_p in ((0, pk_i1, pk_p1), (1, pk_i2, pk_p2)):
                                col = (Q % 8) * 2 + kk
                                P.op("vector", lambda e, col=col, Q=Q, pk_i=pk_i, pk_p=pk_p:
                                     e.scalar_tensor_tensor(
                                         out=t_iv[:, col:col + 1], in0=pk_i[:, Q:Q + 1],
                                         scalar=float(L), in1=pk_p[:, Q:Q + 1],
                                         op0=ALU.mult, op1=ALU.add))
                                P.op("vector", lambda e, col=col, Q=Q, pk_p=pk_p: e.tensor_scalar(
                                    out=t_vm[:, col:col + 1], in0=pk_p[:, Q:Q + 1],
                                    scalar1=float(L), scalar2=None, op0=ALU.is_lt))
                        P.drain("vector")
                        for Q in range(T - 3, T + 1):
                            for kk in range(2):
                                col = (Q % 8) * 2 + kk
                                P.op("vector", lambda e, col=col: e.select(
                                    out=t_kv[:, col:col + 1], mask=t_vm[:, col:col + 1],
                                    on_true=t_iv[:, col:col + 1], on_false=c_bigf[:]))
                        P.drain("vector")
                        for Q in range(T - 3, T + 1):
                            for kk in range(2):
                                col = (Q % 8) * 2 + kk
                                P.op("vector", lambda e, col=col, b8=b8: e.tensor_copy(
                                    out=t_idx[b8 % 2][:, col:col + 1], in_=t_kv[:, col:col + 1]))
                dvb = P.mark("vector", s_dv)
                for T in tiles:
                    par = T % 2
                    P.wait("sync", s_st, xnb_store[T // 4])
                    if T >= 2:
                        P.wait("sync", s_gq, scxn_used[T - 2])
                    P.dma("sync", s_ld, t_scxn[par][:], d_xnb[T * 128:(T + 1) * 128, :])
                    v_xn = s_ld.n
                    P.wait("gpsimd", s_dv, dvb)
                    P.wait("gpsimd", s_ld, v_xn)
                    for kk in range(2):
                        P.op("gpsimd", lambda e, T=T, kk=kk, par=par, b8=b8: e.indirect_dma_start(
                            out=d_xq[:], out_offset=IndirectOffsetOnAxis(
                                ap=t_idx[b8 % 2][:, (T % 8) * 2 + kk:(T % 8) * 2 + kk + 1],
                                axis=0),
                            in_=t_scxn[par][:], in_offset=None,
                            bounds_check=bc_reg["r"], oob_is_err=False), inc=(s_gq, 16))
                    scxn_used[T] = s_gq.n
                scat_batch[b8] = s_gq.n
            v_scat_all = s_gq.n

            for ch in range(8):
                ln_chunk(ch, pair=True)
            v_xnpb_all = s_st.n

            P.wait("tensor", s_gq, v_gwraw)
            gwt_dv = {}
            for k in range(NE):
                if k >= 1:
                    P.wait("tensor", s_dv, gwt_dv[k - 1])
                for r in range(NE):
                    pv = P.op("tensor", lambda e, k=k, r=r: e.transpose(
                        out=ps_gt[:, r * 128:(r + 1) * 128],
                        in_=t_gwraw[:, r * D + k * 128:r * D + (k + 1) * 128],
                        identity=c_idb[:, 0:128]), inc=(s_pe, 1) if r == NE - 1 else None)
                P.wait("vector", s_pe, pv)
                gwt_dv[k] = P.op("vector", lambda e, k=k: e.tensor_copy(
                    out=w_GWT[:, k * D:(k + 1) * D], in_=ps_gt[:]), inc=(s_dv, 1))
            P.wait("tensor", s_dv, gwt_dv[NE - 1])

            bumm = {}
            bu_evac = {}
            bp_last = {}
            w_consumed = {}
            pw_done = {}
            g_consumed = {}
            wst = {}
            gst = {}
            anfm_free = {}
            bp_free = {}
            attn_seq = []

            def attn_chunk(ch, pair):
                par = len(attn_seq) % 2
                key = ("p" if pair else "o", ch)
                attn_seq.append(key)
                srcn = d_xnpb if pair else d_xnb
                iosrc = d_ioP if pair else d_ioT
                if len(attn_seq) >= 3:
                    P.wait("sync", s_pe, anfm_free[attn_seq[-3]])
                    P.wait("sync", s_dv, w_consumed[(attn_seq[-3], NE - 1)])
                P.wait("sync", s_st, v_xnpb_all if pair else xnb_store[ch])
                for k in range(NE):
                    P.tdma("sync", s_ld, t_anfm[par][:, k * 512:(k + 1) * 512],
                           srcn[ch * 512:(ch + 1) * 512, k * 128:(k + 1) * 128])
                P.dma("sync", s_ld, t_iot[par][:], iosrc[:, ch * 512:(ch + 1) * 512])
                v_an = s_ld.n
                P.wait("tensor", s_ld, v_an)
                if len(attn_seq) >= 2:
                    P.wait("tensor", s_dv, bu_evac[attn_seq[-2]])
                else:
                    P.wait("tensor", s_dv, v_bu_free)
                for k in range(NE):
                    P.op("tensor", lambda e, k=k, par=par: e.matmul(
                        out=ps_bu[:], lhsT=w_BwT[:, k * SDIM:(k + 1) * SDIM],
                        rhs=t_anfm[par][:, k * 512:(k + 1) * 512],
                        start=(k == 0), stop=(k == NE - 1)))
                bumm[key] = P.mark("tensor", s_pe)
                P.wait("vector", s_pe, bumm[key])
                if len(attn_seq) >= 2:
                    P.wait("vector", s_pe, bp_last[attn_seq[-2]])
                P.op("vector", lambda e: e.tensor_copy(out=t_bu[:], in_=ps_bu[:]))
                bu_evac[key] = P.mark("vector", s_dv)
                P.wait("tensor", s_dv, bu_evac[key])
                prev_free = bp_free.get(attn_seq[-2]) if len(attn_seq) >= 2 else None
                for dm in range(NE):
                    if dm >= 2:
                        P.wait("tensor", s_dv, w_consumed[(key, dm - 2)])
                    elif prev_free is not None:
                        kind, val = prev_free[dm % 2]
                        P.wait("tensor", s_ac if kind == "ac" else s_dv, val)
                    P.op("tensor", lambda e, dm=dm: e.matmul(
                        out=ps_bp[dm % 2][:], lhsT=w_CwT[:, dm * 128:(dm + 1) * 128],
                        rhs=t_bu[:], start=True, stop=True), inc=(s_pe, 1))
                    mmv = s_pe.n
                    if dm == NE - 1:
                        bp_last[key] = mmv
                    if dm >= 2:
                        P.wait("vector", s_ac, pw_done[(key, dm - 2)])
                    P.op("vector", lambda e, dm=dm, par=par: e.tensor_scalar(
                        out=t_expo[dm % 2][:], in0=t_iot[par][:],
                        scalar1=v_lnd[:, dm:dm + 1], scalar2=None, op0=ALU.mult))
                    ev = P.mark("vector", s_dv)
                    P.wait("scalar", s_dv, ev)
                    P.wait("scalar", s_ac, a_lnd)
                    P.op("scalar", lambda e, dm=dm: e.activation(
                        out=t_pw[dm % 2][:], in_=t_expo[dm % 2][:], func=AF.Exp))
                    pw_done[(key, dm)] = P.mark("scalar", s_ac)
                    P.wait("vector", s_ac, pw_done[(key, dm)])
                    P.wait("vector", s_pe, mmv)
                    if not pair:
                        if (key, dm - 2) in wst:
                            P.wait("vector", s_st, wst[(key, dm - 2)])
                        P.op("vector", lambda e, dm=dm: e.tensor_tensor(
                            out=t_wsb[dm % 2][:], in0=t_pw[dm % 2][:], in1=ps_bp[dm % 2][:],
                            op=ALU.mult))
                        w_consumed[(key, dm)] = P.mark("vector", s_dv)
                        P.wait("scalar", s_dv, w_consumed[(key, dm)])
                        P.dma("scalar", s_st,
                              d_wfm[dm * 128:(dm + 1) * 128, ch * 512:(ch + 1) * 512],
                              t_wsb[dm % 2][:])
                        wst[(key, dm)] = s_st.n
                    else:
                        P.op("vector", lambda e, dm=dm: e.tensor_tensor(
                            out=t_wpf[dm % 2][:], in0=t_pw[dm % 2][:], in1=ps_bp[dm % 2][:],
                            op=ALU.mult))
                        P.drain("vector")
                        P.op("vector", lambda e, dm=dm, ch=ch: e.reduce_sum(
                            out=v_pref[:, dm * 8 + ch:dm * 8 + ch + 1], in_=t_wpf[dm % 2][:],
                            axis=AX.X))
                        w_consumed[(key, dm)] = P.mark("vector", s_dv)
                if not pair:
                    for em in range(NE):
                        if em >= 2:
                            P.wait("tensor", s_ac, g_consumed[(key, em - 2)])
                        else:
                            P.wait("tensor", s_dv, w_consumed[(key, NE - 2 + em)])
                        for k in range(NE):
                            P.op("tensor", lambda e, em=em, k=k, par=par: e.matmul(
                                out=ps_bp[em % 2][:],
                                lhsT=w_GWT[:, k * D + em * 128:k * D + (em + 1) * 128],
                                rhs=t_anfm[par][:, k * 512:(k + 1) * 512],
                                start=(k == 0), stop=(k == NE - 1)),
                                inc=(s_pe, 1) if k == NE - 1 else None)
                        mmv = s_pe.n
                        P.wait("scalar", s_pe, mmv)
                        if (key, em - 2) in gst:
                            P.wait("scalar", s_st, gst[(key, em - 2)])
                        P.op("scalar", lambda e, em=em: e.activation(
                            out=t_gsb[em % 2][:], in_=ps_bp[em % 2][:], func=AF.Sigmoid,
                            bias=v_gb[:, em:em + 1]))
                        g_consumed[(key, em)] = P.mark("scalar", s_ac)
                        P.dma("scalar", s_st,
                              d_gfm[em * 128:(em + 1) * 128, ch * 512:(ch + 1) * 512],
                              t_gsb[em % 2][:])
                        gst[(key, em)] = s_st.n
                    anfm_free[key] = s_pe.n
                    bp_free[key] = [("ac", g_consumed[(key, NE - 2)]),
                                    ("ac", g_consumed[(key, NE - 1)])]
                else:
                    anfm_free[key] = bp_last[key]
                    bp_free[key] = [("dv", w_consumed[(key, NE - 2)]),
                                    ("dv", w_consumed[(key, NE - 1)])]

            for ch in range(8):
                attn_chunk(ch, pair=False)
            for ch in range(8):
                attn_chunk(ch, pair=True)
            v_wfm_all = s_st.n
            v_bu_last = bu_evac[attn_seq[-1]]
            v_bp_att_free = bp_free[attn_seq[-1]]

            for ch in range(8, 64):
                routing_chunk(ch)

            P.wait("scalar", s_dv, seg_evac[7])
            P.dma("scalar", s_st, d_rtm[:], r_tot[:])
            barA = {}
            barA["pe"] = P.mark("tensor", s_pe)
            barA["dv"] = P.mark("vector", s_dv)
            barA["ac"] = P.mark("scalar", s_ac)
            barA["gp"] = P.mark("gpsimd", s_gp)
            barA["ld"] = s_ld.n
            barA["st"] = s_st.n
            barA["gq"] = v_scat_all
            barA["sc"] = s_sc.n

        with contextlib.ExitStack() as Bk:
            ts_w = sb(Bk, "ts_w", [128, 4096], BF16)
            ts_g = sb(Bk, "ts_g", [128, 4096], BF16)
            ts_h = sb(Bk, "ts_h", [128, 512], F32)
            ts_s = [sb(Bk, f"ts_s{p}", [128, 512], BF16) for p in range(2)]
            ts_pc = sb(Bk, "ts_pc", [128, 1], F32)
            ts_cy = sb(Bk, "ts_cy", [128, 1], F32)
            t_W1s = sb(Bk, "t_W1s", [128, HD * NE], BF16)
            t_W2s = sb(Bk, "t_W2s", [128, 1024 * 32], BF16)
            t_xqs = sb(Bk, "t_xqs", [128, L * NE], BF16, is_transpose_dest=True)
            t_hsb = sb(Bk, "t_hsb", [128, 384 * 32], BF16)
            t_ysb = [sb(Bk, f"t_ysb{p}", [128, 512], BF16) for p in range(2)]
            t_b1c = [sb(Bk, f"t_b1c{p}", [128, 32], F32) for p in range(2)]
            t_b2r = sb(Bk, "t_b2r", [1, D], BF16)
            t_b2rep = [sb(Bk, f"t_b2rep{p}", [128, D], BF16) for p in range(2)]
            ps_h = [Bk.enter_context(nc.psum_tensor(f"ps_h{i}", [128, 384], F32))
                    for i in range(4)]
            ps_y = [Bk.enter_context(nc.psum_tensor(f"ps_y{i}", [128, 512], F32))
                    for i in range(2)]
            ps_mb = Bk.enter_context(nc.psum_tensor("ps_mb", [128, 512], F32))

            P = Prog()
            progs.append(P)
            for eng in ENGS:
                P.wait(eng, s_pe, barA["pe"])
                P.wait(eng, s_dv, barA["dv"])
                P.wait(eng, s_ac, barA["ac"])
                P.wait(eng, s_gp, barA["gp"])
                P.wait(eng, s_ld, barA["ld"])
                P.wait(eng, s_st, barA["st"])
                P.wait(eng, s_gq, barA["gq"])
                P.wait(eng, s_sc, barA["sc"])

            c2v = P.mark("vector", s_dv)
            P.wait("scalar", s_dv, c2v)

            w1_loaded = {}
            w2_loaded = {}
            mm1_done = {}
            mm2_done = {}
            xq_loaded = {}
            bprep_done = {}
            bfree_ac = {}
            bfree_dv = {}
            prev_hq01 = [0, 0, 0, 0]
            prev_yv01 = [0, 0]
            prev_yst01 = [0, 0]

            def emit_w1loads(e):
                if e >= 1:
                    P.wait("sync", s_pe, mm1_done[e - 1])
                P.wait("sync", s_gq, wpc[e])
                P.dma("sync", s_ld, t_W1s[:], d_w1b[e * 128:(e + 1) * 128, :])
                w1_loaded[e] = s_ld.n

            def emit_w2loads(e):
                if e >= 1:
                    P.wait("scalar", s_pe, mm2_done[e - 1])
                if e == 0:
                    P.wait("scalar", s_gq, wpc[0])
                P.dma("scalar", s_st, t_W2s[:], d_w2b[e * 128:(e + 1) * 128, :])
                w2_loaded[e] = s_st.n

            emit_w1loads(0)
            emit_w2loads(0)

            def emit_bprep(e):
                bp = e % 2
                if e == 0:
                    P.wait("scalar", s_gq, holders["v_b2cast"])
                if e >= 1:
                    P.wait("scalar", s_pe, bprep_bcast[e - 1])
                P.dma("scalar", s_st, t_b1c[bp][:], d_b1[e * 128:(e + 1) * 128, :])
                P.dma("scalar", s_st, t_b2r[:], d_b2b[e:e + 1, :])
                bprep_b1[e] = s_st.n
                P.wait("tensor", s_st, bprep_b1[e])
                if e >= 1:
                    P.wait("tensor", s_dv, bprep_done[e - 1])
                if e == 0:
                    P.op("vector", lambda e_: e_.tensor_copy(out=c_onrb[:], in_=c_onr[:]))
                    dv0 = P.mark("vector", s_dv)
                    P.wait("tensor", s_dv, dv0)
                for half in range(2):
                    pv = P.op("tensor", lambda e_, half=half: e_.matmul(
                        out=ps_mb[:], lhsT=c_onrb[:],
                        rhs=t_b2r[0:1, half * 512:(half + 1) * 512],
                        start=True, stop=True), inc=(s_pe, 1))
                    P.wait("vector", s_pe, pv)
                    P.op("vector", lambda e_, half=half, bp=bp: e_.tensor_copy(
                        out=t_b2rep[bp][:, half * 512:(half + 1) * 512], in_=ps_mb[:]))
                    dv = P.mark("vector", s_dv)
                    if half == 0:
                        P.wait("tensor", s_dv, dv)
                bprep_bcast[e] = pv
                bprep_done[e] = dv

            def emit_xqloads(e):
                if e >= 1:
                    P.wait("sync", s_pe, mm1_done[e - 1])
                for k in range(NE):
                    P.tdma("sync", s_ld, t_xqs[:, k * L:(k + 1) * L],
                           d_xq[e * L:(e + 1) * L, k * 128:(k + 1) * 128])
                xq_loaded[e] = s_ld.n

            ps_aux = Bk.enter_context(nc.psum_tensor("ps_aux", [128, 128], F32))

            def emit_opref_a():
                halves = 0
                for hh in range(2):
                    if hh == 1:
                        P.wait("sync", s_dv, halves)
                    P.dma("sync", s_ld, t_rt128[:], d_rtm[hh * 128:(hh + 1) * 128, :])
                    v_a1 = s_ld.n
                    P.wait("tensor", s_ld, v_a1)
                    if hh == 1:
                        P.wait("tensor", s_dv, halves)
                    pv = P.op("tensor", lambda e_: e_.transpose(
                        out=ps_aux[0:8, 0:128], in_=t_rt128[:], identity=c_id[:, 0:128]),
                        inc=(s_pe, 1))
                    P.wait("vector", s_pe, pv)
                    P.op("vector", lambda e_, hh=hh: e_.tensor_copy(
                        out=rt_bin[:, hh * 128:(hh + 1) * 128], in_=ps_aux[0:8, 0:128]))
                    halves = P.mark("vector", s_dv)
                P.op("vector", lambda e_: e_.tensor_tensor(out=t_pvmul[:], in0=rt_bin[:],
                                                           in1=c_pvm[:], op=ALU.mult))
                P.drain("vector")
                P.op("vector", lambda e_: e_.reduce_sum(out=t_opcol[:], in_=t_pvmul[:],
                                                        axis=AX.X))
                vb = P.mark("vector", s_dv)
                P.wait("tensor", s_dv, vb)
                pv = P.op("tensor", lambda e_: e_.matmul(out=ps_aux[0:1, 0:8], lhsT=t_opcol[:],
                                                         rhs=c_id[0:8, 0:8], start=True,
                                                         stop=True), inc=(s_pe, 1))
                P.wait("vector", s_pe, pv)
                P.op("vector", lambda e_: e_.tensor_copy(out=t_oprow[:], in_=ps_aux[0:1, 0:8]))
                vc = P.mark("vector", s_dv)
                P.wait("tensor", s_dv, vc)
                pv = P.op("tensor", lambda e_: e_.matmul(out=ps_aux[:, 0:8], lhsT=c_onr[:],
                                                         rhs=t_oprow[:], start=True, stop=True),
                          inc=(s_pe, 1))
                P.wait("vector", s_pe, pv)
                P.op("vector", lambda e_: e_.tensor_copy(out=r_opref[:], in_=ps_aux[:, 0:8]))
                return P.mark("vector", s_dv)

            def emit_p3w_act(v_opref):
                P.wait("scalar", s_dv, v_opref)
                for T in range(32):
                    P.op("scalar", lambda e_, T=T: e_.activation(
                        out=t_g1c[:, T:T + 1], in_=pk_zd[:, T:T + 1], func=AF.Sigmoid,
                        scale=pk_rs[:, T:T + 1]))
                a_g1 = P.mark("scalar", s_ac)
                P.wait("vector", s_ac, a_g1)
                P.op("vector", lambda e_: e_.tensor_scalar(out=t_g2c[:], in0=t_g1c[:],
                                                           scalar1=-1.0, scalar2=1.0,
                                                           op0=ALU.mult, op1=ALU.add))
                P.drain("vector")

            def emit_p3w_batch(b8):
                for T in range(b8 * 8, (b8 + 1) * 8):
                    P.op("vector", lambda e_, T=T: e_.tensor_scalar(
                        out=t_oh1[:, (T % 4) * 8:(T % 4) * 8 + 8], in0=c_io8[:],
                        scalar1=pk_i1[:, T:T + 1], scalar2=None, op0=ALU.is_equal))
                    P.op("vector", lambda e_, T=T: e_.tensor_scalar(
                        out=t_oh2[:, (T % 4) * 8:(T % 4) * 8 + 8], in0=c_io8[:],
                        scalar1=pk_i2[:, T:T + 1], scalar2=None, op0=ALU.is_equal))
                    if T % 4 == 3:
                        P.drain("vector")
                        for Q in range(T - 3, T + 1):
                            P.op("vector", lambda e_, Q=Q: e_.tensor_tensor(
                                out=t_tmp8[:, (Q % 4) * 8:(Q % 4) * 8 + 8], in0=r_opref[:],
                                in1=t_oh1[:, (Q % 4) * 8:(Q % 4) * 8 + 8], op=ALU.mult))
                            P.op("vector", lambda e_, Q=Q: e_.tensor_tensor(
                                out=t_tmp8[:, 32 + (Q % 4) * 8:40 + (Q % 4) * 8], in0=r_opref[:],
                                in1=t_oh2[:, (Q % 4) * 8:(Q % 4) * 8 + 8], op=ALU.mult))
                        P.drain("vector")
                        for Q in range(T - 3, T + 1):
                            P.op("vector", lambda e_, Q=Q: e_.reduce_sum(
                                out=t_osel[:, (Q % 8) * 2:(Q % 8) * 2 + 1],
                                in_=t_tmp8[:, (Q % 4) * 8:(Q % 4) * 8 + 8], axis=AX.X))
                            P.op("vector", lambda e_, Q=Q: e_.reduce_sum(
                                out=t_osel[:, (Q % 8) * 2 + 1:(Q % 8) * 2 + 2],
                                in_=t_tmp8[:, 32 + (Q % 4) * 8:40 + (Q % 4) * 8], axis=AX.X))
                        P.drain("vector")
                        for Q in range(T - 3, T + 1):
                            for kk, pk_p in ((0, pk_p1), (1, pk_p2)):
                                col = (Q % 8) * 2 + kk
                                P.op("vector", lambda e_, Q=Q, col=col, pk_p=pk_p: e_.tensor_tensor(
                                    out=t_iv[:, col:col + 1], in0=pk_p[:, Q:Q + 1],
                                    in1=t_osel[:, col:col + 1], op=ALU.add))
                                P.op("vector", lambda e_, Q=Q, col=col, pk_p=pk_p: e_.tensor_scalar(
                                    out=t_vm[:, col:col + 1], in0=pk_p[:, Q:Q + 1],
                                    scalar1=float(L), scalar2=None, op0=ALU.is_lt))
                        P.drain("vector")
                        for Q in range(T - 3, T + 1):
                            for kk in range(2):
                                col = (Q % 8) * 2 + kk
                                P.op("vector", lambda e_, col=col: e_.tensor_scalar(
                                    out=t_kv[:, col:col + 1], in0=t_iv[:, col:col + 1],
                                    scalar1=CAP, scalar2=None, op0=ALU.is_lt))
                        P.drain("vector")
                        for Q in range(T - 3, T + 1):
                            for kk in range(2):
                                col = (Q % 8) * 2 + kk
                                P.op("vector", lambda e_, col=col: e_.tensor_tensor(
                                    out=t_kv[:, col:col + 1], in0=t_kv[:, col:col + 1],
                                    in1=t_vm[:, col:col + 1], op=ALU.mult))
                        P.drain("vector")
                        for Q in range(T - 3, T + 1):
                            for kk, gcol, pk_w in ((0, t_g1c, pk_w1), (1, t_g2c, pk_w2)):
                                col = (Q % 8) * 2 + kk
                                P.op("vector",
                                     lambda e_, Q=Q, col=col, gcol=gcol, pk_w=pk_w: e_.tensor_tensor(
                                         out=pk_w[:, Q:Q + 1], in0=gcol[:, Q:Q + 1],
                                         in1=t_kv[:, col:col + 1], op=ALU.mult))
                            for kk, pk_i, pk_p in ((0, pk_i1, pk_p1), (1, pk_i2, pk_p2)):
                                col = (Q % 8) * 2 + kk
                                P.op("vector",
                                     lambda e_, Q=Q, col=col, pk_i=pk_i, pk_p=pk_p:
                                     e_.scalar_tensor_tensor(
                                         out=t_iv[:, col:col + 1], in0=pk_i[:, Q:Q + 1],
                                         scalar=float(L), in1=pk_p[:, Q:Q + 1],
                                         op0=ALU.mult, op1=ALU.add))
                        P.drain("vector")
                        for Q in range(T - 3, T + 1):
                            for kk in range(2):
                                col = (Q % 8) * 2 + kk
                                P.op("vector", lambda e_, col=col: e_.select(
                                    out=t_kv[:, col:col + 1], mask=t_vm[:, col:col + 1],
                                    on_true=t_iv[:, col:col + 1], on_false=c_bigf[:]))
                        P.drain("vector")
                        for Q in range(T - 3, T + 1):
                            for kk, pk_g in ((0, pk_g1), (1, pk_g2)):
                                col = (Q % 8) * 2 + kk
                                P.op("vector", lambda e_, Q=Q, col=col, pk_g=pk_g: e_.tensor_copy(
                                    out=pk_g[:, Q:Q + 1], in_=t_kv[:, col:col + 1]))
                        P.drain("vector")

            scan_sst = {}
            slab_ld = {}
            scan_done = {}
            scan_half = {}

            def emit_scan_loads(dm):
                if dm >= 1:
                    P.wait("scalar", s_dv, scan_done[dm - 1])
                P.dma("scalar", s_st, ts_w[:], d_wfm[dm * 128:(dm + 1) * 128, :])
                P.dma("scalar", s_st, ts_g[:], d_gfm[dm * 128:(dm + 1) * 128, :])
                slab_ld[dm] = s_st.n

            def emit_scan_dm(dm):
                P.op("vector", lambda e_, dm=dm: e_.reduce_sum(
                    out=ts_pc[:], in_=v_pref[:, dm * 8:(dm + 1) * 8], axis=AX.X))
                P.drain("vector")
                P.op("vector", lambda e_: e_.tensor_copy(out=ts_cy[:], in_=ts_pc[:]))
                P.drain("vector")
                P.wait("vector", s_st, slab_ld[dm])
                for qq in range(8):
                    if (dm, qq - 2) in scan_sst:
                        P.wait("vector", s_st, scan_sst[(dm, qq - 2)])
                    elif dm >= 1 and (dm - 1, qq + 6) in scan_sst:
                        P.wait("vector", s_st, scan_sst[(dm - 1, qq + 6)])
                    P.op("vector", lambda e_: e_.tensor_copy(out=ts_h[:, 0:1], in_=ts_cy[:]))
                    P.op("vector", lambda e_, qq=qq: e_.tensor_tensor_scan(
                        out=ts_h[:, 1:512], data0=ts_w[:, qq * 512:qq * 512 + 511],
                        data1=ts_w[:, qq * 512:qq * 512 + 511],
                        initial=ts_cy[:], op0=ALU.add, op1=ALU.bypass))
                    P.drain("vector")
                    P.op("vector", lambda e_, qq=qq: e_.tensor_tensor(
                        out=ts_cy[:], in0=ts_h[:, 511:512],
                        in1=ts_w[:, qq * 512 + 511:qq * 512 + 512], op=ALU.add))
                    P.op("vector", lambda e_, qq=qq: e_.tensor_tensor(
                        out=ts_s[qq % 2][:], in0=ts_g[:, qq * 512:(qq + 1) * 512],
                        in1=ts_h[:], op=ALU.mult))
                    sd = P.mark("vector", s_dv)
                    P.wait("scalar", s_dv, sd)
                    P.dma("scalar", s_st,
                          d_sfm[dm * 128:(dm + 1) * 128, qq * 512:(qq + 1) * 512],
                          ts_s[qq % 2][:])
                    scan_sst[(dm, qq)] = s_st.n
                scan_done[dm] = P.mark("vector", s_dv)

            bprep_bcast = {}
            bprep_b1 = {}
            emit_bprep(0)
            emit_xqloads(0)
            for e in range(_NEXP):
                bp = e % 2

                for j in range(_NTIL):
                    base = j * 384
                    if j == 0:
                        emit_scan_loads(e)
                    P.wait("tensor", s_ld, w1_loaded[e])
                    P.wait("tensor", s_ld, xq_loaded[e])
                    hq = {}
                    for m in range(32):
                        if m >= 4:
                            P.wait("tensor", s_ac, hq[m - 4])
                        else:
                            P.wait("tensor", s_ac, prev_hq01[m])
                        for k in range(NE):
                            P.op("tensor", lambda e_, m=m, k=k, base=base: e_.matmul(
                                out=ps_h[m % 4][:],
                                lhsT=t_W1s[:, k * HD + m * 128:k * HD + (m + 1) * 128],
                                rhs=t_xqs[:, k * L + base:k * L + base + 384],
                                start=(k == 0), stop=(k == NE - 1)),
                                inc=(s_pe, 1) if k == NE - 1 else None)
                        mmv = s_pe.n
                        P.wait("scalar", s_pe, mmv)
                        if m == 0 and j == 0:
                            P.wait("scalar", s_st, bprep_b1[e])
                        hq[m] = P.op("scalar", lambda e_, m=m, bp=bp: e_.activation(
                            out=t_hsb[:, m * 384:(m + 1) * 384], in_=ps_h[m % 4][:],
                            func=AF.Gelu_apprx_tanh, bias=t_b1c[bp][:, m:m + 1]),
                            inc=(s_ac, 1))
                    prev_hq01 = [hq[28], hq[29], hq[30], hq[31]]
                    if j == 0 and e < _NEXP - 1:
                        emit_bprep(e + 1)
                    if j == 0:
                        for ee in range(e + 1, min(e + 3, _NEXP)):
                            if ee not in holders["wpc"]:
                                holders["emit_wprecast"](ee)
                    if j == _NTIL - 1:
                        mm1_done[e] = s_pe.n
                        if e < _NEXP - 1:
                            emit_w1loads(e + 1)
                            emit_xqloads(e + 1)
                    P.wait("tensor", s_st, w2_loaded[e])
                    yv_inc = {}
                    yst = {}
                    for g in range(6):
                        m2, dh = divmod(g, 2)
                        if g >= 2:
                            P.wait("tensor", s_dv, yv_inc[g - 2])
                        else:
                            P.wait("tensor", s_dv, prev_yv01[g])
                        for k in range(32):
                            if g == 0:
                                P.wait("tensor", s_ac, hq[k])
                            P.op("tensor", lambda e_, m2=m2, dh=dh, k=k, g=g: e_.matmul(
                                out=ps_y[g % 2][:],
                                lhsT=t_hsb[:, k * 384 + m2 * 128:k * 384 + (m2 + 1) * 128],
                                rhs=t_W2s[:, k * 1024 + dh * 512:k * 1024 + (dh + 1) * 512],
                                start=(k == 0), stop=(k == 31)),
                                inc=(s_pe, 1) if k == 31 else None)
                        mmv = s_pe.n
                        P.wait("vector", s_pe, mmv)
                        if g >= 2:
                            P.wait("vector", s_ld, yst[g - 2])
                        else:
                            P.wait("vector", s_ld, prev_yst01[g])
                        yv_inc[g] = P.op("vector", lambda e_, g=g, dh=dh, bp=bp: e_.tensor_tensor(
                            out=t_ysb[g % 2][:], in0=ps_y[g % 2][:],
                            in1=t_b2rep[bp][:, dh * 512:(dh + 1) * 512], op=ALU.add),
                            inc=(s_dv, 1))
                        P.wait("sync", s_dv, yv_inc[g])
                        P.dma("sync", s_ld,
                              d_yq[e * L + base + m2 * 128: e * L + base + (m2 + 1) * 128,
                                   dh * 512:(dh + 1) * 512],
                              t_ysb[g % 2][:])
                        yst[g] = s_ld.n
                    prev_yv01 = [yv_inc[4], yv_inc[5]]
                    prev_yst01 = [yst[4], yst[5]]
                    if j == _NTIL - 1:
                        mm2_done[e] = s_pe.n
                        if e < _NEXP - 1:
                            emit_w2loads(e + 1)
                bfree_ac[e] = s_ac.n
                bfree_dv[e] = s_dv.n
                if e == 0:
                    holders["v_opref"] = emit_opref_a()
                    emit_p3w_act(holders["v_opref"])
                elif 1 <= e <= 4:
                    emit_p3w_batch(e - 1)
                emit_scan_dm(e)

            barB = {}
            barB["pe"] = P.mark("tensor", s_pe)
            barB["dv"] = P.mark("vector", s_dv)
            barB["ac"] = P.mark("scalar", s_ac)
            barB["gp"] = P.mark("gpsimd", s_gp)
            barB["ld"] = s_ld.n
            barB["st"] = s_st.n
            barB["gq"] = v_scat_all
            holders["v_scat_all"] = v_scat_all
            barB["sc"] = s_sc.n

        with contextlib.ExitStack() as Ck:
            t_stm = [sb(Ck, f"t_stm{p}", [128, D], BF16, is_transpose_dest=True)
                     for p in range(2)]
            t_xr = [sb(Ck, f"t_xr{p}", [128, D], F32) for p in range(2)]
            t_A = [sb(Ck, f"t_A{p}", [128, D], BF16) for p in range(2)]
            t_Bt = [sb(Ck, f"t_Bt{p}", [128, D], BF16) for p in range(2)]
            t_f1 = [sb(Ck, f"t_f1{p}", [128, D], F32) for p in range(2)]
            t_f2 = [sb(Ck, f"t_f2{p}", [128, D], F32) for p in range(2)]
            t_g3 = [sb(Ck, f"t_g3{p}", [128, D], F32) for p in range(2)]
            t_sg3 = [sb(Ck, f"t_sg3{p}", [128, D], F32) for p in range(2)]
            t_o3 = [sb(Ck, f"t_o3{p}", [128, D], F32) for p in range(2)]

            P = Prog()
            progs.append(P)
            for eng in ENGS:
                P.wait(eng, s_pe, barB["pe"])
                P.wait(eng, s_dv, barB["dv"])
                P.wait(eng, s_ac, barB["ac"])
                P.wait(eng, s_gp, barB["gp"])
                P.wait(eng, s_ld, barB["ld"])
                P.wait(eng, s_st, barB["st"])
                P.wait(eng, s_gq, barB["gq"])
                P.wait(eng, s_sc, barB["sc"])

            for tile in (t_A, t_Bt):
                for p in range(2):
                    P.op("vector", lambda e_, tile=tile, p=p: e_.memset(tile[p][:], 0.0))
            v_cz = P.mark("vector", s_dv)
            P.wait("gpsimd", s_dv, v_cz)

            f_done = {}
            g_done = {}
            sg_done = {}
            o_done = {}
            st_done = {}
            for T in range(32):
                par = T % 2
                if T >= 2:
                    P.wait("sync", s_dv, sg_done[T - 2])
                    P.wait("sync", s_gp, o_done[T - 2])
                P.tdma("sync", s_ld, t_stm[par][:], d_sfm[:, T * 128:(T + 1) * 128])
                P.dma("sync", s_ld, t_xr[par][:], d_x[T * 128:(T + 1) * 128, :])
                ld_T = s_ld.n
                if T >= 2:
                    P.wait("gpsimd", s_dv, f_done[T - 2])
                for kk, pkg in ((0, pk_g1), (1, pk_g2)):
                    tile = t_A if kk == 0 else t_Bt
                    P.op("gpsimd", lambda e_, T=T, pkg=pkg, tile=tile, par=par:
                         e_.indirect_dma_start(
                             out=tile[par][:], out_offset=None, in_=d_yq[:],
                             in_offset=IndirectOffsetOnAxis(ap=pkg[:, T:T + 1], axis=0),
                             bounds_check=bc_reg["r"], oob_is_err=False), inc=(s_gq, 16))
                gath = s_gq.n
                P.wait("vector", s_gq, gath)
                if T >= 2:
                    P.wait("vector", s_ac, g_done[T - 2])
                P.op("vector", lambda e_, T=T, par=par: e_.tensor_scalar(
                    out=t_f1[par][:], in0=t_A[par][:], scalar1=pk_w1[:, T:T + 1],
                    scalar2=None, op0=ALU.mult))
                P.drain("vector")
                f_done[T] = P.op("vector", lambda e_, T=T, par=par: e_.scalar_tensor_tensor(
                    out=t_f2[par][:], in0=t_Bt[par][:], scalar=pk_w2[:, T:T + 1],
                    in1=t_f1[par][:], op0=ALU.mult, op1=ALU.add), inc=(s_dv, 1))
                P.wait("scalar", s_dv, f_done[T])
                if T >= 2:
                    P.wait("scalar", s_dv, sg_done[T - 2])
                g_done[T] = P.op("scalar", lambda e_, T=T, par=par: e_.activation(
                    out=t_g3[par][:], in_=t_f2[par][:], func=AF.Silu), inc=(s_ac, 1))
                P.wait("vector", s_ac, g_done[T])
                P.wait("vector", s_ld, ld_T)
                if T >= 2:
                    P.wait("vector", s_gp, o_done[T - 2])
                sg_done[T] = P.op("vector", lambda e_, T=T, par=par: e_.tensor_tensor(
                    out=t_sg3[par][:], in0=t_g3[par][:], in1=t_stm[par][:], op=ALU.mult),
                    inc=(s_dv, 1))
                P.wait("gpsimd", s_dv, sg_done[T])
                if T >= 2:
                    P.wait("gpsimd", s_st, st_done[T - 2])
                o_done[T] = P.op("gpsimd", lambda e_, T=T, par=par: e_.tensor_tensor(
                    out=t_o3[par][:], in0=t_sg3[par][:], in1=t_xr[par][:], op=ALU.add),
                    inc=(s_gp, 1))
                P.wait("scalar", s_gp, o_done[T])
                P.dma("scalar", s_st, d_out[T * 128:(T + 1) * 128, :], t_o3[par][:])
                st_done[T] = s_st.n
            P.wait("scalar", s_st, st_done[31])
    emit_block(nc, progs)

    simulate(progs)
    return nc


_NC_CACHE = [None]


def kernel(L0_attn_x, L0_attn_B_w, L0_attn_C_w, L0_attn_gate_w, L0_attn_gate_b,
           L0_attn_log_decay, L0_ffn_Wg, L0_ffn_W1, L0_ffn_b1, L0_ffn_W2, L0_ffn_b2):
    xf = np.ascontiguousarray(np.asarray(L0_attn_x, np.float32).reshape(TALL, D))
    Bw = np.ascontiguousarray(np.asarray(L0_attn_B_w, np.float32))
    Cw = np.ascontiguousarray(np.asarray(L0_attn_C_w, np.float32).reshape(NE, 128, SDIM).transpose(1, 0, 2).reshape(128, NE * SDIM))
    GW = np.ascontiguousarray(np.asarray(L0_attn_gate_w, np.float32))
    gb = np.ascontiguousarray(np.asarray(L0_attn_gate_b, np.float32).reshape(NE, 128).T)
    ldec = np.ascontiguousarray(np.asarray(L0_attn_log_decay, np.float32).reshape(NE, 128).T)
    Wg = np.ascontiguousarray(np.asarray(L0_ffn_Wg, np.float32).reshape(NE, 128, NE).transpose(1, 0, 2).reshape(128, NE * NE))
    W1 = np.ascontiguousarray(np.asarray(L0_ffn_W1, np.float32).reshape(NE, NE, 128, HD).transpose(0, 2, 1, 3).reshape(NE * 128, NE * HD))
    b1 = np.ascontiguousarray(np.asarray(L0_ffn_b1, np.float32).reshape(NE, 32, 128).transpose(0, 2, 1).reshape(NE * 128, 32))
    W2 = np.ascontiguousarray(np.asarray(L0_ffn_W2, np.float32).reshape(NE, 32, 128, D).transpose(0, 2, 1, 3).reshape(NE * 128, 32 * D))
    b2 = np.ascontiguousarray(np.asarray(L0_ffn_b2, np.float32))

    Lx = np.triu(np.ones((128, 128), np.float32), 1)
    ident = np.eye(128, dtype=np.float32)
    onesr = np.ones((1, 128), np.float32)
    iota8 = np.tile(np.arange(NE, dtype=np.float32), (128, 1))
    iotaP = np.ascontiguousarray(np.tile(np.arange(TOK, dtype=np.float32), (128, 1)))
    bigio = ((1 << 20) + np.arange(128, dtype=np.uint32)).reshape(128, 1)

    if _NC_CACHE[0] is None:
        _NC_CACHE[0] = build_nc()
    nc = _NC_CACHE[0]

    in_maps = []
    for c in range(NCORES):
        xrot = np.ascontiguousarray(np.concatenate([xf[c * TOK:], xf[:c * TOK]], axis=0))
        xrotT = np.ascontiguousarray(xrot.T)
        if c % 2 == 1:
            xpair = np.ascontiguousarray(xf[(c - 1) * TOK: c * TOK])
        else:
            xpair = np.zeros((TOK, D), np.float32)
        half = c % 2
        iotaT = np.ascontiguousarray(
            np.tile(np.arange(TOK, dtype=np.float32) + half * TOK, (128, 1)))
        owner = (c + (np.arange(256) // 32)) % NCORES
        pvm = np.ascontiguousarray(
            np.tile((owner < c).astype(np.float32), (NE, 1)))
        in_maps.append({
            "x": xrot, "xT": xrotT, "xpair": xpair, "Bw": Bw, "Cw": Cw, "GW": GW, "gb": gb,
            "ldec": ldec, "Wg": Wg, "W1": W1, "b1": b1, "W2": W2, "b2": b2,
            "Lx": Lx, "ident": ident, "onesr": onesr, "iota8": iota8,
            "iotaT": iotaT, "iotaP": iotaP, "bigio": bigio, "pvm": pvm,
        })

    res = run_bass_kernel_spmd(nc, in_maps, core_ids=list(range(NCORES)))
    out = np.concatenate([res.results[c]["out"] for c in range(NCORES)], axis=0)
    return out.reshape(4, 8192, 1024).astype(np.float32)


# version 36
# speedup vs baseline: 1.1015x; 1.1015x over previous
import contextlib
import math
import numpy as np
import ml_dtypes
import concourse.bass as bass
import concourse.mybir as mybir
from concourse.bass_utils import run_bass_kernel_spmd
from concourse.bass import IndirectOffsetOnAxis

F32 = mybir.dt.float32
BF16 = mybir.dt.bfloat16
U32 = mybir.dt.uint32
AF = mybir.ActivationFunctionType
ALU = mybir.AluOpType
AX = mybir.AxisListType
BF = ml_dtypes.bfloat16

NCORES = 8
TOK = 4096
TALL = 32768
D = 1024
SDIM = 64
HD = 4096
NE = 8
CAP = 8602.0
L = 1152
QROWS = NE * L
C2 = 2.0 * math.sqrt(2.0 / math.pi)
GC = 0.044715
EPS = 1e-5

ENGS = ("sync", "scalar", "vector", "gpsimd", "tensor")
_NEXP = NE
_NTIL = 3


class Sem:
    def __init__(self, h, name):
        self.h = h
        self.name = name
        self.n = 0


class Prog:
    def __init__(self):
        self.q = {e: [] for e in ENGS}

    def wait(self, eng, sem, val):
        if val and val > 0:
            self.q[eng].append(("wait", sem, int(val)))

    def op(self, eng, fn, inc=None):
        tgt = None
        if inc is not None:
            inc[0].n += inc[1]
            tgt = inc[0].n
        self.q[eng].append(("op", fn, inc))
        return tgt

    def dma(self, eng, sem, out, in_, **kw):
        return self.op(eng, lambda e, o=out, i=in_, k=dict(kw): e.dma_start(out=o, in_=i, **k),
                       inc=(sem, 16))

    def tdma(self, eng, sem, out, in_):
        return self.op(eng, lambda e, o=out, i=in_: e.dma_start(out=o, in_=i, transpose=True),
                       inc=(sem, 16))

    def drain(self, eng):
        self.op(eng, lambda e: e.drain())

    def mark(self, eng, sem):
        return self.op(eng, lambda e: e.drain(), inc=(sem, 1))

    def emit(self, eng_name, eng):
        for item in self.q[eng_name]:
            if item[0] == "wait":
                eng.wait_ge(item[1].h, item[2])
            else:
                inst = item[1](eng)
                if item[2] is not None:
                    assert inst is not None
                    inst.then_inc(item[2][0].h, item[2][1])


def simulate(progs):
    qs = {e: [] for e in ENGS}
    for p in progs:
        for e in ENGS:
            qs[e].extend(p.q[e])
    counts = {}
    ptr = {e: 0 for e in ENGS}
    progress = True
    while progress:
        progress = False
        for e in ENGS:
            while ptr[e] < len(qs[e]):
                item = qs[e][ptr[e]]
                if item[0] == "wait":
                    if counts.get(item[1], 0) < item[2]:
                        break
                elif item[2] is not None:
                    counts[item[2][0]] = counts.get(item[2][0], 0) + item[2][1]
                ptr[e] += 1
                progress = True
    stalled = {e: ptr[e] for e in ENGS if ptr[e] < len(qs[e])}
    if stalled:
        det = []
        for e, p in stalled.items():
            it = qs[e][p]
            det.append(f"{e}@{p}/{len(qs[e])} waits {it[1].name}>={it[2]}"
                       f" (have {counts.get(it[1], 0)})")
        raise RuntimeError("planned-program deadlock: " + "; ".join(det))


def emit_block(nc, progs):
    with nc.Block() as blk:
        @blk.sync
        def _(eng):
            for P in progs:
                P.emit("sync", eng)

        @blk.scalar
        def _(eng):
            for P in progs:
                P.emit("scalar", eng)

        @blk.vector
        def _(eng):
            for P in progs:
                P.emit("vector", eng)

        @blk.gpsimd
        def _(eng):
            for P in progs:
                P.emit("gpsimd", eng)

        @blk.tensor
        def _(eng):
            for P in progs:
                P.emit("tensor", eng)


def build_nc():
    nc = bass.Bass("TRN2", target_bir_lowering=False)
    dt = nc.dram_tensor
    d_x = dt("x", [TALL, D], F32, kind="ExternalInput").ap()
    d_xp = dt("xpair", [TOK, D], F32, kind="ExternalInput").ap()
    d_Bw = dt("Bw", [SDIM, D], F32, kind="ExternalInput").ap()
    d_Cw = dt("Cw", [128, NE * SDIM], F32, kind="ExternalInput").ap()
    d_GW = dt("GW", [D, D], F32, kind="ExternalInput").ap()
    d_gb = dt("gb", [128, NE], F32, kind="ExternalInput").ap()
    d_ld = dt("ldec", [128, NE], F32, kind="ExternalInput").ap()
    d_Wg = dt("Wg", [128, NE * NE], F32, kind="ExternalInput").ap()
    d_W1 = dt("W1", [NE * 128, NE * HD], F32, kind="ExternalInput").ap()
    d_W2 = dt("W2", [NE * 128, 32 * D], F32, kind="ExternalInput").ap()
    d_b1 = dt("b1", [NE * 128, 32], F32, kind="ExternalInput").ap()
    d_b2 = dt("b2", [NE, D], F32, kind="ExternalInput").ap()
    d_Lx = dt("Lx", [128, 128], F32, kind="ExternalInput").ap()
    d_id = dt("ident", [128, 128], F32, kind="ExternalInput").ap()
    d_onr = dt("onesr", [1, 128], F32, kind="ExternalInput").ap()
    d_io8 = dt("iota8", [128, NE], F32, kind="ExternalInput").ap()
    d_ioT = dt("iotaT", [128, TOK], F32, kind="ExternalInput").ap()
    d_ioP = dt("iotaP", [128, TOK], F32, kind="ExternalInput").ap()
    d_big = dt("bigio", [128, 1], U32, kind="ExternalInput").ap()
    d_pvm = dt("pvm", [NE, 256], F32, kind="ExternalInput").ap()
    d_out = dt("out", [TOK, D], F32, kind="ExternalOutput").ap()
    d_xT = dt("xT", [D, TALL], F32, kind="ExternalInput").ap()
    d_xnb = dt("xnb", [TOK, D], BF16, kind="Internal").ap()
    d_xnpb = dt("xnpb", [TOK, D], BF16, kind="Internal").ap()
    d_gwb = dt("gwb", [D, D], BF16, kind="Internal").ap()
    d_wfm = dt("wfm", [D, TOK], BF16, kind="Internal").ap()
    d_gfm = dt("gfm", [D, TOK], BF16, kind="Internal").ap()
    d_sfm = dt("sfm", [D, TOK], BF16, kind="Internal").ap()
    d_xq = dt("xq", [QROWS, D], BF16, kind="Internal").ap()
    d_yq = dt("yq", [QROWS, D], BF16, kind="Internal").ap()
    d_rtm = dt("rtm", [256, NE], F32, kind="Internal").ap()
    d_b2b = dt("b2b", [NE, D], BF16, kind="Internal").ap()
    d_w1b = dt("w1b", [NE * 128, NE * HD], BF16, kind="Internal").ap()
    d_w2b = dt("w2b", [NE * 128, 32 * D], BF16, kind="Internal").ap()

    progs = []
    holders = {}
    with contextlib.ExitStack() as top:
        def mksem(name):
            return Sem(top.enter_context(nc.semaphore(name)), name)

        s_ld = mksem("s_ld")
        s_st = mksem("s_st")
        s_gq = mksem("s_gq")
        s_pe = mksem("s_pe")
        s_dv = mksem("s_dv")
        s_ac = mksem("s_ac")
        s_gp = mksem("s_gp")
        s_sc = mksem("s_sc")

        def sb(stack, name, shape, dtp, **kw):
            return stack.enter_context(nc.sbuf_tensor(name, shape, dtp, **kw))

        c_id = sb(top, "c_id", [128, 128], F32)
        c_onr = sb(top, "c_onr", [1, 128], F32)
        c_onrb = sb(top, "c_onrb", [1, 128], BF16)
        c_io8 = sb(top, "c_io8", [128, NE], F32)
        c_bigf = sb(top, "c_bigf", [128, 1], F32)
        pk_i1 = sb(top, "pk_i1", [128, 32], F32)
        pk_i2 = sb(top, "pk_i2", [128, 32], F32)
        pk_c1 = sb(top, "pk_c1", [128, 32], F32)
        pk_c2 = sb(top, "pk_c2", [128, 32], F32)
        pk_zd = sb(top, "pk_zd", [128, 32], F32)
        pk_rs = sb(top, "pk_rs", [128, 32], F32)
        pk_p1 = sb(top, "pk_p1", [128, 32], F32)
        pk_p2 = sb(top, "pk_p2", [128, 32], F32)
        pk_w1 = sb(top, "pk_w1", [128, 32], F32)
        pk_w2 = sb(top, "pk_w2", [128, 32], F32)
        pk_g1 = sb(top, "pk_g1", [128, 32], U32)
        pk_g2 = sb(top, "pk_g2", [128, 32], U32)
        bc_reg = {}
        c_pvm = sb(top, "c_pvm", [NE, 256], F32)
        r_opref = sb(top, "r_opref", [128, NE], F32)
        t_oh1 = sb(top, "t_oh1", [128, NE * 4], F32)
        t_oh2 = sb(top, "t_oh2", [128, NE * 4], F32)
        t_tmp8 = sb(top, "t_tmp8", [128, NE * 8], F32)
        rt_bin = sb(top, "rt_bin", [NE, 256], F32)
        t_rt128 = sb(top, "t_rt128", [128, NE], F32)
        t_osel = sb(top, "t_osel", [128, 16], F32)
        t_kv = sb(top, "t_kv", [128, 16], F32)
        t_vm = sb(top, "t_vm", [128, 16], U32)
        t_iv = sb(top, "t_iv", [128, 16], F32)
        t_g1c = sb(top, "t_g1c", [128, 32], F32)
        t_g2c = sb(top, "t_g2c", [128, 32], F32)
        v_pref = sb(top, "v_pref", [128, NE * 8], F32)
        t_oprow = sb(top, "t_oprow", [1, NE], F32)
        t_opcol = sb(top, "t_opcol", [NE, 1], F32)
        t_pvmul = sb(top, "t_pvmul", [NE, 256], F32)

        with contextlib.ExitStack() as A:
            c_idb = sb(A, "c_idb", [128, 128], BF16)
            c_Lxb = sb(A, "c_Lxb", [128, 128], BF16)
            c_oncb = sb(A, "c_oncb", [128, 1], BF16)
            rt_base = sb(A, "rt_base", [NE, 32], F32)
            r_tot = sb(A, "r_tot", [1, 2048], F32)
            r_srep = sb(A, "r_srep", [128, NE], F32)
            w_ext = sb(A, "w_ext", [128, 9 * NE], BF16)
            w_BwT = sb(A, "w_BwT", [128, SDIM * NE], BF16)
            w_CwT = sb(A, "w_CwT", [64, 128 * NE], BF16)
            w_GWT = sb(A, "w_GWT", [128, D * NE], BF16)
            t_gwraw = sb(A, "t_gwraw", [128, D * NE], BF16)
            v_lnd = sb(A, "v_lnd", [128, NE], F32)
            v_gb = sb(A, "v_gb", [128, NE], F32)
            t_wgf = sb(A, "t_wgf", [128, NE * NE], F32)
            t_ldf = sb(A, "t_ldf", [128, NE], F32)
            t_dec = sb(A, "t_dec", [128, NE], F32)
            t_bwf = sb(A, "t_bwf", [64, D], F32)
            t_bwb = sb(A, "t_bwb", [64, D], BF16)
            t_cwf = sb(A, "t_cwf", [128, SDIM * NE], F32)
            t_cwb = sb(A, "t_cwb", [128, SDIM * NE], BF16)
            t_lxf = sb(A, "t_lxf", [128, 128], F32)
            t_srow = sb(A, "t_srow", [1, NE], F32)
            t_bigu = sb(A, "t_bigu", [128, 1], U32)
            t_xfm = sb(A, "t_xfm", [128, 512 * NE], BF16)
            t_xff = [sb(A, f"t_xff{p}", [128, 512 * NE], F32)
                     for p in range(2)]
            t_a = sb(A, "t_a", [128, NE * 4], F32)
            t_zsh = sb(A, "t_zsh", [128, NE * 4], F32)
            t_mx = sb(A, "t_mx", [128, NE * 4], F32)
            t_mi = sb(A, "t_mi", [128, NE * 4], U32)
            t_mif = sb(A, "t_mif", [128, 8], F32)
            t_S = sb(A, "t_S", [128, NE * 4], BF16)
            t_x = [sb(A, f"t_x{p}", [128, D * 4], F32) for p in range(2)]
            t_xn = [sb(A, f"t_xn{p}", [128, D * 4], BF16) for p in range(2)]
            t_bnst = sb(A, "t_bnst", [128, 12 * 4], F32)
            t_bnag = sb(A, "t_bnag", [128, 2 * 4], F32)
            t_sd = sb(A, "t_sd", [128, 4], F32)
            t_rst = sb(A, "t_rst", [128, 4], F32)
            t_anfm = [sb(A, f"t_anfm{p}", [128, 512 * NE], BF16, is_transpose_dest=True)
                      for p in range(2)]
            t_bu = sb(A, "t_bu", [64, 512], BF16)
            t_iot = [sb(A, f"t_iot{p}", [128, 512], F32) for p in range(2)]
            t_expo = [sb(A, f"t_expo{p}", [128, 512], F32) for p in range(2)]
            t_pw = [sb(A, f"t_pw{p}", [128, 512], F32) for p in range(2)]
            t_wsb = [sb(A, f"t_wsb{p}", [128, 512], BF16) for p in range(2)]
            t_wpf = [sb(A, f"t_wpf{p}", [128, 512], F32) for p in range(2)]
            t_gsb = [sb(A, f"t_gsb{p}", [128, 512], BF16) for p in range(2)]
            t_scxn = [sb(A, f"t_scxn{p}", [128, D], BF16) for p in range(2)]
            t_idx = [sb(A, f"t_idx{p}", [128, 16], U32) for p in range(2)]
            t_rt32 = sb(A, "t_rt32", [32, NE], F32)
            t_brep = sb(A, "t_brep", [128, 256], F32)
            t_rowb = sb(A, "t_rowb", [1, 256], F32)
            t_eps = sb(A, "t_eps", [128, 1], F32)

            ps_ztm = [A.enter_context(nc.psum_tensor(f"ps_ztm{i}", [128, 36], F32))
                      for i in range(2)]
            ps_tot = A.enter_context(nc.psum_tensor("ps_tot", [1, 256], F32))
            ps_pos = A.enter_context(nc.psum_tensor("ps_pos", [128, 32], F32))
            ps_bu = A.enter_context(nc.psum_tensor("ps_bu", [64, 512], F32))
            prepstack = contextlib.ExitStack()
            ps_tb = prepstack.enter_context(nc.psum_tensor("ps_tb", [128, 128], BF16))

            P = Prog()
            progs.append(P)

            P.dma("sync", s_ld, c_id[:], d_id[:])
            P.dma("sync", s_ld, t_lxf[:], d_Lx[:])
            P.dma("sync", s_ld, c_onr[:], d_onr[:])
            P.dma("sync", s_ld, c_io8[:], d_io8[:])
            P.dma("sync", s_ld, t_bigu[:], d_big[:])
            P.dma("sync", s_ld, c_pvm[:], d_pvm[:])
            v_consts = s_ld.n
            P.dma("scalar", s_st, t_wgf[:], d_Wg[:])
            P.dma("scalar", s_st, t_ldf[:], d_ld[:])
            P.dma("scalar", s_st, v_gb[:], d_gb[:])
            P.dma("scalar", s_st, t_bwf[:], d_Bw[:])
            P.dma("scalar", s_st, t_cwf[:], d_Cw[:])
            v_consts2 = s_st.n

            def _init_bcreg(e):
                bc_reg["r"] = e.alloc_register(name="bcreg")
                return e.reg_mov(bc_reg["r"], QROWS - 1)

            P.op("gpsimd", _init_bcreg)
            P.dma("gpsimd", s_gq, d_gwb.rearrange("(a p) d -> a (p d)", p=128),
                  d_GW.rearrange("(a p) d -> a (p d)", p=128))
            v_gwb = s_gq.n
            P.dma("gpsimd", s_gq, d_b2b[:], d_b2[:])
            v_b2cast = s_gq.n
            wpc = {}
            holders["wpc"] = wpc

            def emit_wprecast(e):
                for q4 in range(4):
                    P.dma("gpsimd", s_gq,
                          d_w1b[e * 128 + q4 * 32:e * 128 + (q4 + 1) * 32, :],
                          d_W1[e * 128 + q4 * 32:e * 128 + (q4 + 1) * 32, :])
                for q4 in range(4):
                    P.dma("gpsimd", s_gq,
                          d_w2b[e * 128 + q4 * 32:e * 128 + (q4 + 1) * 32, :],
                          d_W2[e * 128 + q4 * 32:e * 128 + (q4 + 1) * 32, :])
                wpc[e] = s_gq.n
            holders["emit_wprecast"] = emit_wprecast

            for r in range(NE):
                P.dma("gpsimd", s_gq, t_gwraw[:, r * D:(r + 1) * D],
                      d_gwb[r * 128:(r + 1) * 128, :])
            v_gwraw = s_gq.n
            emit_wprecast(0)

            P.wait("vector", s_ld, v_consts)
            P.wait("vector", s_st, v_consts2)
            P.op("vector", lambda e: e.tensor_copy(out=c_idb[:], in_=c_id[:]))
            P.op("vector", lambda e: e.tensor_copy(out=c_Lxb[:], in_=t_lxf[:]))
            P.op("vector", lambda e: e.memset(c_oncb[:], 1.0))
            P.op("vector", lambda e: e.tensor_copy(out=c_bigf[:], in_=t_bigu[:]))
            P.op("vector", lambda e: e.memset(rt_base[:], 0.0))
            P.op("vector", lambda e: e.memset(t_eps[:], EPS))
            for k in range(NE):
                P.op("vector", lambda e, k=k: e.tensor_copy(
                    out=w_ext[:, k * 9:k * 9 + 8], in_=t_wgf[:, k * NE:(k + 1) * NE]))
                P.op("vector", lambda e, k=k: e.memset(w_ext[:, k * 9 + 8:k * 9 + 9], 1.0))
            P.op("vector", lambda e: e.tensor_copy(out=t_bwb[:], in_=t_bwf[:]))
            P.op("vector", lambda e: e.tensor_copy(out=t_cwb[:], in_=t_cwf[:]))
            v_prep = P.mark("vector", s_dv)

            holders["v_b2cast"] = v_b2cast

            P.wait("scalar", s_st, v_consts2)
            P.op("scalar", lambda e: e.activation(out=t_dec[:], in_=t_ldf[:], func=AF.Sigmoid))
            a_sig = P.mark("scalar", s_ac)
            P.wait("vector", s_ac, a_sig)
            P.op("vector", lambda e: e.tensor_scalar(out=t_dec[:], in0=t_dec[:], scalar1=1e-19,
                                                     scalar2=None, op0=ALU.max))
            v_clamp = P.mark("vector", s_dv)
            P.wait("scalar", s_dv, v_clamp)
            P.op("scalar", lambda e: e.activation(out=v_lnd[:], in_=t_dec[:], func=AF.Ln))
            a_lnd = P.mark("scalar", s_ac)

            P.wait("tensor", s_dv, v_prep)
            for k in range(NE):
                pv = P.op("tensor", lambda e, k=k: e.transpose(
                    out=ps_tb[:, 0:64], in_=t_bwb[:, k * 128:(k + 1) * 128],
                    identity=c_idb[0:64, 0:64]), inc=(s_pe, 1))
                P.wait("vector", s_pe, pv)
                dv = P.op("vector", lambda e, k=k: e.tensor_copy(
                    out=w_BwT[:, k * SDIM:(k + 1) * SDIM], in_=ps_tb[:, 0:64]), inc=(s_dv, 1))
                P.wait("tensor", s_dv, dv)
            for dm in range(NE):
                pv = P.op("tensor", lambda e, dm=dm: e.transpose(
                    out=ps_tb[0:64, 0:128], in_=t_cwb[:, dm * SDIM:(dm + 1) * SDIM],
                    identity=c_idb[:, 0:128]), inc=(s_pe, 1))
                P.wait("vector", s_pe, pv)
                dv = P.op("vector", lambda e, dm=dm: e.tensor_copy(
                    out=w_CwT[:, dm * 128:(dm + 1) * 128], in_=ps_tb[0:64, 0:128]), inc=(s_dv, 1))
                P.wait("tensor", s_dv, dv)
            for k in range(NE):
                P.op("tensor", lambda e, k=k: e.matmul(
                    out=ps_ztm[0][0:1, 0:9], lhsT=c_oncb[:], rhs=w_ext[:, k * 9:(k + 1) * 9],
                    start=(k == 0), stop=(k == NE - 1)))
            pv = P.mark("tensor", s_pe)
            P.wait("vector", s_pe, pv)
            P.op("vector", lambda e: e.tensor_scalar(out=t_srow[:], in0=ps_ztm[0][0:1, 0:NE],
                                                     scalar1=1.0 / D, scalar2=None, op0=ALU.mult))
            dv = P.mark("vector", s_dv)
            P.wait("tensor", s_dv, dv)
            pv = P.op("tensor", lambda e: e.matmul(out=ps_pos[:, 0:NE], lhsT=c_onr[:],
                                                   rhs=t_srow[:], start=True, stop=True),
                      inc=(s_pe, 1))
            P.wait("vector", s_pe, pv)
            v_srep = P.op("vector", lambda e: e.tensor_copy(out=r_srep[:], in_=ps_pos[:, 0:NE]),
                          inc=(s_dv, 1))


            prepstack.close()
            ps_bp = [A.enter_context(nc.psum_tensor(f"ps_bp{i}", [128, 512], F32))
                     for i in range(2)]
            ps_gt = A.enter_context(nc.psum_tensor("ps_gt", [128, 1024], BF16))

            zmm_done = {}
            zfree = {}
            stageF = {}
            totals_pe = {}
            seg_evac = {}
            pos_used = {}
            tot_extra = {}
            xfm_cast = {}
            first_own_mm = [True]

            def routing_chunk(ch):
                par = ch % 2
                own = ch < 8
                seg = ch // 8
                if ch >= 2:
                    P.wait("sync", s_dv, xfm_cast[ch - 2])
                for k in range(NE):
                    P.dma("sync", s_ld, t_xff[par][:, k * 512:(k + 1) * 512],
                          d_xT[k * 128:(k + 1) * 128, ch * 512:(ch + 1) * 512])
                v_x = s_ld.n
                P.wait("vector", s_ld, v_x)
                if ch >= 1:
                    P.wait("vector", s_pe, zmm_done[ch - 1])
                P.op("vector", lambda e, par=par: e.tensor_copy(out=t_xfm[:],
                                                                in_=t_xff[par][:]))
                xfm_cast[ch] = P.mark("vector", s_dv)
                P.wait("tensor", s_dv, xfm_cast[ch])
                if ch == 0:
                    P.wait("tensor", s_dv, v_prep)
                    P.wait("tensor", s_dv, v_srep)
                if ch >= 2:
                    P.wait("tensor", s_dv, zfree[ch - 2])
                for q in range(4):
                    for k in range(NE):
                        P.op("tensor", lambda e, q=q, k=k, par=par: e.matmul(
                            out=ps_ztm[par][:, q * 9:(q + 1) * 9],
                            lhsT=t_xfm[:, k * 512 + q * 128:k * 512 + (q + 1) * 128],
                            rhs=w_ext[:, k * 9:(k + 1) * 9],
                            start=(k == 0), stop=(k == NE - 1)),
                            inc=(s_pe, 1) if (q == 3 and k == NE - 1) else None)
                zmm_done[ch] = s_pe.n
                if ch >= 1:
                    P.wait("vector", s_pe, totals_pe[ch - 1])
                P.wait("vector", s_pe, zmm_done[ch])
                for q in range(4):
                    P.op("vector", lambda e, q=q, par=par: e.tensor_scalar(
                        out=t_a[:, q * 8:(q + 1) * 8], in0=r_srep[:],
                        scalar1=ps_ztm[par][:, q * 9 + 8:q * 9 + 9], scalar2=1.0 / D,
                        op0=ALU.mult, op1=ALU.mult))
                P.drain("vector")
                for q in range(4):
                    P.op("vector", lambda e, q=q, par=par: e.tensor_tensor(
                        out=t_zsh[:, q * 8:(q + 1) * 8], in0=ps_ztm[par][:, q * 9:q * 9 + 8],
                        in1=t_a[:, q * 8:(q + 1) * 8], op=ALU.subtract))
                zfree[ch] = P.mark("vector", s_dv)
                P.drain("vector")
                for q in range(4):
                    P.op("vector", lambda e, q=q: e.max(
                        out=t_mx[:, q * 8:(q + 1) * 8], in_=t_zsh[:, q * 8:(q + 1) * 8]))
                P.drain("vector")
                for q in range(4):
                    P.op("vector", lambda e, q=q: e.max_index(
                        out=t_mi[:, q * 8:(q + 1) * 8], in_max=t_mx[:, q * 8:(q + 1) * 8],
                        in_values=t_zsh[:, q * 8:(q + 1) * 8]))
                P.drain("vector")
                for q in range(4):
                    P.op("vector", lambda e, q=q: e.tensor_copy(
                        out=t_mif[:, q * 2:q * 2 + 1], in_=t_mi[:, q * 8:q * 8 + 1]))
                    P.op("vector", lambda e, q=q: e.tensor_copy(
                        out=t_mif[:, q * 2 + 1:q * 2 + 2], in_=t_mi[:, q * 8 + 1:q * 8 + 2]))
                P.drain("vector")
                for q in range(4):
                    P.op("vector", lambda e, q=q: e.tensor_scalar(
                        out=t_oh1[:, q * 8:(q + 1) * 8], in0=c_io8[:],
                        scalar1=t_mif[:, q * 2:q * 2 + 1], scalar2=None, op0=ALU.is_equal))
                    P.op("vector", lambda e, q=q: e.tensor_scalar(
                        out=t_oh2[:, q * 8:(q + 1) * 8], in0=c_io8[:],
                        scalar1=t_mif[:, q * 2 + 1:q * 2 + 2], scalar2=None, op0=ALU.is_equal))
                P.drain("vector")
                for q in range(4):
                    P.op("vector", lambda e, q=q: e.tensor_tensor(
                        out=t_S[:, q * 8:(q + 1) * 8], in0=t_oh1[:, q * 8:(q + 1) * 8],
                        in1=t_oh2[:, q * 8:(q + 1) * 8], op=ALU.add))
                stageF[ch] = P.mark("vector", s_dv)
                P.wait("tensor", s_dv, stageF[ch])
                if ch % 8 == 0 and seg >= 1:
                    P.wait("tensor", s_dv, seg_evac[seg - 1])
                if ch in tot_extra:
                    P.wait("tensor", s_dv, tot_extra[ch])
                for q in range(4):
                    T = ch * 4 + q
                    P.op("tensor", lambda e, q=q, T=T: e.matmul(
                        out=ps_tot[0:1, (T % 32) * 8:(T % 32) * 8 + 8], lhsT=c_oncb[:],
                        rhs=t_S[:, q * 8:(q + 1) * 8], start=True, stop=True))
                if own:
                    if first_own_mm[0]:
                        P.wait("tensor", s_dv, v_srep)
                        first_own_mm[0] = False
                    if ch >= 1:
                        P.wait("tensor", s_dv, pos_used[ch - 1])
                    for q in range(4):
                        P.op("tensor", lambda e, q=q: e.matmul(
                            out=ps_pos[:, q * 8:(q + 1) * 8], lhsT=c_Lxb[:],
                            rhs=t_S[:, q * 8:(q + 1) * 8], start=True, stop=True))
                totals_pe[ch] = P.mark("tensor", s_pe)
                if own:
                    P.wait("vector", s_pe, totals_pe[ch])
                    for q in range(4):
                        P.op("vector", lambda e, q=q: e.tensor_tensor(
                            out=t_tmp8[:, q * 8:(q + 1) * 8], in0=ps_pos[:, q * 8:(q + 1) * 8],
                            in1=t_oh1[:, q * 8:(q + 1) * 8], op=ALU.mult))
                        P.op("vector", lambda e, q=q: e.tensor_tensor(
                            out=t_tmp8[:, 32 + q * 8:40 + q * 8], in0=ps_pos[:, q * 8:(q + 1) * 8],
                            in1=t_oh2[:, q * 8:(q + 1) * 8], op=ALU.mult))
                    P.drain("vector")
                    for q in range(4):
                        T = ch * 4 + q
                        P.op("vector", lambda e, q=q, T=T: e.reduce_sum(
                            out=pk_c1[:, T:T + 1], in_=t_tmp8[:, q * 8:(q + 1) * 8], axis=AX.X))
                        P.op("vector", lambda e, q=q, T=T: e.reduce_sum(
                            out=pk_c2[:, T:T + 1], in_=t_tmp8[:, 32 + q * 8:40 + q * 8], axis=AX.X))
                        P.op("vector", lambda e, q=q, T=T: e.tensor_tensor(
                            out=pk_zd[:, T:T + 1], in0=t_mx[:, q * 8:q * 8 + 1],
                            in1=t_mx[:, q * 8 + 1:q * 8 + 2], op=ALU.subtract))
                        P.op("vector", lambda e, q=q, T=T: e.tensor_copy(
                            out=pk_i1[:, T:T + 1], in_=t_mi[:, q * 8:q * 8 + 1]))
                        P.op("vector", lambda e, q=q, T=T: e.tensor_copy(
                            out=pk_i2[:, T:T + 1], in_=t_mi[:, q * 8 + 1:q * 8 + 2]))
                    pos_used[ch] = P.mark("vector", s_dv)
                if ch % 8 == 7:
                    P.wait("vector", s_pe, totals_pe[ch])
                    P.op("vector", lambda e, seg=seg: e.tensor_copy(
                        out=r_tot[0:1, seg * 256:(seg + 1) * 256], in_=ps_tot[:]))
                    seg_evac[seg] = P.mark("vector", s_dv)

            ln_apply = {}
            last_sqrt = [0]
            xnb_store = {}

            def ln_chunk(ch, pair):
                par = ch % 2
                src = d_xp if pair else d_x
                dst = d_xnpb if pair else d_xnb
                key = ("p" if pair else "o", ch)
                prev = ("p" if pair else "o", ch - 2)
                if prev in ln_apply:
                    P.wait("scalar", s_dv, ln_apply[prev])
                elif pair and ch < 2:
                    P.wait("scalar", s_dv, ln_apply[("o", 6 + ch)])
                for t in range(4):
                    P.dma("scalar", s_st, t_x[par][:, t * D:(t + 1) * D],
                          src[ch * 512 + t * 128: ch * 512 + (t + 1) * 128, :])
                v_x = s_st.n
                P.wait("vector", s_st, v_x)
                P.wait("vector", s_ac, last_sqrt[0])
                for t in range(4):
                    P.op("vector", lambda e, t=t, par=par: e.bn_stats(
                        out=t_bnst[:, t * 12:t * 12 + 6], in_=t_x[par][:, t * D:t * D + 512]))
                    P.op("vector", lambda e, t=t, par=par: e.bn_stats(
                        out=t_bnst[:, t * 12 + 6:t * 12 + 12],
                        in_=t_x[par][:, t * D + 512:(t + 1) * D]))
                P.drain("vector")
                for t in range(4):
                    P.op("vector", lambda e, t=t: e.bn_aggr(
                        out=t_bnag[:, t * 2:(t + 1) * 2], in_=t_bnst[:, t * 12:(t + 1) * 12]))
                v_agg = P.mark("vector", s_dv)
                P.wait("scalar", s_dv, v_agg)
                for t in range(4):
                    P.op("scalar", lambda e, t=t: e.activation(
                        out=t_sd[:, t:t + 1], in_=t_bnag[:, t * 2 + 1:t * 2 + 2],
                        func=AF.Sqrt, bias=t_eps[:]))
                last_sqrt[0] = P.mark("scalar", s_ac)
                P.wait("vector", s_ac, last_sqrt[0])
                for t in range(4):
                    P.op("vector", lambda e, t=t: e.reciprocal(
                        out=t_rst[:, t:t + 1], in_=t_sd[:, t:t + 1]))
                P.drain("vector")
                for t in range(4):
                    P.op("vector", lambda e, t=t, par=par: e.tensor_scalar(
                        out=t_xn[par][:, t * D:(t + 1) * D], in0=t_x[par][:, t * D:(t + 1) * D],
                        scalar1=t_bnag[:, t * 2:t * 2 + 1], scalar2=t_rst[:, t:t + 1],
                        op0=ALU.subtract, op1=ALU.mult))
                    if not pair:
                        P.op("vector", lambda e, t=t, ch=ch: e.tensor_copy(
                            out=pk_rs[:, ch * 4 + t:ch * 4 + t + 1], in_=t_rst[:, t:t + 1]))
                ln_apply[key] = P.mark("vector", s_dv)
                P.wait("scalar", s_dv, ln_apply[key])
                for t in range(4):
                    P.dma("scalar", s_st, dst[ch * 512 + t * 128: ch * 512 + (t + 1) * 128, :],
                          t_xn[par][:, t * D:(t + 1) * D])
                return s_st.n

            for ch in range(8):
                routing_chunk(ch)
                xnb_store[ch] = ln_chunk(ch, pair=False)

            P.wait("scalar", s_dv, seg_evac[0])
            P.dma("scalar", s_st, d_rtm[0:32, :], r_tot[0:1, 0:256])
            v_rtm = s_st.n
            P.wait("sync", s_st, v_rtm)
            P.dma("sync", s_ld, t_rt32[:], d_rtm[0:32, :])
            v_rt32 = s_ld.n
            P.wait("tensor", s_ld, v_rt32)
            pv = P.op("tensor", lambda e: e.transpose(
                out=ps_bu[0:8, 0:32], in_=t_rt32[:], identity=c_id[0:32, 0:32]), inc=(s_pe, 1))
            P.wait("vector", s_pe, pv)
            P.op("vector", lambda e: e.tensor_copy(out=rt_bin[:, 0:32], in_=ps_bu[0:8, 0:32]))
            P.drain("vector")
            P.op("vector", lambda e: e.tensor_tensor_scan(
                out=rt_base[:, 1:32], data0=rt_bin[:, 0:31], data1=rt_bin[:, 0:31],
                initial=0.0, op0=ALU.add, op1=ALU.bypass))
            v_base = P.mark("vector", s_dv)
            v_bu_free = v_base

            P.wait("tensor", s_dv, v_base)
            P.wait("tensor", s_dv, seg_evac[0])
            for T in range(32):
                P.op("tensor", lambda e, T=T: e.matmul(
                    out=ps_tot[0:1, (T % 32) * 8:(T % 32) * 8 + 8], lhsT=rt_base[:, T:T + 1],
                    rhs=c_id[0:8, 0:8], start=True, stop=True))
            pv = P.mark("tensor", s_pe)
            P.wait("vector", s_pe, pv)
            P.op("vector", lambda e: e.tensor_copy(out=t_rowb[:], in_=ps_tot[:]))
            v_rowb = P.mark("vector", s_dv)
            tot_extra[8] = v_rowb
            P.wait("tensor", s_dv, v_rowb)
            P.wait("tensor", s_dv, pos_used[7])
            v_brep_last = 0
            for half in range(8):
                for T in range(half * 4, (half + 1) * 4):
                    P.op("tensor", lambda e, T=T: e.matmul(
                        out=ps_pos[:, (T % 4) * 8:(T % 4) * 8 + 8], lhsT=c_onr[:],
                        rhs=t_rowb[0:1, T * 8:(T + 1) * 8], start=True, stop=True))
                pv = P.mark("tensor", s_pe)
                P.wait("vector", s_pe, pv)
                for T in range(half * 4, (half + 1) * 4):
                    P.op("vector", lambda e, T=T: e.tensor_copy(
                        out=t_brep[:, T * 8:(T + 1) * 8],
                        in_=ps_pos[:, (T % 4) * 8:(T % 4) * 8 + 8]))
                v_brep_last = P.mark("vector", s_dv)
                if half < 7:
                    P.wait("tensor", s_dv, v_brep_last)

            scxn_used = {}
            scat_batch = {}
            for b8 in range(4):
                tiles = list(range(b8 * 8, (b8 + 1) * 8))
                if b8 >= 2:
                    P.wait("vector", s_gq, scat_batch[b8 - 2])
                for T in tiles:
                    P.op("vector", lambda e, T=T: e.tensor_scalar(
                        out=t_oh1[:, (T % 4) * 8:(T % 4) * 8 + 8], in0=c_io8[:],
                        scalar1=pk_i1[:, T:T + 1], scalar2=None, op0=ALU.is_equal))
                    P.op("vector", lambda e, T=T: e.tensor_scalar(
                        out=t_oh2[:, (T % 4) * 8:(T % 4) * 8 + 8], in0=c_io8[:],
                        scalar1=pk_i2[:, T:T + 1], scalar2=None, op0=ALU.is_equal))
                    if T % 4 == 3:
                        P.drain("vector")
                        for Q in range(T - 3, T + 1):
                            P.op("vector", lambda e, Q=Q: e.tensor_tensor(
                                out=t_tmp8[:, (Q % 4) * 8:(Q % 4) * 8 + 8],
                                in0=t_brep[:, Q * 8:(Q + 1) * 8],
                                in1=t_oh1[:, (Q % 4) * 8:(Q % 4) * 8 + 8], op=ALU.mult))
                            P.op("vector", lambda e, Q=Q: e.tensor_tensor(
                                out=t_tmp8[:, 32 + (Q % 4) * 8:40 + (Q % 4) * 8],
                                in0=t_brep[:, Q * 8:(Q + 1) * 8],
                                in1=t_oh2[:, (Q % 4) * 8:(Q % 4) * 8 + 8], op=ALU.mult))
                        P.drain("vector")
                        for Q in range(T - 3, T + 1):
                            P.op("vector", lambda e, Q=Q: e.reduce_sum(
                                out=t_osel[:, (Q % 8) * 2:(Q % 8) * 2 + 1],
                                in_=t_tmp8[:, (Q % 4) * 8:(Q % 4) * 8 + 8], axis=AX.X))
                            P.op("vector", lambda e, Q=Q: e.reduce_sum(
                                out=t_osel[:, (Q % 8) * 2 + 1:(Q % 8) * 2 + 2],
                                in_=t_tmp8[:, 32 + (Q % 4) * 8:40 + (Q % 4) * 8], axis=AX.X))
                        P.drain("vector")
                        for Q in range(T - 3, T + 1):
                            P.op("vector", lambda e, Q=Q: e.tensor_tensor(
                                out=pk_p1[:, Q:Q + 1], in0=pk_c1[:, Q:Q + 1],
                                in1=t_osel[:, (Q % 8) * 2:(Q % 8) * 2 + 1], op=ALU.add))
                            P.op("vector", lambda e, Q=Q: e.tensor_tensor(
                                out=pk_p2[:, Q:Q + 1], in0=pk_c2[:, Q:Q + 1],
                                in1=t_osel[:, (Q % 8) * 2 + 1:(Q % 8) * 2 + 2], op=ALU.add))
                        P.drain("vector")
                        for Q in range(T - 3, T + 1):
                            for kk, pk_i, pk_p in ((0, pk_i1, pk_p1), (1, pk_i2, pk_p2)):
                                col = (Q % 8) * 2 + kk
                                P.op("vector", lambda e, col=col, Q=Q, pk_i=pk_i, pk_p=pk_p:
                                     e.scalar_tensor_tensor(
                                         out=t_iv[:, col:col + 1], in0=pk_i[:, Q:Q + 1],
                                         scalar=float(L), in1=pk_p[:, Q:Q + 1],
                                         op0=ALU.mult, op1=ALU.add))
                                P.op("vector", lambda e, col=col, Q=Q, pk_p=pk_p: e.tensor_scalar(
                                    out=t_vm[:, col:col + 1], in0=pk_p[:, Q:Q + 1],
                                    scalar1=float(L), scalar2=None, op0=ALU.is_lt))
                        P.drain("vector")
                        for Q in range(T - 3, T + 1):
                            for kk in range(2):
                                col = (Q % 8) * 2 + kk
                                P.op("vector", lambda e, col=col: e.select(
                                    out=t_kv[:, col:col + 1], mask=t_vm[:, col:col + 1],
                                    on_true=t_iv[:, col:col + 1], on_false=c_bigf[:]))
                        P.drain("vector")
                        for Q in range(T - 3, T + 1):
                            for kk in range(2):
                                col = (Q % 8) * 2 + kk
                                P.op("vector", lambda e, col=col, b8=b8: e.tensor_copy(
                                    out=t_idx[b8 % 2][:, col:col + 1], in_=t_kv[:, col:col + 1]))
                dvb = P.mark("vector", s_dv)
                for T in tiles:
                    par = T % 2
                    P.wait("sync", s_st, xnb_store[T // 4])
                    if T >= 2:
                        P.wait("sync", s_gq, scxn_used[T - 2])
                    P.dma("sync", s_ld, t_scxn[par][:], d_xnb[T * 128:(T + 1) * 128, :])
                    v_xn = s_ld.n
                    P.wait("gpsimd", s_dv, dvb)
                    P.wait("gpsimd", s_ld, v_xn)
                    for kk in range(2):
                        P.op("gpsimd", lambda e, T=T, kk=kk, par=par, b8=b8: e.indirect_dma_start(
                            out=d_xq[:], out_offset=IndirectOffsetOnAxis(
                                ap=t_idx[b8 % 2][:, (T % 8) * 2 + kk:(T % 8) * 2 + kk + 1],
                                axis=0),
                            in_=t_scxn[par][:], in_offset=None,
                            bounds_check=bc_reg["r"], oob_is_err=False), inc=(s_gq, 16))
                    scxn_used[T] = s_gq.n
                scat_batch[b8] = s_gq.n
            v_scat_all = s_gq.n
            v_scat_drain = P.op("gpsimd", lambda e: e.drain(), inc=(s_gp, 1))

            for ch in range(8):
                ln_chunk(ch, pair=True)
            v_xnpb_all = s_st.n

            P.wait("tensor", s_gq, v_gwraw)
            gwt_dv = {}
            for k in range(NE):
                if k >= 1:
                    P.wait("tensor", s_dv, gwt_dv[k - 1])
                for r in range(NE):
                    pv = P.op("tensor", lambda e, k=k, r=r: e.transpose(
                        out=ps_gt[:, r * 128:(r + 1) * 128],
                        in_=t_gwraw[:, r * D + k * 128:r * D + (k + 1) * 128],
                        identity=c_idb[:, 0:128]), inc=(s_pe, 1) if r == NE - 1 else None)
                P.wait("vector", s_pe, pv)
                gwt_dv[k] = P.op("vector", lambda e, k=k: e.tensor_copy(
                    out=w_GWT[:, k * D:(k + 1) * D], in_=ps_gt[:]), inc=(s_dv, 1))
            P.wait("tensor", s_dv, gwt_dv[NE - 1])

            bumm = {}
            bu_evac = {}
            bp_last = {}
            w_consumed = {}
            pw_done = {}
            g_consumed = {}
            wst = {}
            gst = {}
            anfm_free = {}
            bp_free = {}
            attn_seq = []

            def attn_chunk(ch, pair):
                par = len(attn_seq) % 2
                key = ("p" if pair else "o", ch)
                attn_seq.append(key)
                srcn = d_xnpb if pair else d_xnb
                iosrc = d_ioP if pair else d_ioT
                if len(attn_seq) >= 3:
                    P.wait("sync", s_pe, anfm_free[attn_seq[-3]])
                    P.wait("sync", s_dv, w_consumed[(attn_seq[-3], NE - 1)])
                P.wait("sync", s_st, v_xnpb_all if pair else xnb_store[ch])
                for k in range(NE):
                    P.tdma("sync", s_ld, t_anfm[par][:, k * 512:(k + 1) * 512],
                           srcn[ch * 512:(ch + 1) * 512, k * 128:(k + 1) * 128])
                P.dma("sync", s_ld, t_iot[par][:], iosrc[:, ch * 512:(ch + 1) * 512])
                v_an = s_ld.n
                P.wait("tensor", s_ld, v_an)
                if len(attn_seq) >= 2:
                    P.wait("tensor", s_dv, bu_evac[attn_seq[-2]])
                else:
                    P.wait("tensor", s_dv, v_bu_free)
                for k in range(NE):
                    P.op("tensor", lambda e, k=k, par=par: e.matmul(
                        out=ps_bu[:], lhsT=w_BwT[:, k * SDIM:(k + 1) * SDIM],
                        rhs=t_anfm[par][:, k * 512:(k + 1) * 512],
                        start=(k == 0), stop=(k == NE - 1)))
                bumm[key] = P.mark("tensor", s_pe)
                P.wait("vector", s_pe, bumm[key])
                if len(attn_seq) >= 2:
                    P.wait("vector", s_pe, bp_last[attn_seq[-2]])
                P.op("vector", lambda e: e.tensor_copy(out=t_bu[:], in_=ps_bu[:]))
                bu_evac[key] = P.mark("vector", s_dv)
                P.wait("tensor", s_dv, bu_evac[key])
                prev_free = bp_free.get(attn_seq[-2]) if len(attn_seq) >= 2 else None
                for dm in range(NE):
                    if dm >= 2:
                        P.wait("tensor", s_dv, w_consumed[(key, dm - 2)])
                    elif prev_free is not None:
                        kind, val = prev_free[dm % 2]
                        P.wait("tensor", s_ac if kind == "ac" else s_dv, val)
                    P.op("tensor", lambda e, dm=dm: e.matmul(
                        out=ps_bp[dm % 2][:], lhsT=w_CwT[:, dm * 128:(dm + 1) * 128],
                        rhs=t_bu[:], start=True, stop=True), inc=(s_pe, 1))
                    mmv = s_pe.n
                    if dm == NE - 1:
                        bp_last[key] = mmv
                    if dm >= 2:
                        P.wait("vector", s_ac, pw_done[(key, dm - 2)])
                    P.op("vector", lambda e, dm=dm, par=par: e.tensor_scalar(
                        out=t_expo[dm % 2][:], in0=t_iot[par][:],
                        scalar1=v_lnd[:, dm:dm + 1], scalar2=None, op0=ALU.mult))
                    ev = P.mark("vector", s_dv)
                    P.wait("scalar", s_dv, ev)
                    P.wait("scalar", s_ac, a_lnd)
                    P.op("scalar", lambda e, dm=dm: e.activation(
                        out=t_pw[dm % 2][:], in_=t_expo[dm % 2][:], func=AF.Exp))
                    pw_done[(key, dm)] = P.mark("scalar", s_ac)
                    P.wait("vector", s_ac, pw_done[(key, dm)])
                    P.wait("vector", s_pe, mmv)
                    if not pair:
                        if (key, dm - 2) in wst:
                            P.wait("vector", s_st, wst[(key, dm - 2)])
                        P.op("vector", lambda e, dm=dm: e.tensor_tensor(
                            out=t_wsb[dm % 2][:], in0=t_pw[dm % 2][:], in1=ps_bp[dm % 2][:],
                            op=ALU.mult))
                        w_consumed[(key, dm)] = P.mark("vector", s_dv)
                        P.wait("scalar", s_dv, w_consumed[(key, dm)])
                        P.dma("scalar", s_st,
                              d_wfm[dm * 128:(dm + 1) * 128, ch * 512:(ch + 1) * 512],
                              t_wsb[dm % 2][:])
                        wst[(key, dm)] = s_st.n
                    else:
                        P.op("vector", lambda e, dm=dm: e.tensor_tensor(
                            out=t_wpf[dm % 2][:], in0=t_pw[dm % 2][:], in1=ps_bp[dm % 2][:],
                            op=ALU.mult))
                        P.drain("vector")
                        P.op("vector", lambda e, dm=dm, ch=ch: e.reduce_sum(
                            out=v_pref[:, dm * 8 + ch:dm * 8 + ch + 1], in_=t_wpf[dm % 2][:],
                            axis=AX.X))
                        w_consumed[(key, dm)] = P.mark("vector", s_dv)
                if not pair:
                    for em in range(NE):
                        if em >= 2:
                            P.wait("tensor", s_ac, g_consumed[(key, em - 2)])
                        else:
                            P.wait("tensor", s_dv, w_consumed[(key, NE - 2 + em)])
                        for k in range(NE):
                            P.op("tensor", lambda e, em=em, k=k, par=par: e.matmul(
                                out=ps_bp[em % 2][:],
                                lhsT=w_GWT[:, k * D + em * 128:k * D + (em + 1) * 128],
                                rhs=t_anfm[par][:, k * 512:(k + 1) * 512],
                                start=(k == 0), stop=(k == NE - 1)),
                                inc=(s_pe, 1) if k == NE - 1 else None)
                        mmv = s_pe.n
                        P.wait("scalar", s_pe, mmv)
                        if (key, em - 2) in gst:
                            P.wait("scalar", s_st, gst[(key, em - 2)])
                        P.op("scalar", lambda e, em=em: e.activation(
                            out=t_gsb[em % 2][:], in_=ps_bp[em % 2][:], func=AF.Sigmoid,
                            bias=v_gb[:, em:em + 1]))
                        g_consumed[(key, em)] = P.mark("scalar", s_ac)
                        P.dma("scalar", s_st,
                              d_gfm[em * 128:(em + 1) * 128, ch * 512:(ch + 1) * 512],
                              t_gsb[em % 2][:])
                        gst[(key, em)] = s_st.n
                    anfm_free[key] = s_pe.n
                    bp_free[key] = [("ac", g_consumed[(key, NE - 2)]),
                                    ("ac", g_consumed[(key, NE - 1)])]
                else:
                    anfm_free[key] = bp_last[key]
                    bp_free[key] = [("dv", w_consumed[(key, NE - 2)]),
                                    ("dv", w_consumed[(key, NE - 1)])]

            for ch in range(8):
                attn_chunk(ch, pair=False)
            for ch in range(8):
                attn_chunk(ch, pair=True)
            v_wfm_all = s_st.n
            v_bu_last = bu_evac[attn_seq[-1]]
            v_bp_att_free = bp_free[attn_seq[-1]]

            for ch in range(8, 64):
                routing_chunk(ch)

            P.wait("scalar", s_dv, seg_evac[7])
            P.dma("scalar", s_st, d_rtm[:], r_tot[:])
            barA = {}
            barA["pe"] = P.mark("tensor", s_pe)
            barA["dv"] = P.mark("vector", s_dv)
            barA["ac"] = P.mark("scalar", s_ac)
            barA["gp"] = P.mark("gpsimd", s_gp)
            barA["ld"] = s_ld.n
            barA["st"] = s_st.n
            barA["gq"] = v_scat_all
            barA["sc"] = s_sc.n

        with contextlib.ExitStack() as Bk:
            ts_w = sb(Bk, "ts_w", [128, 4096], BF16)
            ts_g = sb(Bk, "ts_g", [128, 4096], BF16)
            ts_h = sb(Bk, "ts_h", [128, 512], F32)
            ts_s = [sb(Bk, f"ts_s{p}", [128, 512], BF16) for p in range(2)]
            ts_pc = sb(Bk, "ts_pc", [128, 1], F32)
            ts_cy = sb(Bk, "ts_cy", [128, 1], F32)
            t_W1s = sb(Bk, "t_W1s", [128, HD * NE], BF16)
            t_W2s = sb(Bk, "t_W2s", [128, 1024 * 32], BF16)
            t_xqs = sb(Bk, "t_xqs", [128, L * NE], BF16, is_transpose_dest=True)
            t_hsb = sb(Bk, "t_hsb", [128, 384 * 32], BF16)
            t_ysb = [sb(Bk, f"t_ysb{p}", [128, 512], BF16) for p in range(2)]
            t_b1c = [sb(Bk, f"t_b1c{p}", [128, 32], F32) for p in range(2)]
            t_b2r = sb(Bk, "t_b2r", [1, D], BF16)
            t_b2rep = [sb(Bk, f"t_b2rep{p}", [128, D], BF16) for p in range(2)]
            ps_h = [Bk.enter_context(nc.psum_tensor(f"ps_h{i}", [128, 384], F32))
                    for i in range(4)]
            ps_y = [Bk.enter_context(nc.psum_tensor(f"ps_y{i}", [128, 512], F32))
                    for i in range(2)]
            ps_mb = Bk.enter_context(nc.psum_tensor("ps_mb", [128, 512], F32))

            P = Prog()
            progs.append(P)
            for eng in ENGS:
                P.wait(eng, s_pe, barA["pe"])
                P.wait(eng, s_dv, barA["dv"])
                P.wait(eng, s_ac, barA["ac"])
                P.wait(eng, s_gp, barA["gp"])
                P.wait(eng, s_ld, barA["ld"])
                P.wait(eng, s_st, barA["st"])
                P.wait(eng, s_gq, barA["gq"])
                P.wait(eng, s_sc, barA["sc"])

            c2v = P.mark("vector", s_dv)
            P.wait("scalar", s_dv, c2v)

            w1_loaded = {}
            w2_loaded = {}
            mm1_done = {}
            mm2_done = {}
            xq_loaded = {}
            bprep_done = {}
            bfree_ac = {}
            bfree_dv = {}
            prev_hq01 = [0, 0, 0, 0]
            prev_yv01 = [0, 0]
            prev_yst01 = [0, 0]

            def emit_w1loads(e):
                if e >= 1:
                    P.wait("sync", s_pe, mm1_done[e - 1])
                P.wait("sync", s_gq, wpc[e])
                P.dma("sync", s_ld, t_W1s[:], d_w1b[e * 128:(e + 1) * 128, :])
                w1_loaded[e] = s_ld.n

            def emit_w2loads(e):
                if e >= 1:
                    P.wait("scalar", s_pe, mm2_done[e - 1])
                if e == 0:
                    P.wait("scalar", s_gq, wpc[0])
                P.dma("scalar", s_st, t_W2s[:], d_w2b[e * 128:(e + 1) * 128, :])
                w2_loaded[e] = s_st.n

            emit_w1loads(0)
            emit_w2loads(0)

            def emit_bprep(e):
                bp = e % 2
                if e == 0:
                    P.wait("scalar", s_gq, holders["v_b2cast"])
                if e >= 1:
                    P.wait("scalar", s_pe, bprep_bcast[e - 1])
                P.dma("scalar", s_st, t_b1c[bp][:], d_b1[e * 128:(e + 1) * 128, :])
                P.dma("scalar", s_st, t_b2r[:], d_b2b[e:e + 1, :])
                bprep_b1[e] = s_st.n
                P.wait("tensor", s_st, bprep_b1[e])
                if e >= 1:
                    P.wait("tensor", s_dv, bprep_done[e - 1])
                if e == 0:
                    P.op("vector", lambda e_: e_.tensor_copy(out=c_onrb[:], in_=c_onr[:]))
                    dv0 = P.mark("vector", s_dv)
                    P.wait("tensor", s_dv, dv0)
                for half in range(2):
                    pv = P.op("tensor", lambda e_, half=half: e_.matmul(
                        out=ps_mb[:], lhsT=c_onrb[:],
                        rhs=t_b2r[0:1, half * 512:(half + 1) * 512],
                        start=True, stop=True), inc=(s_pe, 1))
                    P.wait("vector", s_pe, pv)
                    P.op("vector", lambda e_, half=half, bp=bp: e_.tensor_copy(
                        out=t_b2rep[bp][:, half * 512:(half + 1) * 512], in_=ps_mb[:]))
                    dv = P.mark("vector", s_dv)
                    if half == 0:
                        P.wait("tensor", s_dv, dv)
                bprep_bcast[e] = pv
                bprep_done[e] = dv

            def emit_xqloads(e):
                if e >= 1:
                    P.wait("sync", s_pe, mm1_done[e - 1])
                for k in range(NE):
                    P.tdma("sync", s_ld, t_xqs[:, k * L:(k + 1) * L],
                           d_xq[e * L:(e + 1) * L, k * 128:(k + 1) * 128])
                xq_loaded[e] = s_ld.n

            ps_aux = Bk.enter_context(nc.psum_tensor("ps_aux", [128, 128], F32))

            def emit_opref_a():
                halves = 0
                for hh in range(2):
                    if hh == 1:
                        P.wait("sync", s_dv, halves)
                    P.dma("sync", s_ld, t_rt128[:], d_rtm[hh * 128:(hh + 1) * 128, :])
                    v_a1 = s_ld.n
                    P.wait("tensor", s_ld, v_a1)
                    if hh == 1:
                        P.wait("tensor", s_dv, halves)
                    pv = P.op("tensor", lambda e_: e_.transpose(
                        out=ps_aux[0:8, 0:128], in_=t_rt128[:], identity=c_id[:, 0:128]),
                        inc=(s_pe, 1))
                    P.wait("vector", s_pe, pv)
                    P.op("vector", lambda e_, hh=hh: e_.tensor_copy(
                        out=rt_bin[:, hh * 128:(hh + 1) * 128], in_=ps_aux[0:8, 0:128]))
                    halves = P.mark("vector", s_dv)
                P.op("vector", lambda e_: e_.tensor_tensor(out=t_pvmul[:], in0=rt_bin[:],
                                                           in1=c_pvm[:], op=ALU.mult))
                P.drain("vector")
                P.op("vector", lambda e_: e_.reduce_sum(out=t_opcol[:], in_=t_pvmul[:],
                                                        axis=AX.X))
                vb = P.mark("vector", s_dv)
                P.wait("tensor", s_dv, vb)
                pv = P.op("tensor", lambda e_: e_.matmul(out=ps_aux[0:1, 0:8], lhsT=t_opcol[:],
                                                         rhs=c_id[0:8, 0:8], start=True,
                                                         stop=True), inc=(s_pe, 1))
                P.wait("vector", s_pe, pv)
                P.op("vector", lambda e_: e_.tensor_copy(out=t_oprow[:], in_=ps_aux[0:1, 0:8]))
                vc = P.mark("vector", s_dv)
                P.wait("tensor", s_dv, vc)
                pv = P.op("tensor", lambda e_: e_.matmul(out=ps_aux[:, 0:8], lhsT=c_onr[:],
                                                         rhs=t_oprow[:], start=True, stop=True),
                          inc=(s_pe, 1))
                P.wait("vector", s_pe, pv)
                P.op("vector", lambda e_: e_.tensor_copy(out=r_opref[:], in_=ps_aux[:, 0:8]))
                return P.mark("vector", s_dv)

            def emit_p3w_act(v_opref):
                P.wait("scalar", s_dv, v_opref)
                for T in range(32):
                    P.op("scalar", lambda e_, T=T: e_.activation(
                        out=t_g1c[:, T:T + 1], in_=pk_zd[:, T:T + 1], func=AF.Sigmoid,
                        scale=pk_rs[:, T:T + 1]))
                a_g1 = P.mark("scalar", s_ac)
                P.wait("vector", s_ac, a_g1)
                P.op("vector", lambda e_: e_.tensor_scalar(out=t_g2c[:], in0=t_g1c[:],
                                                           scalar1=-1.0, scalar2=1.0,
                                                           op0=ALU.mult, op1=ALU.add))
                P.drain("vector")

            def emit_p3w_batch(b8):
                for T in range(b8 * 8, (b8 + 1) * 8):
                    P.op("vector", lambda e_, T=T: e_.tensor_scalar(
                        out=t_oh1[:, (T % 4) * 8:(T % 4) * 8 + 8], in0=c_io8[:],
                        scalar1=pk_i1[:, T:T + 1], scalar2=None, op0=ALU.is_equal))
                    P.op("vector", lambda e_, T=T: e_.tensor_scalar(
                        out=t_oh2[:, (T % 4) * 8:(T % 4) * 8 + 8], in0=c_io8[:],
                        scalar1=pk_i2[:, T:T + 1], scalar2=None, op0=ALU.is_equal))
                    if T % 4 == 3:
                        P.drain("vector")
                        for Q in range(T - 3, T + 1):
                            P.op("vector", lambda e_, Q=Q: e_.tensor_tensor(
                                out=t_tmp8[:, (Q % 4) * 8:(Q % 4) * 8 + 8], in0=r_opref[:],
                                in1=t_oh1[:, (Q % 4) * 8:(Q % 4) * 8 + 8], op=ALU.mult))
                            P.op("vector", lambda e_, Q=Q: e_.tensor_tensor(
                                out=t_tmp8[:, 32 + (Q % 4) * 8:40 + (Q % 4) * 8], in0=r_opref[:],
                                in1=t_oh2[:, (Q % 4) * 8:(Q % 4) * 8 + 8], op=ALU.mult))
                        P.drain("vector")
                        for Q in range(T - 3, T + 1):
                            P.op("vector", lambda e_, Q=Q: e_.reduce_sum(
                                out=t_osel[:, (Q % 8) * 2:(Q % 8) * 2 + 1],
                                in_=t_tmp8[:, (Q % 4) * 8:(Q % 4) * 8 + 8], axis=AX.X))
                            P.op("vector", lambda e_, Q=Q: e_.reduce_sum(
                                out=t_osel[:, (Q % 8) * 2 + 1:(Q % 8) * 2 + 2],
                                in_=t_tmp8[:, 32 + (Q % 4) * 8:40 + (Q % 4) * 8], axis=AX.X))
                        P.drain("vector")
                        for Q in range(T - 3, T + 1):
                            for kk, pk_p in ((0, pk_p1), (1, pk_p2)):
                                col = (Q % 8) * 2 + kk
                                P.op("vector", lambda e_, Q=Q, col=col, pk_p=pk_p: e_.tensor_tensor(
                                    out=t_iv[:, col:col + 1], in0=pk_p[:, Q:Q + 1],
                                    in1=t_osel[:, col:col + 1], op=ALU.add))
                                P.op("vector", lambda e_, Q=Q, col=col, pk_p=pk_p: e_.tensor_scalar(
                                    out=t_vm[:, col:col + 1], in0=pk_p[:, Q:Q + 1],
                                    scalar1=float(L), scalar2=None, op0=ALU.is_lt))
                        P.drain("vector")
                        for Q in range(T - 3, T + 1):
                            for kk in range(2):
                                col = (Q % 8) * 2 + kk
                                P.op("vector", lambda e_, col=col: e_.tensor_scalar(
                                    out=t_kv[:, col:col + 1], in0=t_iv[:, col:col + 1],
                                    scalar1=CAP, scalar2=None, op0=ALU.is_lt))
                        P.drain("vector")
                        for Q in range(T - 3, T + 1):
                            for kk in range(2):
                                col = (Q % 8) * 2 + kk
                                P.op("vector", lambda e_, col=col: e_.tensor_tensor(
                                    out=t_kv[:, col:col + 1], in0=t_kv[:, col:col + 1],
                                    in1=t_vm[:, col:col + 1], op=ALU.mult))
                        P.drain("vector")
                        for Q in range(T - 3, T + 1):
                            for kk, gcol, pk_w in ((0, t_g1c, pk_w1), (1, t_g2c, pk_w2)):
                                col = (Q % 8) * 2 + kk
                                P.op("vector",
                                     lambda e_, Q=Q, col=col, gcol=gcol, pk_w=pk_w: e_.tensor_tensor(
                                         out=pk_w[:, Q:Q + 1], in0=gcol[:, Q:Q + 1],
                                         in1=t_kv[:, col:col + 1], op=ALU.mult))
                            for kk, pk_i, pk_p in ((0, pk_i1, pk_p1), (1, pk_i2, pk_p2)):
                                col = (Q % 8) * 2 + kk
                                P.op("vector",
                                     lambda e_, Q=Q, col=col, pk_i=pk_i, pk_p=pk_p:
                                     e_.scalar_tensor_tensor(
                                         out=t_iv[:, col:col + 1], in0=pk_i[:, Q:Q + 1],
                                         scalar=float(L), in1=pk_p[:, Q:Q + 1],
                                         op0=ALU.mult, op1=ALU.add))
                        P.drain("vector")
                        for Q in range(T - 3, T + 1):
                            for kk in range(2):
                                col = (Q % 8) * 2 + kk
                                P.op("vector", lambda e_, col=col: e_.select(
                                    out=t_kv[:, col:col + 1], mask=t_vm[:, col:col + 1],
                                    on_true=t_iv[:, col:col + 1], on_false=c_bigf[:]))
                        P.drain("vector")
                        for Q in range(T - 3, T + 1):
                            for kk, pk_g in ((0, pk_g1), (1, pk_g2)):
                                col = (Q % 8) * 2 + kk
                                P.op("vector", lambda e_, Q=Q, col=col, pk_g=pk_g: e_.tensor_copy(
                                    out=pk_g[:, Q:Q + 1], in_=t_kv[:, col:col + 1]))
                        P.drain("vector")

            scan_sst = {}
            slab_ld = {}
            scan_done = {}
            scan_half = {}

            def emit_scan_loads(dm):
                if dm >= 1:
                    P.wait("scalar", s_dv, scan_done[dm - 1])
                P.dma("scalar", s_st, ts_w[:], d_wfm[dm * 128:(dm + 1) * 128, :])
                P.dma("scalar", s_st, ts_g[:], d_gfm[dm * 128:(dm + 1) * 128, :])
                slab_ld[dm] = s_st.n

            def emit_scan_dm(dm):
                P.op("vector", lambda e_, dm=dm: e_.reduce_sum(
                    out=ts_pc[:], in_=v_pref[:, dm * 8:(dm + 1) * 8], axis=AX.X))
                P.drain("vector")
                P.op("vector", lambda e_: e_.tensor_copy(out=ts_cy[:], in_=ts_pc[:]))
                P.drain("vector")
                P.wait("vector", s_st, slab_ld[dm])
                for qq in range(8):
                    if (dm, qq - 2) in scan_sst:
                        P.wait("vector", s_st, scan_sst[(dm, qq - 2)])
                    elif dm >= 1 and (dm - 1, qq + 6) in scan_sst:
                        P.wait("vector", s_st, scan_sst[(dm - 1, qq + 6)])
                    P.op("vector", lambda e_: e_.tensor_copy(out=ts_h[:, 0:1], in_=ts_cy[:]))
                    P.op("vector", lambda e_, qq=qq: e_.tensor_tensor_scan(
                        out=ts_h[:, 1:512], data0=ts_w[:, qq * 512:qq * 512 + 511],
                        data1=ts_w[:, qq * 512:qq * 512 + 511],
                        initial=ts_cy[:], op0=ALU.add, op1=ALU.bypass))
                    P.drain("vector")
                    P.op("vector", lambda e_, qq=qq: e_.tensor_tensor(
                        out=ts_cy[:], in0=ts_h[:, 511:512],
                        in1=ts_w[:, qq * 512 + 511:qq * 512 + 512], op=ALU.add))
                    P.op("vector", lambda e_, qq=qq: e_.tensor_tensor(
                        out=ts_s[qq % 2][:], in0=ts_g[:, qq * 512:(qq + 1) * 512],
                        in1=ts_h[:], op=ALU.mult))
                    sd = P.mark("vector", s_dv)
                    P.wait("scalar", s_dv, sd)
                    P.dma("scalar", s_st,
                          d_sfm[dm * 128:(dm + 1) * 128, qq * 512:(qq + 1) * 512],
                          ts_s[qq % 2][:])
                    scan_sst[(dm, qq)] = s_st.n
                scan_done[dm] = P.mark("vector", s_dv)

            bprep_bcast = {}
            bprep_b1 = {}
            emit_bprep(0)
            emit_xqloads(0)
            for e in range(_NEXP):
                bp = e % 2

                for j in range(_NTIL):
                    base = j * 384
                    if j == 0:
                        emit_scan_loads(e)
                    P.wait("tensor", s_ld, w1_loaded[e])
                    P.wait("tensor", s_ld, xq_loaded[e])
                    hq = {}
                    for m in range(32):
                        if m >= 4:
                            P.wait("tensor", s_ac, hq[m - 4])
                        else:
                            P.wait("tensor", s_ac, prev_hq01[m])
                        for k in range(NE):
                            P.op("tensor", lambda e_, m=m, k=k, base=base: e_.matmul(
                                out=ps_h[m % 4][:],
                                lhsT=t_W1s[:, k * HD + m * 128:k * HD + (m + 1) * 128],
                                rhs=t_xqs[:, k * L + base:k * L + base + 384],
                                start=(k == 0), stop=(k == NE - 1)),
                                inc=(s_pe, 1) if k == NE - 1 else None)
                        mmv = s_pe.n
                        P.wait("scalar", s_pe, mmv)
                        if m == 0 and j == 0:
                            P.wait("scalar", s_st, bprep_b1[e])
                        hq[m] = P.op("scalar", lambda e_, m=m, bp=bp: e_.activation(
                            out=t_hsb[:, m * 384:(m + 1) * 384], in_=ps_h[m % 4][:],
                            func=AF.Gelu_apprx_tanh, bias=t_b1c[bp][:, m:m + 1]),
                            inc=(s_ac, 1))
                    prev_hq01 = [hq[28], hq[29], hq[30], hq[31]]
                    if j == 0 and e < _NEXP - 1:
                        emit_bprep(e + 1)
                    if j == 0:
                        for ee in range(e + 1, min(e + 3, _NEXP)):
                            if ee not in holders["wpc"]:
                                holders["emit_wprecast"](ee)
                    if j == _NTIL - 1:
                        mm1_done[e] = s_pe.n
                        if e < _NEXP - 1:
                            emit_w1loads(e + 1)
                            emit_xqloads(e + 1)
                    P.wait("tensor", s_st, w2_loaded[e])
                    yv_inc = {}
                    yst = {}
                    for g in range(6):
                        m2, dh = divmod(g, 2)
                        if g >= 2:
                            P.wait("tensor", s_dv, yv_inc[g - 2])
                        else:
                            P.wait("tensor", s_dv, prev_yv01[g])
                        for k in range(32):
                            if g == 0:
                                P.wait("tensor", s_ac, hq[k])
                            P.op("tensor", lambda e_, m2=m2, dh=dh, k=k, g=g: e_.matmul(
                                out=ps_y[g % 2][:],
                                lhsT=t_hsb[:, k * 384 + m2 * 128:k * 384 + (m2 + 1) * 128],
                                rhs=t_W2s[:, k * 1024 + dh * 512:k * 1024 + (dh + 1) * 512],
                                start=(k == 0), stop=(k == 31)),
                                inc=(s_pe, 1) if k == 31 else None)
                        mmv = s_pe.n
                        P.wait("vector", s_pe, mmv)
                        if g >= 2:
                            P.wait("vector", s_ld, yst[g - 2])
                        else:
                            P.wait("vector", s_ld, prev_yst01[g])
                        yv_inc[g] = P.op("vector", lambda e_, g=g, dh=dh, bp=bp: e_.tensor_tensor(
                            out=t_ysb[g % 2][:], in0=ps_y[g % 2][:],
                            in1=t_b2rep[bp][:, dh * 512:(dh + 1) * 512], op=ALU.add),
                            inc=(s_dv, 1))
                        P.wait("sync", s_dv, yv_inc[g])
                        P.dma("sync", s_ld,
                              d_yq[e * L + base + m2 * 128: e * L + base + (m2 + 1) * 128,
                                   dh * 512:(dh + 1) * 512],
                              t_ysb[g % 2][:])
                        yst[g] = s_ld.n
                    prev_yv01 = [yv_inc[4], yv_inc[5]]
                    prev_yst01 = [yst[4], yst[5]]
                    if j == _NTIL - 1:
                        mm2_done[e] = s_pe.n
                        if e < _NEXP - 1:
                            emit_w2loads(e + 1)
                bfree_ac[e] = s_ac.n
                bfree_dv[e] = s_dv.n
                if e == 0:
                    holders["v_opref"] = emit_opref_a()
                    emit_p3w_act(holders["v_opref"])
                elif 1 <= e <= 4:
                    emit_p3w_batch(e - 1)
                emit_scan_dm(e)

            barB = {}
            barB["pe"] = P.mark("tensor", s_pe)
            barB["dv"] = P.mark("vector", s_dv)
            barB["ac"] = P.mark("scalar", s_ac)
            barB["gp"] = P.mark("gpsimd", s_gp)
            barB["ld"] = s_ld.n
            barB["st"] = s_st.n
            barB["gq"] = v_scat_all
            holders["v_scat_all"] = v_scat_all
            barB["sc"] = s_sc.n

        with contextlib.ExitStack() as Ck:
            t_stm = [sb(Ck, f"t_stm{p}", [128, D], BF16, is_transpose_dest=True)
                     for p in range(2)]
            t_xr = [sb(Ck, f"t_xr{p}", [128, D], F32) for p in range(2)]
            t_A = [sb(Ck, f"t_A{p}", [128, D], BF16) for p in range(2)]
            t_Bt = [sb(Ck, f"t_Bt{p}", [128, D], BF16) for p in range(2)]
            t_f1 = [sb(Ck, f"t_f1{p}", [128, D], F32) for p in range(2)]
            t_f2 = [sb(Ck, f"t_f2{p}", [128, D], F32) for p in range(2)]
            t_g3 = [sb(Ck, f"t_g3{p}", [128, D], F32) for p in range(2)]
            t_sg3 = [sb(Ck, f"t_sg3{p}", [128, D], F32) for p in range(2)]
            t_o3 = [sb(Ck, f"t_o3{p}", [128, D], F32) for p in range(2)]

            P = Prog()
            progs.append(P)
            for eng in ENGS:
                P.wait(eng, s_pe, barB["pe"])
                P.wait(eng, s_dv, barB["dv"])
                P.wait(eng, s_ac, barB["ac"])
                P.wait(eng, s_gp, barB["gp"])
                P.wait(eng, s_ld, barB["ld"])
                P.wait(eng, s_st, barB["st"])
                P.wait(eng, s_gq, barB["gq"])
                P.wait(eng, s_sc, barB["sc"])

            for tile in (t_A, t_Bt):
                for p in range(2):
                    P.op("vector", lambda e_, tile=tile, p=p: e_.memset(tile[p][:], 0.0))
            v_cz = P.mark("vector", s_dv)
            P.wait("gpsimd", s_dv, v_cz)

            f_done = {}
            g_done = {}
            sg_done = {}
            o_done = {}
            st_done = {}
            for T in range(32):
                par = T % 2
                if T >= 2:
                    P.wait("sync", s_dv, sg_done[T - 2])
                    P.wait("sync", s_gp, o_done[T - 2])
                P.tdma("sync", s_ld, t_stm[par][:], d_sfm[:, T * 128:(T + 1) * 128])
                P.dma("sync", s_ld, t_xr[par][:], d_x[T * 128:(T + 1) * 128, :])
                ld_T = s_ld.n
                if T >= 2:
                    P.wait("gpsimd", s_dv, f_done[T - 2])
                for kk, pkg in ((0, pk_g1), (1, pk_g2)):
                    tile = t_A if kk == 0 else t_Bt
                    P.op("gpsimd", lambda e_, T=T, pkg=pkg, tile=tile, par=par:
                         e_.indirect_dma_start(
                             out=tile[par][:], out_offset=None, in_=d_yq[:],
                             in_offset=IndirectOffsetOnAxis(ap=pkg[:, T:T + 1], axis=0),
                             bounds_check=bc_reg["r"], oob_is_err=False), inc=(s_gq, 16))
                gath = s_gq.n
                P.wait("vector", s_gq, gath)
                if T >= 2:
                    P.wait("vector", s_ac, g_done[T - 2])
                P.op("vector", lambda e_, T=T, par=par: e_.tensor_scalar(
                    out=t_f1[par][:], in0=t_A[par][:], scalar1=pk_w1[:, T:T + 1],
                    scalar2=None, op0=ALU.mult))
                P.drain("vector")
                f_done[T] = P.op("vector", lambda e_, T=T, par=par: e_.scalar_tensor_tensor(
                    out=t_f2[par][:], in0=t_Bt[par][:], scalar=pk_w2[:, T:T + 1],
                    in1=t_f1[par][:], op0=ALU.mult, op1=ALU.add), inc=(s_dv, 1))
                P.wait("scalar", s_dv, f_done[T])
                if T >= 2:
                    P.wait("scalar", s_dv, sg_done[T - 2])
                g_done[T] = P.op("scalar", lambda e_, T=T, par=par: e_.activation(
                    out=t_g3[par][:], in_=t_f2[par][:], func=AF.Silu), inc=(s_ac, 1))
                P.wait("vector", s_ac, g_done[T])
                P.wait("vector", s_ld, ld_T)
                if T >= 2:
                    P.wait("vector", s_gp, o_done[T - 2])
                sg_done[T] = P.op("vector", lambda e_, T=T, par=par: e_.tensor_tensor(
                    out=t_sg3[par][:], in0=t_g3[par][:], in1=t_stm[par][:], op=ALU.mult),
                    inc=(s_dv, 1))
                P.wait("gpsimd", s_dv, sg_done[T])
                if T >= 2:
                    P.wait("gpsimd", s_st, st_done[T - 2])
                o_done[T] = P.op("gpsimd", lambda e_, T=T, par=par: e_.tensor_tensor(
                    out=t_o3[par][:], in0=t_sg3[par][:], in1=t_xr[par][:], op=ALU.add),
                    inc=(s_gp, 1))
                P.wait("scalar", s_gp, o_done[T])
                P.dma("scalar", s_st, d_out[T * 128:(T + 1) * 128, :], t_o3[par][:])
                st_done[T] = s_st.n
            P.wait("scalar", s_st, st_done[31])
    emit_block(nc, progs)

    simulate(progs)
    return nc


_NC_CACHE = [None]


def kernel(L0_attn_x, L0_attn_B_w, L0_attn_C_w, L0_attn_gate_w, L0_attn_gate_b,
           L0_attn_log_decay, L0_ffn_Wg, L0_ffn_W1, L0_ffn_b1, L0_ffn_W2, L0_ffn_b2):
    xf = np.ascontiguousarray(np.asarray(L0_attn_x, np.float32).reshape(TALL, D))
    Bw = np.ascontiguousarray(np.asarray(L0_attn_B_w, np.float32))
    Cw = np.ascontiguousarray(np.asarray(L0_attn_C_w, np.float32).reshape(NE, 128, SDIM).transpose(1, 0, 2).reshape(128, NE * SDIM))
    GW = np.ascontiguousarray(np.asarray(L0_attn_gate_w, np.float32))
    gb = np.ascontiguousarray(np.asarray(L0_attn_gate_b, np.float32).reshape(NE, 128).T)
    ldec = np.ascontiguousarray(np.asarray(L0_attn_log_decay, np.float32).reshape(NE, 128).T)
    Wg = np.ascontiguousarray(np.asarray(L0_ffn_Wg, np.float32).reshape(NE, 128, NE).transpose(1, 0, 2).reshape(128, NE * NE))
    W1 = np.ascontiguousarray(np.asarray(L0_ffn_W1, np.float32).reshape(NE, NE, 128, HD).transpose(0, 2, 1, 3).reshape(NE * 128, NE * HD))
    b1 = np.ascontiguousarray(np.asarray(L0_ffn_b1, np.float32).reshape(NE, 32, 128).transpose(0, 2, 1).reshape(NE * 128, 32))
    W2 = np.ascontiguousarray(np.asarray(L0_ffn_W2, np.float32).reshape(NE, 32, 128, D).transpose(0, 2, 1, 3).reshape(NE * 128, 32 * D))
    b2 = np.ascontiguousarray(np.asarray(L0_ffn_b2, np.float32))

    Lx = np.triu(np.ones((128, 128), np.float32), 1)
    ident = np.eye(128, dtype=np.float32)
    onesr = np.ones((1, 128), np.float32)
    iota8 = np.tile(np.arange(NE, dtype=np.float32), (128, 1))
    iotaP = np.ascontiguousarray(np.tile(np.arange(TOK, dtype=np.float32), (128, 1)))
    bigio = ((1 << 20) + np.arange(128, dtype=np.uint32)).reshape(128, 1)

    if _NC_CACHE[0] is None:
        _NC_CACHE[0] = build_nc()
    nc = _NC_CACHE[0]

    in_maps = []
    for c in range(NCORES):
        xrot = np.ascontiguousarray(np.concatenate([xf[c * TOK:], xf[:c * TOK]], axis=0))
        xrotT = np.ascontiguousarray(xrot.T)
        if c % 2 == 1:
            xpair = np.ascontiguousarray(xf[(c - 1) * TOK: c * TOK])
        else:
            xpair = np.zeros((TOK, D), np.float32)
        half = c % 2
        iotaT = np.ascontiguousarray(
            np.tile(np.arange(TOK, dtype=np.float32) + half * TOK, (128, 1)))
        owner = (c + (np.arange(256) // 32)) % NCORES
        pvm = np.ascontiguousarray(
            np.tile((owner < c).astype(np.float32), (NE, 1)))
        in_maps.append({
            "x": xrot, "xT": xrotT, "xpair": xpair, "Bw": Bw, "Cw": Cw, "GW": GW, "gb": gb,
            "ldec": ldec, "Wg": Wg, "W1": W1, "b1": b1, "W2": W2, "b2": b2,
            "Lx": Lx, "ident": ident, "onesr": onesr, "iota8": iota8,
            "iotaT": iotaT, "iotaP": iotaP, "bigio": bigio, "pvm": pvm,
        })

    res = run_bass_kernel_spmd(nc, in_maps, core_ids=list(range(NCORES)))
    out = np.concatenate([res.results[c]["out"] for c in range(NCORES)], axis=0)
    return out.reshape(4, 8192, 1024).astype(np.float32)


# version 37
# speedup vs baseline: 1.1446x; 1.0391x over previous
import contextlib
import math
import numpy as np
import ml_dtypes
import concourse.bass as bass
import concourse.mybir as mybir
from concourse.bass_utils import run_bass_kernel_spmd
from concourse.bass import IndirectOffsetOnAxis

F32 = mybir.dt.float32
BF16 = mybir.dt.bfloat16
U32 = mybir.dt.uint32
AF = mybir.ActivationFunctionType
ALU = mybir.AluOpType
AX = mybir.AxisListType
BF = ml_dtypes.bfloat16

NCORES = 8
TOK = 4096
TALL = 32768
D = 1024
SDIM = 64
HD = 4096
NE = 8
CAP = 8602.0
L = 1152
QROWS = NE * L
C2 = 2.0 * math.sqrt(2.0 / math.pi)
GC = 0.044715
EPS = 1e-5

ENGS = ("sync", "scalar", "vector", "gpsimd", "tensor")
_NEXP = NE
_NTIL = 3


class Sem:
    def __init__(self, h, name):
        self.h = h
        self.name = name
        self.n = 0


class Prog:
    def __init__(self):
        self.q = {e: [] for e in ENGS}

    def wait(self, eng, sem, val):
        if val and val > 0:
            self.q[eng].append(("wait", sem, int(val)))

    def op(self, eng, fn, inc=None):
        tgt = None
        if inc is not None:
            inc[0].n += inc[1]
            tgt = inc[0].n
        self.q[eng].append(("op", fn, inc))
        return tgt

    def dma(self, eng, sem, out, in_, **kw):
        return self.op(eng, lambda e, o=out, i=in_, k=dict(kw): e.dma_start(out=o, in_=i, **k),
                       inc=(sem, 16))

    def tdma(self, eng, sem, out, in_):
        return self.op(eng, lambda e, o=out, i=in_: e.dma_start(out=o, in_=i, transpose=True),
                       inc=(sem, 16))

    def drain(self, eng):
        self.op(eng, lambda e: e.drain())

    def mark(self, eng, sem):
        return self.op(eng, lambda e: e.drain(), inc=(sem, 1))

    def emit(self, eng_name, eng):
        for item in self.q[eng_name]:
            if item[0] == "wait":
                eng.wait_ge(item[1].h, item[2])
            else:
                inst = item[1](eng)
                if item[2] is not None:
                    assert inst is not None
                    inst.then_inc(item[2][0].h, item[2][1])


def simulate(progs):
    qs = {e: [] for e in ENGS}
    for p in progs:
        for e in ENGS:
            qs[e].extend(p.q[e])
    counts = {}
    ptr = {e: 0 for e in ENGS}
    progress = True
    while progress:
        progress = False
        for e in ENGS:
            while ptr[e] < len(qs[e]):
                item = qs[e][ptr[e]]
                if item[0] == "wait":
                    if counts.get(item[1], 0) < item[2]:
                        break
                elif item[2] is not None:
                    counts[item[2][0]] = counts.get(item[2][0], 0) + item[2][1]
                ptr[e] += 1
                progress = True
    stalled = {e: ptr[e] for e in ENGS if ptr[e] < len(qs[e])}
    if stalled:
        det = []
        for e, p in stalled.items():
            it = qs[e][p]
            det.append(f"{e}@{p}/{len(qs[e])} waits {it[1].name}>={it[2]}"
                       f" (have {counts.get(it[1], 0)})")
        raise RuntimeError("planned-program deadlock: " + "; ".join(det))


def emit_block(nc, progs):
    with nc.Block() as blk:
        @blk.sync
        def _(eng):
            for P in progs:
                P.emit("sync", eng)

        @blk.scalar
        def _(eng):
            for P in progs:
                P.emit("scalar", eng)

        @blk.vector
        def _(eng):
            for P in progs:
                P.emit("vector", eng)

        @blk.gpsimd
        def _(eng):
            for P in progs:
                P.emit("gpsimd", eng)

        @blk.tensor
        def _(eng):
            for P in progs:
                P.emit("tensor", eng)


def build_nc():
    nc = bass.Bass("TRN2", target_bir_lowering=False)
    dt = nc.dram_tensor
    d_x = dt("x", [TALL, D], F32, kind="ExternalInput").ap()
    d_xp = dt("xpair", [TOK, D], F32, kind="ExternalInput").ap()
    d_Bw = dt("Bw", [SDIM, D], F32, kind="ExternalInput").ap()
    d_Cw = dt("Cw", [128, NE * SDIM], F32, kind="ExternalInput").ap()
    d_GW = dt("GW", [D, D], F32, kind="ExternalInput").ap()
    d_gb = dt("gb", [128, NE], F32, kind="ExternalInput").ap()
    d_ld = dt("ldec", [128, NE], F32, kind="ExternalInput").ap()
    d_Wg = dt("Wg", [128, NE * NE], F32, kind="ExternalInput").ap()
    d_W1 = dt("W1", [NE * 128, NE * HD], F32, kind="ExternalInput").ap()
    d_W2 = dt("W2", [NE * 128, 32 * D], F32, kind="ExternalInput").ap()
    d_b1 = dt("b1", [NE * 128, 32], F32, kind="ExternalInput").ap()
    d_b2 = dt("b2", [NE, D], F32, kind="ExternalInput").ap()
    d_Lx = dt("Lx", [128, 128], F32, kind="ExternalInput").ap()
    d_id = dt("ident", [128, 128], F32, kind="ExternalInput").ap()
    d_onr = dt("onesr", [1, 128], F32, kind="ExternalInput").ap()
    d_io8 = dt("iota8", [128, NE], F32, kind="ExternalInput").ap()
    d_ioT = dt("iotaT", [128, TOK], F32, kind="ExternalInput").ap()
    d_ioP = dt("iotaP", [128, TOK], F32, kind="ExternalInput").ap()
    d_big = dt("bigio", [128, 1], U32, kind="ExternalInput").ap()
    d_pvm = dt("pvm", [NE, 256], F32, kind="ExternalInput").ap()
    d_out = dt("out", [TOK, D], F32, kind="ExternalOutput").ap()
    d_xT = dt("xT", [D, TALL], F32, kind="ExternalInput").ap()
    d_xnb = dt("xnb", [TOK, D], BF16, kind="Internal").ap()
    d_xnpb = dt("xnpb", [TOK, D], BF16, kind="Internal").ap()
    d_gwb = dt("gwb", [D, D], BF16, kind="Internal").ap()
    d_wfm = dt("wfm", [D, TOK], BF16, kind="Internal").ap()
    d_gfm = dt("gfm", [D, TOK], BF16, kind="Internal").ap()
    d_sfm = dt("sfm", [D, TOK], BF16, kind="Internal").ap()
    d_xq = dt("xq", [QROWS, D], BF16, kind="Internal").ap()
    d_yq = dt("yq", [QROWS, D], BF16, kind="Internal").ap()
    d_rtm = dt("rtm", [256, NE], F32, kind="Internal").ap()
    d_b2b = dt("b2b", [NE, D], BF16, kind="Internal").ap()
    d_w1b = dt("w1b", [NE * 128, NE * HD], BF16, kind="Internal").ap()
    d_w2b = dt("w2b", [NE * 128, 32 * D], BF16, kind="Internal").ap()

    progs = []
    holders = {}
    with contextlib.ExitStack() as top:
        def mksem(name):
            return Sem(top.enter_context(nc.semaphore(name)), name)

        s_ld = mksem("s_ld")
        s_st = mksem("s_st")
        s_gq = mksem("s_gq")
        s_pe = mksem("s_pe")
        s_dv = mksem("s_dv")
        s_ac = mksem("s_ac")
        s_gp = mksem("s_gp")
        s_sc = mksem("s_sc")

        def sb(stack, name, shape, dtp, **kw):
            return stack.enter_context(nc.sbuf_tensor(name, shape, dtp, **kw))

        c_id = sb(top, "c_id", [128, 128], F32)
        c_onr = sb(top, "c_onr", [1, 128], F32)
        c_onrb = sb(top, "c_onrb", [1, 128], BF16)
        c_io8 = sb(top, "c_io8", [128, NE], F32)
        c_bigf = sb(top, "c_bigf", [128, 1], F32)
        pk_i1 = sb(top, "pk_i1", [128, 32], F32)
        pk_i2 = sb(top, "pk_i2", [128, 32], F32)
        pk_c1 = sb(top, "pk_c1", [128, 32], F32)
        pk_c2 = sb(top, "pk_c2", [128, 32], F32)
        pk_zd = sb(top, "pk_zd", [128, 32], F32)
        pk_rs = sb(top, "pk_rs", [128, 32], F32)
        pk_p1 = sb(top, "pk_p1", [128, 32], F32)
        pk_p2 = sb(top, "pk_p2", [128, 32], F32)
        pk_w1 = sb(top, "pk_w1", [128, 32], F32)
        pk_w2 = sb(top, "pk_w2", [128, 32], F32)
        pk_g1 = sb(top, "pk_g1", [128, 32], U32)
        pk_g2 = sb(top, "pk_g2", [128, 32], U32)
        bc_reg = {}
        c_pvm = sb(top, "c_pvm", [NE, 256], F32)
        r_opref = sb(top, "r_opref", [128, NE], F32)
        t_oh1 = sb(top, "t_oh1", [128, NE * 4], F32)
        t_oh2 = sb(top, "t_oh2", [128, NE * 4], F32)
        t_tmp8 = sb(top, "t_tmp8", [128, NE * 8], F32)
        rt_bin = sb(top, "rt_bin", [NE, 256], F32)
        t_rt128 = sb(top, "t_rt128", [128, NE], F32)
        t_osel = sb(top, "t_osel", [128, 16], F32)
        t_kv = sb(top, "t_kv", [128, 16], F32)
        t_vm = sb(top, "t_vm", [128, 16], U32)
        t_iv = sb(top, "t_iv", [128, 16], F32)
        t_g1c = sb(top, "t_g1c", [128, 32], F32)
        t_g2c = sb(top, "t_g2c", [128, 32], F32)
        v_pref = sb(top, "v_pref", [128, NE * 8], F32)
        t_oprow = sb(top, "t_oprow", [1, NE], F32)
        t_opcol = sb(top, "t_opcol", [NE, 1], F32)
        t_pvmul = sb(top, "t_pvmul", [NE, 256], F32)

        with contextlib.ExitStack() as A:
            c_idb = sb(A, "c_idb", [128, 128], BF16)
            c_Lxb = sb(A, "c_Lxb", [128, 128], BF16)
            c_oncb = sb(A, "c_oncb", [128, 1], BF16)
            rt_base = sb(A, "rt_base", [NE, 32], F32)
            r_tot = sb(A, "r_tot", [1, 2048], F32)
            r_srep = sb(A, "r_srep", [128, NE], F32)
            w_ext = sb(A, "w_ext", [128, 9 * NE], BF16)
            w_BwT = sb(A, "w_BwT", [128, SDIM * NE], BF16)
            w_CwT = sb(A, "w_CwT", [64, 128 * NE], BF16)
            w_GWT = sb(A, "w_GWT", [128, D * NE], BF16)
            t_gwraw = sb(A, "t_gwraw", [128, D * NE], BF16)
            v_lnd = sb(A, "v_lnd", [128, NE], F32)
            v_gb = sb(A, "v_gb", [128, NE], F32)
            t_wgf = sb(A, "t_wgf", [128, NE * NE], F32)
            t_ldf = sb(A, "t_ldf", [128, NE], F32)
            t_dec = sb(A, "t_dec", [128, NE], F32)
            t_bwf = sb(A, "t_bwf", [64, D], F32)
            t_bwb = sb(A, "t_bwb", [64, D], BF16)
            t_cwf = sb(A, "t_cwf", [128, SDIM * NE], F32)
            t_cwb = sb(A, "t_cwb", [128, SDIM * NE], BF16)
            t_lxf = sb(A, "t_lxf", [128, 128], F32)
            t_srow = sb(A, "t_srow", [1, NE], F32)
            t_bigu = sb(A, "t_bigu", [128, 1], U32)
            t_xfm = sb(A, "t_xfm", [128, 512 * NE], BF16)
            t_xff = [sb(A, f"t_xff{p}", [128, 512 * NE], F32)
                     for p in range(2)]
            t_a = sb(A, "t_a", [128, NE * 4], F32)
            t_zsh = sb(A, "t_zsh", [128, NE * 4], F32)
            t_mx = sb(A, "t_mx", [128, NE * 4], F32)
            t_mi = sb(A, "t_mi", [128, NE * 4], U32)
            t_mif = sb(A, "t_mif", [128, 8], F32)
            t_S = sb(A, "t_S", [128, NE * 4], BF16)
            t_x = [sb(A, f"t_x{p}", [128, D * 4], F32) for p in range(2)]
            t_xn = [sb(A, f"t_xn{p}", [128, D * 4], BF16) for p in range(2)]
            t_bnst = sb(A, "t_bnst", [128, 12 * 4], F32)
            t_bnag = sb(A, "t_bnag", [128, 2 * 4], F32)
            t_sd = sb(A, "t_sd", [128, 4], F32)
            t_rst = sb(A, "t_rst", [128, 4], F32)
            t_anfm = [sb(A, f"t_anfm{p}", [128, 512 * NE], BF16, is_transpose_dest=True)
                      for p in range(2)]
            t_bu = sb(A, "t_bu", [64, 512], BF16)
            t_iot = [sb(A, f"t_iot{p}", [128, 512], F32) for p in range(2)]
            t_expo = [sb(A, f"t_expo{p}", [128, 512], F32) for p in range(2)]
            t_pw = [sb(A, f"t_pw{p}", [128, 512], F32) for p in range(2)]
            t_wsb = [sb(A, f"t_wsb{p}", [128, 512], BF16) for p in range(2)]
            t_wpf = [sb(A, f"t_wpf{p}", [128, 512], F32) for p in range(2)]
            t_gsb = [sb(A, f"t_gsb{p}", [128, 512], BF16) for p in range(2)]
            t_scxn = [sb(A, f"t_scxn{p}", [128, D], BF16) for p in range(2)]
            t_idx = [sb(A, f"t_idx{p}", [128, 16], U32) for p in range(2)]
            t_rt32 = sb(A, "t_rt32", [32, NE], F32)
            t_brep = sb(A, "t_brep", [128, 256], F32)
            t_rowb = sb(A, "t_rowb", [1, 256], F32)
            t_eps = sb(A, "t_eps", [128, 1], F32)

            ps_ztm = [A.enter_context(nc.psum_tensor(f"ps_ztm{i}", [128, 36], F32))
                      for i in range(2)]
            ps_tot = A.enter_context(nc.psum_tensor("ps_tot", [1, 256], F32))
            ps_pos = A.enter_context(nc.psum_tensor("ps_pos", [128, 32], F32))
            ps_bu = A.enter_context(nc.psum_tensor("ps_bu", [64, 512], F32))
            prepstack = contextlib.ExitStack()
            ps_tb = prepstack.enter_context(nc.psum_tensor("ps_tb", [128, 128], BF16))

            P = Prog()
            progs.append(P)

            P.dma("sync", s_ld, c_id[:], d_id[:])
            P.dma("sync", s_ld, t_lxf[:], d_Lx[:])
            P.dma("sync", s_ld, c_onr[:], d_onr[:])
            P.dma("sync", s_ld, c_io8[:], d_io8[:])
            P.dma("sync", s_ld, t_bigu[:], d_big[:])
            P.dma("sync", s_ld, c_pvm[:], d_pvm[:])
            v_consts = s_ld.n
            P.dma("scalar", s_st, t_wgf[:], d_Wg[:])
            P.dma("scalar", s_st, t_ldf[:], d_ld[:])
            P.dma("scalar", s_st, v_gb[:], d_gb[:])
            P.dma("scalar", s_st, t_bwf[:], d_Bw[:])
            P.dma("scalar", s_st, t_cwf[:], d_Cw[:])
            v_consts2 = s_st.n

            def _init_bcreg(e):
                bc_reg["r"] = e.alloc_register(name="bcreg")
                return e.reg_mov(bc_reg["r"], QROWS - 1)

            P.op("gpsimd", _init_bcreg)
            P.dma("gpsimd", s_gq, d_gwb.rearrange("(a p) d -> a (p d)", p=128),
                  d_GW.rearrange("(a p) d -> a (p d)", p=128))
            v_gwb = s_gq.n
            P.dma("gpsimd", s_gq, d_b2b[:], d_b2[:])
            v_b2cast = s_gq.n
            wpc = {}
            holders["wpc"] = wpc

            wpc_dr = {}

            def emit_wprecast(e):
                for q4 in range(4):
                    P.dma("gpsimd", s_gq,
                          d_w1b[e * 128 + q4 * 32:e * 128 + (q4 + 1) * 32, :],
                          d_W1[e * 128 + q4 * 32:e * 128 + (q4 + 1) * 32, :])
                for q4 in range(4):
                    P.dma("gpsimd", s_gq,
                          d_w2b[e * 128 + q4 * 32:e * 128 + (q4 + 1) * 32, :],
                          d_W2[e * 128 + q4 * 32:e * 128 + (q4 + 1) * 32, :])
                wpc[e] = s_gq.n
                wpc_dr[e] = P.op("gpsimd", lambda e_: e_.drain(), inc=(s_gp, 1))
            holders["emit_wprecast"] = emit_wprecast
            holders["wpc_dr"] = wpc_dr

            for r in range(NE):
                P.dma("gpsimd", s_gq, t_gwraw[:, r * D:(r + 1) * D],
                      d_gwb[r * 128:(r + 1) * 128, :])
            v_gwraw = s_gq.n
            v_prol_dr = P.op("gpsimd", lambda e: e.drain(), inc=(s_gp, 1))
            holders["v_prol_dr"] = v_prol_dr
            emit_wprecast(0)

            P.wait("vector", s_ld, v_consts)
            P.wait("vector", s_st, v_consts2)
            P.op("vector", lambda e: e.tensor_copy(out=c_idb[:], in_=c_id[:]))
            P.op("vector", lambda e: e.tensor_copy(out=c_Lxb[:], in_=t_lxf[:]))
            P.op("vector", lambda e: e.memset(c_oncb[:], 1.0))
            P.op("vector", lambda e: e.tensor_copy(out=c_bigf[:], in_=t_bigu[:]))
            P.op("vector", lambda e: e.memset(rt_base[:], 0.0))
            P.op("vector", lambda e: e.memset(t_eps[:], EPS))
            for k in range(NE):
                P.op("vector", lambda e, k=k: e.tensor_copy(
                    out=w_ext[:, k * 9:k * 9 + 8], in_=t_wgf[:, k * NE:(k + 1) * NE]))
                P.op("vector", lambda e, k=k: e.memset(w_ext[:, k * 9 + 8:k * 9 + 9], 1.0))
            P.op("vector", lambda e: e.tensor_copy(out=t_bwb[:], in_=t_bwf[:]))
            P.op("vector", lambda e: e.tensor_copy(out=t_cwb[:], in_=t_cwf[:]))
            v_prep = P.mark("vector", s_dv)

            holders["v_b2cast"] = v_b2cast

            P.wait("scalar", s_st, v_consts2)
            P.op("scalar", lambda e: e.activation(out=t_dec[:], in_=t_ldf[:], func=AF.Sigmoid))
            a_sig = P.mark("scalar", s_ac)
            P.wait("vector", s_ac, a_sig)
            P.op("vector", lambda e: e.tensor_scalar(out=t_dec[:], in0=t_dec[:], scalar1=1e-19,
                                                     scalar2=None, op0=ALU.max))
            v_clamp = P.mark("vector", s_dv)
            P.wait("scalar", s_dv, v_clamp)
            P.op("scalar", lambda e: e.activation(out=v_lnd[:], in_=t_dec[:], func=AF.Ln))
            a_lnd = P.mark("scalar", s_ac)

            P.wait("tensor", s_dv, v_prep)
            for k in range(NE):
                pv = P.op("tensor", lambda e, k=k: e.transpose(
                    out=ps_tb[:, 0:64], in_=t_bwb[:, k * 128:(k + 1) * 128],
                    identity=c_idb[0:64, 0:64]), inc=(s_pe, 1))
                P.wait("vector", s_pe, pv)
                dv = P.op("vector", lambda e, k=k: e.tensor_copy(
                    out=w_BwT[:, k * SDIM:(k + 1) * SDIM], in_=ps_tb[:, 0:64]), inc=(s_dv, 1))
                P.wait("tensor", s_dv, dv)
            for dm in range(NE):
                pv = P.op("tensor", lambda e, dm=dm: e.transpose(
                    out=ps_tb[0:64, 0:128], in_=t_cwb[:, dm * SDIM:(dm + 1) * SDIM],
                    identity=c_idb[:, 0:128]), inc=(s_pe, 1))
                P.wait("vector", s_pe, pv)
                dv = P.op("vector", lambda e, dm=dm: e.tensor_copy(
                    out=w_CwT[:, dm * 128:(dm + 1) * 128], in_=ps_tb[0:64, 0:128]), inc=(s_dv, 1))
                P.wait("tensor", s_dv, dv)
            for k in range(NE):
                P.op("tensor", lambda e, k=k: e.matmul(
                    out=ps_ztm[0][0:1, 0:9], lhsT=c_oncb[:], rhs=w_ext[:, k * 9:(k + 1) * 9],
                    start=(k == 0), stop=(k == NE - 1)))
            pv = P.mark("tensor", s_pe)
            P.wait("vector", s_pe, pv)
            P.op("vector", lambda e: e.tensor_scalar(out=t_srow[:], in0=ps_ztm[0][0:1, 0:NE],
                                                     scalar1=1.0 / D, scalar2=None, op0=ALU.mult))
            dv = P.mark("vector", s_dv)
            P.wait("tensor", s_dv, dv)
            pv = P.op("tensor", lambda e: e.matmul(out=ps_pos[:, 0:NE], lhsT=c_onr[:],
                                                   rhs=t_srow[:], start=True, stop=True),
                      inc=(s_pe, 1))
            P.wait("vector", s_pe, pv)
            v_srep = P.op("vector", lambda e: e.tensor_copy(out=r_srep[:], in_=ps_pos[:, 0:NE]),
                          inc=(s_dv, 1))


            prepstack.close()
            ps_bp = [A.enter_context(nc.psum_tensor(f"ps_bp{i}", [128, 512], F32))
                     for i in range(2)]
            ps_gt = A.enter_context(nc.psum_tensor("ps_gt", [128, 1024], BF16))

            zmm_done = {}
            zfree = {}
            stageF = {}
            totals_pe = {}
            seg_evac = {}
            pos_used = {}
            tot_extra = {}
            xfm_cast = {}
            first_own_mm = [True]

            def routing_chunk(ch):
                par = ch % 2
                own = ch < 8
                seg = ch // 8
                if ch >= 2:
                    P.wait("sync", s_dv, xfm_cast[ch - 2])
                for k in range(NE):
                    P.dma("sync", s_ld, t_xff[par][:, k * 512:(k + 1) * 512],
                          d_xT[k * 128:(k + 1) * 128, ch * 512:(ch + 1) * 512])
                v_x = s_ld.n
                P.wait("vector", s_ld, v_x)
                if ch >= 1:
                    P.wait("vector", s_pe, zmm_done[ch - 1])
                P.op("vector", lambda e, par=par: e.tensor_copy(out=t_xfm[:],
                                                                in_=t_xff[par][:]))
                xfm_cast[ch] = P.mark("vector", s_dv)
                P.wait("tensor", s_dv, xfm_cast[ch])
                if ch == 0:
                    P.wait("tensor", s_dv, v_prep)
                    P.wait("tensor", s_dv, v_srep)
                if ch >= 2:
                    P.wait("tensor", s_dv, zfree[ch - 2])
                for q in range(4):
                    for k in range(NE):
                        P.op("tensor", lambda e, q=q, k=k, par=par: e.matmul(
                            out=ps_ztm[par][:, q * 9:(q + 1) * 9],
                            lhsT=t_xfm[:, k * 512 + q * 128:k * 512 + (q + 1) * 128],
                            rhs=w_ext[:, k * 9:(k + 1) * 9],
                            start=(k == 0), stop=(k == NE - 1)),
                            inc=(s_pe, 1) if (q == 3 and k == NE - 1) else None)
                zmm_done[ch] = s_pe.n
                if ch >= 1:
                    P.wait("vector", s_pe, totals_pe[ch - 1])
                P.wait("vector", s_pe, zmm_done[ch])
                for q in range(4):
                    P.op("vector", lambda e, q=q, par=par: e.tensor_scalar(
                        out=t_a[:, q * 8:(q + 1) * 8], in0=r_srep[:],
                        scalar1=ps_ztm[par][:, q * 9 + 8:q * 9 + 9], scalar2=1.0 / D,
                        op0=ALU.mult, op1=ALU.mult))
                P.drain("vector")
                for q in range(4):
                    P.op("vector", lambda e, q=q, par=par: e.tensor_tensor(
                        out=t_zsh[:, q * 8:(q + 1) * 8], in0=ps_ztm[par][:, q * 9:q * 9 + 8],
                        in1=t_a[:, q * 8:(q + 1) * 8], op=ALU.subtract))
                zfree[ch] = P.mark("vector", s_dv)
                P.drain("vector")
                for q in range(4):
                    P.op("vector", lambda e, q=q: e.max(
                        out=t_mx[:, q * 8:(q + 1) * 8], in_=t_zsh[:, q * 8:(q + 1) * 8]))
                P.drain("vector")
                for q in range(4):
                    P.op("vector", lambda e, q=q: e.max_index(
                        out=t_mi[:, q * 8:(q + 1) * 8], in_max=t_mx[:, q * 8:(q + 1) * 8],
                        in_values=t_zsh[:, q * 8:(q + 1) * 8]))
                P.drain("vector")
                for q in range(4):
                    P.op("vector", lambda e, q=q: e.tensor_copy(
                        out=t_mif[:, q * 2:q * 2 + 1], in_=t_mi[:, q * 8:q * 8 + 1]))
                    P.op("vector", lambda e, q=q: e.tensor_copy(
                        out=t_mif[:, q * 2 + 1:q * 2 + 2], in_=t_mi[:, q * 8 + 1:q * 8 + 2]))
                P.drain("vector")
                for q in range(4):
                    P.op("vector", lambda e, q=q: e.tensor_scalar(
                        out=t_oh1[:, q * 8:(q + 1) * 8], in0=c_io8[:],
                        scalar1=t_mif[:, q * 2:q * 2 + 1], scalar2=None, op0=ALU.is_equal))
                    P.op("vector", lambda e, q=q: e.tensor_scalar(
                        out=t_oh2[:, q * 8:(q + 1) * 8], in0=c_io8[:],
                        scalar1=t_mif[:, q * 2 + 1:q * 2 + 2], scalar2=None, op0=ALU.is_equal))
                P.drain("vector")
                for q in range(4):
                    P.op("vector", lambda e, q=q: e.tensor_tensor(
                        out=t_S[:, q * 8:(q + 1) * 8], in0=t_oh1[:, q * 8:(q + 1) * 8],
                        in1=t_oh2[:, q * 8:(q + 1) * 8], op=ALU.add))
                stageF[ch] = P.mark("vector", s_dv)
                P.wait("tensor", s_dv, stageF[ch])
                if ch % 8 == 0 and seg >= 1:
                    P.wait("tensor", s_dv, seg_evac[seg - 1])
                if ch in tot_extra:
                    P.wait("tensor", s_dv, tot_extra[ch])
                for q in range(4):
                    T = ch * 4 + q
                    P.op("tensor", lambda e, q=q, T=T: e.matmul(
                        out=ps_tot[0:1, (T % 32) * 8:(T % 32) * 8 + 8], lhsT=c_oncb[:],
                        rhs=t_S[:, q * 8:(q + 1) * 8], start=True, stop=True))
                if own:
                    if first_own_mm[0]:
                        P.wait("tensor", s_dv, v_srep)
                        first_own_mm[0] = False
                    if ch >= 1:
                        P.wait("tensor", s_dv, pos_used[ch - 1])
                    for q in range(4):
                        P.op("tensor", lambda e, q=q: e.matmul(
                            out=ps_pos[:, q * 8:(q + 1) * 8], lhsT=c_Lxb[:],
                            rhs=t_S[:, q * 8:(q + 1) * 8], start=True, stop=True))
                totals_pe[ch] = P.mark("tensor", s_pe)
                if own:
                    P.wait("vector", s_pe, totals_pe[ch])
                    for q in range(4):
                        P.op("vector", lambda e, q=q: e.tensor_tensor(
                            out=t_tmp8[:, q * 8:(q + 1) * 8], in0=ps_pos[:, q * 8:(q + 1) * 8],
                            in1=t_oh1[:, q * 8:(q + 1) * 8], op=ALU.mult))
                        P.op("vector", lambda e, q=q: e.tensor_tensor(
                            out=t_tmp8[:, 32 + q * 8:40 + q * 8], in0=ps_pos[:, q * 8:(q + 1) * 8],
                            in1=t_oh2[:, q * 8:(q + 1) * 8], op=ALU.mult))
                    P.drain("vector")
                    for q in range(4):
                        T = ch * 4 + q
                        P.op("vector", lambda e, q=q, T=T: e.reduce_sum(
                            out=pk_c1[:, T:T + 1], in_=t_tmp8[:, q * 8:(q + 1) * 8], axis=AX.X))
                        P.op("vector", lambda e, q=q, T=T: e.reduce_sum(
                            out=pk_c2[:, T:T + 1], in_=t_tmp8[:, 32 + q * 8:40 + q * 8], axis=AX.X))
                        P.op("vector", lambda e, q=q, T=T: e.tensor_tensor(
                            out=pk_zd[:, T:T + 1], in0=t_mx[:, q * 8:q * 8 + 1],
                            in1=t_mx[:, q * 8 + 1:q * 8 + 2], op=ALU.subtract))
                        P.op("vector", lambda e, q=q, T=T: e.tensor_copy(
                            out=pk_i1[:, T:T + 1], in_=t_mi[:, q * 8:q * 8 + 1]))
                        P.op("vector", lambda e, q=q, T=T: e.tensor_copy(
                            out=pk_i2[:, T:T + 1], in_=t_mi[:, q * 8 + 1:q * 8 + 2]))
                    pos_used[ch] = P.mark("vector", s_dv)
                if ch % 8 == 7:
                    P.wait("vector", s_pe, totals_pe[ch])
                    P.op("vector", lambda e, seg=seg: e.tensor_copy(
                        out=r_tot[0:1, seg * 256:(seg + 1) * 256], in_=ps_tot[:]))
                    seg_evac[seg] = P.mark("vector", s_dv)

            ln_apply = {}
            last_sqrt = [0]
            xnb_store = {}

            def ln_chunk(ch, pair):
                par = ch % 2
                src = d_xp if pair else d_x
                dst = d_xnpb if pair else d_xnb
                key = ("p" if pair else "o", ch)
                prev = ("p" if pair else "o", ch - 2)
                if prev in ln_apply:
                    P.wait("scalar", s_dv, ln_apply[prev])
                elif pair and ch < 2:
                    P.wait("scalar", s_dv, ln_apply[("o", 6 + ch)])
                for t in range(4):
                    P.dma("scalar", s_st, t_x[par][:, t * D:(t + 1) * D],
                          src[ch * 512 + t * 128: ch * 512 + (t + 1) * 128, :])
                v_x = s_st.n
                P.wait("vector", s_st, v_x)
                P.wait("vector", s_ac, last_sqrt[0])
                for t in range(4):
                    P.op("vector", lambda e, t=t, par=par: e.bn_stats(
                        out=t_bnst[:, t * 12:t * 12 + 6], in_=t_x[par][:, t * D:t * D + 512]))
                    P.op("vector", lambda e, t=t, par=par: e.bn_stats(
                        out=t_bnst[:, t * 12 + 6:t * 12 + 12],
                        in_=t_x[par][:, t * D + 512:(t + 1) * D]))
                P.drain("vector")
                for t in range(4):
                    P.op("vector", lambda e, t=t: e.bn_aggr(
                        out=t_bnag[:, t * 2:(t + 1) * 2], in_=t_bnst[:, t * 12:(t + 1) * 12]))
                v_agg = P.mark("vector", s_dv)
                P.wait("scalar", s_dv, v_agg)
                for t in range(4):
                    P.op("scalar", lambda e, t=t: e.activation(
                        out=t_sd[:, t:t + 1], in_=t_bnag[:, t * 2 + 1:t * 2 + 2],
                        func=AF.Sqrt, bias=t_eps[:]))
                last_sqrt[0] = P.mark("scalar", s_ac)
                P.wait("vector", s_ac, last_sqrt[0])
                for t in range(4):
                    P.op("vector", lambda e, t=t: e.reciprocal(
                        out=t_rst[:, t:t + 1], in_=t_sd[:, t:t + 1]))
                P.drain("vector")
                for t in range(4):
                    P.op("vector", lambda e, t=t, par=par: e.tensor_scalar(
                        out=t_xn[par][:, t * D:(t + 1) * D], in0=t_x[par][:, t * D:(t + 1) * D],
                        scalar1=t_bnag[:, t * 2:t * 2 + 1], scalar2=t_rst[:, t:t + 1],
                        op0=ALU.subtract, op1=ALU.mult))
                    if not pair:
                        P.op("vector", lambda e, t=t, ch=ch: e.tensor_copy(
                            out=pk_rs[:, ch * 4 + t:ch * 4 + t + 1], in_=t_rst[:, t:t + 1]))
                ln_apply[key] = P.mark("vector", s_dv)
                P.wait("scalar", s_dv, ln_apply[key])
                for t in range(4):
                    P.dma("scalar", s_st, dst[ch * 512 + t * 128: ch * 512 + (t + 1) * 128, :],
                          t_xn[par][:, t * D:(t + 1) * D])
                return s_st.n

            for ch in range(8):
                routing_chunk(ch)
                xnb_store[ch] = ln_chunk(ch, pair=False)

            P.wait("scalar", s_dv, seg_evac[0])
            P.dma("scalar", s_st, d_rtm[0:32, :], r_tot[0:1, 0:256])
            v_rtm = s_st.n
            P.wait("sync", s_st, v_rtm)
            P.dma("sync", s_ld, t_rt32[:], d_rtm[0:32, :])
            v_rt32 = s_ld.n
            P.wait("tensor", s_ld, v_rt32)
            pv = P.op("tensor", lambda e: e.transpose(
                out=ps_bu[0:8, 0:32], in_=t_rt32[:], identity=c_id[0:32, 0:32]), inc=(s_pe, 1))
            P.wait("vector", s_pe, pv)
            P.op("vector", lambda e: e.tensor_copy(out=rt_bin[:, 0:32], in_=ps_bu[0:8, 0:32]))
            P.drain("vector")
            P.op("vector", lambda e: e.tensor_tensor_scan(
                out=rt_base[:, 1:32], data0=rt_bin[:, 0:31], data1=rt_bin[:, 0:31],
                initial=0.0, op0=ALU.add, op1=ALU.bypass))
            v_base = P.mark("vector", s_dv)
            v_bu_free = v_base

            P.wait("tensor", s_dv, v_base)
            P.wait("tensor", s_dv, seg_evac[0])
            for T in range(32):
                P.op("tensor", lambda e, T=T: e.matmul(
                    out=ps_tot[0:1, (T % 32) * 8:(T % 32) * 8 + 8], lhsT=rt_base[:, T:T + 1],
                    rhs=c_id[0:8, 0:8], start=True, stop=True))
            pv = P.mark("tensor", s_pe)
            P.wait("vector", s_pe, pv)
            P.op("vector", lambda e: e.tensor_copy(out=t_rowb[:], in_=ps_tot[:]))
            v_rowb = P.mark("vector", s_dv)
            tot_extra[8] = v_rowb
            P.wait("tensor", s_dv, v_rowb)
            P.wait("tensor", s_dv, pos_used[7])
            v_brep_last = 0
            for half in range(8):
                for T in range(half * 4, (half + 1) * 4):
                    P.op("tensor", lambda e, T=T: e.matmul(
                        out=ps_pos[:, (T % 4) * 8:(T % 4) * 8 + 8], lhsT=c_onr[:],
                        rhs=t_rowb[0:1, T * 8:(T + 1) * 8], start=True, stop=True))
                pv = P.mark("tensor", s_pe)
                P.wait("vector", s_pe, pv)
                for T in range(half * 4, (half + 1) * 4):
                    P.op("vector", lambda e, T=T: e.tensor_copy(
                        out=t_brep[:, T * 8:(T + 1) * 8],
                        in_=ps_pos[:, (T % 4) * 8:(T % 4) * 8 + 8]))
                v_brep_last = P.mark("vector", s_dv)
                if half < 7:
                    P.wait("tensor", s_dv, v_brep_last)

            scxn_used = {}
            scat_batch = {}
            for b8 in range(4):
                tiles = list(range(b8 * 8, (b8 + 1) * 8))
                if b8 >= 2:
                    P.wait("vector", s_gq, scat_batch[b8 - 2])
                for T in tiles:
                    P.op("vector", lambda e, T=T: e.tensor_scalar(
                        out=t_oh1[:, (T % 4) * 8:(T % 4) * 8 + 8], in0=c_io8[:],
                        scalar1=pk_i1[:, T:T + 1], scalar2=None, op0=ALU.is_equal))
                    P.op("vector", lambda e, T=T: e.tensor_scalar(
                        out=t_oh2[:, (T % 4) * 8:(T % 4) * 8 + 8], in0=c_io8[:],
                        scalar1=pk_i2[:, T:T + 1], scalar2=None, op0=ALU.is_equal))
                    if T % 4 == 3:
                        P.drain("vector")
                        for Q in range(T - 3, T + 1):
                            P.op("vector", lambda e, Q=Q: e.tensor_tensor(
                                out=t_tmp8[:, (Q % 4) * 8:(Q % 4) * 8 + 8],
                                in0=t_brep[:, Q * 8:(Q + 1) * 8],
                                in1=t_oh1[:, (Q % 4) * 8:(Q % 4) * 8 + 8], op=ALU.mult))
                            P.op("vector", lambda e, Q=Q: e.tensor_tensor(
                                out=t_tmp8[:, 32 + (Q % 4) * 8:40 + (Q % 4) * 8],
                                in0=t_brep[:, Q * 8:(Q + 1) * 8],
                                in1=t_oh2[:, (Q % 4) * 8:(Q % 4) * 8 + 8], op=ALU.mult))
                        P.drain("vector")
                        for Q in range(T - 3, T + 1):
                            P.op("vector", lambda e, Q=Q: e.reduce_sum(
                                out=t_osel[:, (Q % 8) * 2:(Q % 8) * 2 + 1],
                                in_=t_tmp8[:, (Q % 4) * 8:(Q % 4) * 8 + 8], axis=AX.X))
                            P.op("vector", lambda e, Q=Q: e.reduce_sum(
                                out=t_osel[:, (Q % 8) * 2 + 1:(Q % 8) * 2 + 2],
                                in_=t_tmp8[:, 32 + (Q % 4) * 8:40 + (Q % 4) * 8], axis=AX.X))
                        P.drain("vector")
                        for Q in range(T - 3, T + 1):
                            P.op("vector", lambda e, Q=Q: e.tensor_tensor(
                                out=pk_p1[:, Q:Q + 1], in0=pk_c1[:, Q:Q + 1],
                                in1=t_osel[:, (Q % 8) * 2:(Q % 8) * 2 + 1], op=ALU.add))
                            P.op("vector", lambda e, Q=Q: e.tensor_tensor(
                                out=pk_p2[:, Q:Q + 1], in0=pk_c2[:, Q:Q + 1],
                                in1=t_osel[:, (Q % 8) * 2 + 1:(Q % 8) * 2 + 2], op=ALU.add))
                        P.drain("vector")
                        for Q in range(T - 3, T + 1):
                            for kk, pk_i, pk_p in ((0, pk_i1, pk_p1), (1, pk_i2, pk_p2)):
                                col = (Q % 8) * 2 + kk
                                P.op("vector", lambda e, col=col, Q=Q, pk_i=pk_i, pk_p=pk_p:
                                     e.scalar_tensor_tensor(
                                         out=t_iv[:, col:col + 1], in0=pk_i[:, Q:Q + 1],
                                         scalar=float(L), in1=pk_p[:, Q:Q + 1],
                                         op0=ALU.mult, op1=ALU.add))
                                P.op("vector", lambda e, col=col, Q=Q, pk_p=pk_p: e.tensor_scalar(
                                    out=t_vm[:, col:col + 1], in0=pk_p[:, Q:Q + 1],
                                    scalar1=float(L), scalar2=None, op0=ALU.is_lt))
                        P.drain("vector")
                        for Q in range(T - 3, T + 1):
                            for kk in range(2):
                                col = (Q % 8) * 2 + kk
                                P.op("vector", lambda e, col=col: e.select(
                                    out=t_kv[:, col:col + 1], mask=t_vm[:, col:col + 1],
                                    on_true=t_iv[:, col:col + 1], on_false=c_bigf[:]))
                        P.drain("vector")
                        for Q in range(T - 3, T + 1):
                            for kk in range(2):
                                col = (Q % 8) * 2 + kk
                                P.op("vector", lambda e, col=col, b8=b8: e.tensor_copy(
                                    out=t_idx[b8 % 2][:, col:col + 1], in_=t_kv[:, col:col + 1]))
                dvb = P.mark("vector", s_dv)
                for T in tiles:
                    par = T % 2
                    P.wait("sync", s_st, xnb_store[T // 4])
                    if T >= 2:
                        P.wait("sync", s_gq, scxn_used[T - 2])
                    P.dma("sync", s_ld, t_scxn[par][:], d_xnb[T * 128:(T + 1) * 128, :])
                    v_xn = s_ld.n
                    P.wait("gpsimd", s_dv, dvb)
                    P.wait("gpsimd", s_ld, v_xn)
                    for kk in range(2):
                        P.op("gpsimd", lambda e, T=T, kk=kk, par=par, b8=b8: e.indirect_dma_start(
                            out=d_xq[:], out_offset=IndirectOffsetOnAxis(
                                ap=t_idx[b8 % 2][:, (T % 8) * 2 + kk:(T % 8) * 2 + kk + 1],
                                axis=0),
                            in_=t_scxn[par][:], in_offset=None,
                            bounds_check=bc_reg["r"], oob_is_err=False), inc=(s_gq, 16))
                    scxn_used[T] = s_gq.n
                scat_batch[b8] = s_gq.n
            v_scat_all = s_gq.n
            v_scat_drain = P.op("gpsimd", lambda e: e.drain(), inc=(s_gp, 1))

            for ch in range(8):
                ln_chunk(ch, pair=True)
            v_xnpb_all = s_st.n

            P.wait("tensor", s_gq, v_gwraw)
            P.wait("tensor", s_gp, v_prol_dr)
            gwt_dv = {}
            for k in range(NE):
                if k >= 1:
                    P.wait("tensor", s_dv, gwt_dv[k - 1])
                for r in range(NE):
                    pv = P.op("tensor", lambda e, k=k, r=r: e.transpose(
                        out=ps_gt[:, r * 128:(r + 1) * 128],
                        in_=t_gwraw[:, r * D + k * 128:r * D + (k + 1) * 128],
                        identity=c_idb[:, 0:128]), inc=(s_pe, 1) if r == NE - 1 else None)
                P.wait("vector", s_pe, pv)
                gwt_dv[k] = P.op("vector", lambda e, k=k: e.tensor_copy(
                    out=w_GWT[:, k * D:(k + 1) * D], in_=ps_gt[:]), inc=(s_dv, 1))
            P.wait("tensor", s_dv, gwt_dv[NE - 1])

            bumm = {}
            bu_evac = {}
            bp_last = {}
            w_consumed = {}
            pw_done = {}
            g_consumed = {}
            wst = {}
            gst = {}
            anfm_free = {}
            bp_free = {}
            attn_seq = []

            def attn_chunk(ch, pair):
                par = len(attn_seq) % 2
                key = ("p" if pair else "o", ch)
                attn_seq.append(key)
                srcn = d_xnpb if pair else d_xnb
                iosrc = d_ioP if pair else d_ioT
                if len(attn_seq) >= 3:
                    P.wait("sync", s_pe, anfm_free[attn_seq[-3]])
                    P.wait("sync", s_dv, w_consumed[(attn_seq[-3], NE - 1)])
                P.wait("sync", s_st, v_xnpb_all if pair else xnb_store[ch])
                for k in range(NE):
                    P.tdma("sync", s_ld, t_anfm[par][:, k * 512:(k + 1) * 512],
                           srcn[ch * 512:(ch + 1) * 512, k * 128:(k + 1) * 128])
                P.dma("sync", s_ld, t_iot[par][:], iosrc[:, ch * 512:(ch + 1) * 512])
                v_an = s_ld.n
                P.wait("tensor", s_ld, v_an)
                if len(attn_seq) >= 2:
                    P.wait("tensor", s_dv, bu_evac[attn_seq[-2]])
                else:
                    P.wait("tensor", s_dv, v_bu_free)
                for k in range(NE):
                    P.op("tensor", lambda e, k=k, par=par: e.matmul(
                        out=ps_bu[:], lhsT=w_BwT[:, k * SDIM:(k + 1) * SDIM],
                        rhs=t_anfm[par][:, k * 512:(k + 1) * 512],
                        start=(k == 0), stop=(k == NE - 1)))
                bumm[key] = P.mark("tensor", s_pe)
                P.wait("vector", s_pe, bumm[key])
                if len(attn_seq) >= 2:
                    P.wait("vector", s_pe, bp_last[attn_seq[-2]])
                P.op("vector", lambda e: e.tensor_copy(out=t_bu[:], in_=ps_bu[:]))
                bu_evac[key] = P.mark("vector", s_dv)
                P.wait("tensor", s_dv, bu_evac[key])
                prev_free = bp_free.get(attn_seq[-2]) if len(attn_seq) >= 2 else None
                for dm in range(NE):
                    if dm >= 2:
                        P.wait("tensor", s_dv, w_consumed[(key, dm - 2)])
                    elif prev_free is not None:
                        kind, val = prev_free[dm % 2]
                        P.wait("tensor", s_ac if kind == "ac" else s_dv, val)
                    P.op("tensor", lambda e, dm=dm: e.matmul(
                        out=ps_bp[dm % 2][:], lhsT=w_CwT[:, dm * 128:(dm + 1) * 128],
                        rhs=t_bu[:], start=True, stop=True), inc=(s_pe, 1))
                    mmv = s_pe.n
                    if dm == NE - 1:
                        bp_last[key] = mmv
                    if dm >= 2:
                        P.wait("vector", s_ac, pw_done[(key, dm - 2)])
                    P.op("vector", lambda e, dm=dm, par=par: e.tensor_scalar(
                        out=t_expo[dm % 2][:], in0=t_iot[par][:],
                        scalar1=v_lnd[:, dm:dm + 1], scalar2=None, op0=ALU.mult))
                    ev = P.mark("vector", s_dv)
                    P.wait("scalar", s_dv, ev)
                    P.wait("scalar", s_ac, a_lnd)
                    P.op("scalar", lambda e, dm=dm: e.activation(
                        out=t_pw[dm % 2][:], in_=t_expo[dm % 2][:], func=AF.Exp))
                    pw_done[(key, dm)] = P.mark("scalar", s_ac)
                    P.wait("vector", s_ac, pw_done[(key, dm)])
                    P.wait("vector", s_pe, mmv)
                    if not pair:
                        if (key, dm - 2) in wst:
                            P.wait("vector", s_st, wst[(key, dm - 2)])
                        P.op("vector", lambda e, dm=dm: e.tensor_tensor(
                            out=t_wsb[dm % 2][:], in0=t_pw[dm % 2][:], in1=ps_bp[dm % 2][:],
                            op=ALU.mult))
                        w_consumed[(key, dm)] = P.mark("vector", s_dv)
                        P.wait("scalar", s_dv, w_consumed[(key, dm)])
                        P.dma("scalar", s_st,
                              d_wfm[dm * 128:(dm + 1) * 128, ch * 512:(ch + 1) * 512],
                              t_wsb[dm % 2][:])
                        wst[(key, dm)] = s_st.n
                    else:
                        P.op("vector", lambda e, dm=dm: e.tensor_tensor(
                            out=t_wpf[dm % 2][:], in0=t_pw[dm % 2][:], in1=ps_bp[dm % 2][:],
                            op=ALU.mult))
                        P.drain("vector")
                        P.op("vector", lambda e, dm=dm, ch=ch: e.reduce_sum(
                            out=v_pref[:, dm * 8 + ch:dm * 8 + ch + 1], in_=t_wpf[dm % 2][:],
                            axis=AX.X))
                        w_consumed[(key, dm)] = P.mark("vector", s_dv)
                if not pair:
                    for em in range(NE):
                        if em >= 2:
                            P.wait("tensor", s_ac, g_consumed[(key, em - 2)])
                        else:
                            P.wait("tensor", s_dv, w_consumed[(key, NE - 2 + em)])
                        for k in range(NE):
                            P.op("tensor", lambda e, em=em, k=k, par=par: e.matmul(
                                out=ps_bp[em % 2][:],
                                lhsT=w_GWT[:, k * D + em * 128:k * D + (em + 1) * 128],
                                rhs=t_anfm[par][:, k * 512:(k + 1) * 512],
                                start=(k == 0), stop=(k == NE - 1)),
                                inc=(s_pe, 1) if k == NE - 1 else None)
                        mmv = s_pe.n
                        P.wait("scalar", s_pe, mmv)
                        if (key, em - 2) in gst:
                            P.wait("scalar", s_st, gst[(key, em - 2)])
                        P.op("scalar", lambda e, em=em: e.activation(
                            out=t_gsb[em % 2][:], in_=ps_bp[em % 2][:], func=AF.Sigmoid,
                            bias=v_gb[:, em:em + 1]))
                        g_consumed[(key, em)] = P.mark("scalar", s_ac)
                        P.dma("scalar", s_st,
                              d_gfm[em * 128:(em + 1) * 128, ch * 512:(ch + 1) * 512],
                              t_gsb[em % 2][:])
                        gst[(key, em)] = s_st.n
                    anfm_free[key] = s_pe.n
                    bp_free[key] = [("ac", g_consumed[(key, NE - 2)]),
                                    ("ac", g_consumed[(key, NE - 1)])]
                else:
                    anfm_free[key] = bp_last[key]
                    bp_free[key] = [("dv", w_consumed[(key, NE - 2)]),
                                    ("dv", w_consumed[(key, NE - 1)])]

            for ch in range(8):
                attn_chunk(ch, pair=False)
            for ch in range(8):
                attn_chunk(ch, pair=True)
            v_wfm_all = s_st.n
            v_bu_last = bu_evac[attn_seq[-1]]
            v_bp_att_free = bp_free[attn_seq[-1]]

            for ch in range(8, 64):
                routing_chunk(ch)

            P.wait("scalar", s_dv, seg_evac[7])
            P.dma("scalar", s_st, d_rtm[:], r_tot[:])
            barA = {}
            barA["pe"] = P.mark("tensor", s_pe)
            barA["dv"] = P.mark("vector", s_dv)
            barA["ac"] = P.mark("scalar", s_ac)
            barA["gp"] = P.mark("gpsimd", s_gp)
            barA["ld"] = s_ld.n
            barA["st"] = s_st.n
            barA["gq"] = v_scat_all
            barA["sc"] = s_sc.n

        with contextlib.ExitStack() as Bk:
            ts_w = sb(Bk, "ts_w", [128, 4096], BF16)
            ts_g = sb(Bk, "ts_g", [128, 4096], BF16)
            ts_h = sb(Bk, "ts_h", [128, 512], F32)
            ts_s = [sb(Bk, f"ts_s{p}", [128, 512], BF16) for p in range(2)]
            ts_pc = sb(Bk, "ts_pc", [128, 1], F32)
            ts_cy = sb(Bk, "ts_cy", [128, 1], F32)
            t_W1s = sb(Bk, "t_W1s", [128, HD * NE], BF16)
            t_W2s = sb(Bk, "t_W2s", [128, 1024 * 32], BF16)
            t_xqs = sb(Bk, "t_xqs", [128, L * NE], BF16, is_transpose_dest=True)
            t_hsb = sb(Bk, "t_hsb", [128, 384 * 32], BF16)
            t_ysb = [sb(Bk, f"t_ysb{p}", [128, 512], BF16) for p in range(2)]
            t_b1c = [sb(Bk, f"t_b1c{p}", [128, 32], F32) for p in range(2)]
            t_b2r = sb(Bk, "t_b2r", [1, D], BF16)
            t_b2rep = [sb(Bk, f"t_b2rep{p}", [128, D], BF16) for p in range(2)]
            ps_h = [Bk.enter_context(nc.psum_tensor(f"ps_h{i}", [128, 384], F32))
                    for i in range(4)]
            ps_y = [Bk.enter_context(nc.psum_tensor(f"ps_y{i}", [128, 512], F32))
                    for i in range(2)]
            ps_mb = Bk.enter_context(nc.psum_tensor("ps_mb", [128, 512], F32))

            P = Prog()
            progs.append(P)
            for eng in ENGS:
                P.wait(eng, s_pe, barA["pe"])
                P.wait(eng, s_dv, barA["dv"])
                P.wait(eng, s_ac, barA["ac"])
                P.wait(eng, s_gp, barA["gp"])
                P.wait(eng, s_ld, barA["ld"])
                P.wait(eng, s_st, barA["st"])
                P.wait(eng, s_gq, barA["gq"])
                P.wait(eng, s_sc, barA["sc"])

            c2v = P.mark("vector", s_dv)
            P.wait("scalar", s_dv, c2v)

            w1_loaded = {}
            w2_loaded = {}
            mm1_done = {}
            mm2_done = {}
            xq_loaded = {}
            bprep_done = {}
            bfree_ac = {}
            bfree_dv = {}
            prev_hq01 = [0, 0, 0, 0]
            prev_yv01 = [0, 0]
            prev_yst01 = [0, 0]

            def emit_w1loads(e):
                if e >= 1:
                    P.wait("sync", s_pe, mm1_done[e - 1])
                P.wait("sync", s_gq, wpc[e])
                P.wait("sync", s_gp, wpc_dr[e])
                P.dma("sync", s_ld, t_W1s[:], d_w1b[e * 128:(e + 1) * 128, :])
                w1_loaded[e] = s_ld.n

            def emit_w2loads(e):
                if e >= 1:
                    P.wait("scalar", s_pe, mm2_done[e - 1])
                P.wait("scalar", s_gp, wpc_dr[e])
                P.dma("scalar", s_st, t_W2s[:], d_w2b[e * 128:(e + 1) * 128, :])
                w2_loaded[e] = s_st.n

            emit_w1loads(0)
            emit_w2loads(0)

            def emit_bprep(e):
                bp = e % 2
                if e == 0:
                    P.wait("scalar", s_gq, holders["v_b2cast"])
                    P.wait("scalar", s_gp, holders["v_prol_dr"])
                if e >= 1:
                    P.wait("scalar", s_pe, bprep_bcast[e - 1])
                P.dma("scalar", s_st, t_b1c[bp][:], d_b1[e * 128:(e + 1) * 128, :])
                P.dma("scalar", s_st, t_b2r[:], d_b2b[e:e + 1, :])
                bprep_b1[e] = s_st.n
                P.wait("tensor", s_st, bprep_b1[e])
                if e >= 1:
                    P.wait("tensor", s_dv, bprep_done[e - 1])
                if e == 0:
                    P.op("vector", lambda e_: e_.tensor_copy(out=c_onrb[:], in_=c_onr[:]))
                    dv0 = P.mark("vector", s_dv)
                    P.wait("tensor", s_dv, dv0)
                for half in range(2):
                    pv = P.op("tensor", lambda e_, half=half: e_.matmul(
                        out=ps_mb[:], lhsT=c_onrb[:],
                        rhs=t_b2r[0:1, half * 512:(half + 1) * 512],
                        start=True, stop=True), inc=(s_pe, 1))
                    P.wait("vector", s_pe, pv)
                    P.op("vector", lambda e_, half=half, bp=bp: e_.tensor_copy(
                        out=t_b2rep[bp][:, half * 512:(half + 1) * 512], in_=ps_mb[:]))
                    dv = P.mark("vector", s_dv)
                    if half == 0:
                        P.wait("tensor", s_dv, dv)
                bprep_bcast[e] = pv
                bprep_done[e] = dv

            def emit_xqloads(e):
                if e >= 1:
                    P.wait("sync", s_pe, mm1_done[e - 1])
                for k in range(NE):
                    P.tdma("sync", s_ld, t_xqs[:, k * L:(k + 1) * L],
                           d_xq[e * L:(e + 1) * L, k * 128:(k + 1) * 128])
                xq_loaded[e] = s_ld.n

            ps_aux = Bk.enter_context(nc.psum_tensor("ps_aux", [128, 128], F32))

            def emit_opref_a():
                halves = 0
                for hh in range(2):
                    if hh == 1:
                        P.wait("sync", s_dv, halves)
                    P.dma("sync", s_ld, t_rt128[:], d_rtm[hh * 128:(hh + 1) * 128, :])
                    v_a1 = s_ld.n
                    P.wait("tensor", s_ld, v_a1)
                    if hh == 1:
                        P.wait("tensor", s_dv, halves)
                    pv = P.op("tensor", lambda e_: e_.transpose(
                        out=ps_aux[0:8, 0:128], in_=t_rt128[:], identity=c_id[:, 0:128]),
                        inc=(s_pe, 1))
                    P.wait("vector", s_pe, pv)
                    P.op("vector", lambda e_, hh=hh: e_.tensor_copy(
                        out=rt_bin[:, hh * 128:(hh + 1) * 128], in_=ps_aux[0:8, 0:128]))
                    halves = P.mark("vector", s_dv)
                P.op("vector", lambda e_: e_.tensor_tensor(out=t_pvmul[:], in0=rt_bin[:],
                                                           in1=c_pvm[:], op=ALU.mult))
                P.drain("vector")
                P.op("vector", lambda e_: e_.reduce_sum(out=t_opcol[:], in_=t_pvmul[:],
                                                        axis=AX.X))
                vb = P.mark("vector", s_dv)
                P.wait("tensor", s_dv, vb)
                pv = P.op("tensor", lambda e_: e_.matmul(out=ps_aux[0:1, 0:8], lhsT=t_opcol[:],
                                                         rhs=c_id[0:8, 0:8], start=True,
                                                         stop=True), inc=(s_pe, 1))
                P.wait("vector", s_pe, pv)
                P.op("vector", lambda e_: e_.tensor_copy(out=t_oprow[:], in_=ps_aux[0:1, 0:8]))
                vc = P.mark("vector", s_dv)
                P.wait("tensor", s_dv, vc)
                pv = P.op("tensor", lambda e_: e_.matmul(out=ps_aux[:, 0:8], lhsT=c_onr[:],
                                                         rhs=t_oprow[:], start=True, stop=True),
                          inc=(s_pe, 1))
                P.wait("vector", s_pe, pv)
                P.op("vector", lambda e_: e_.tensor_copy(out=r_opref[:], in_=ps_aux[:, 0:8]))
                return P.mark("vector", s_dv)

            def emit_p3w_act(v_opref):
                P.wait("scalar", s_dv, v_opref)
                for T in range(32):
                    P.op("scalar", lambda e_, T=T: e_.activation(
                        out=t_g1c[:, T:T + 1], in_=pk_zd[:, T:T + 1], func=AF.Sigmoid,
                        scale=pk_rs[:, T:T + 1]))
                a_g1 = P.mark("scalar", s_ac)
                P.wait("vector", s_ac, a_g1)
                P.op("vector", lambda e_: e_.tensor_scalar(out=t_g2c[:], in0=t_g1c[:],
                                                           scalar1=-1.0, scalar2=1.0,
                                                           op0=ALU.mult, op1=ALU.add))
                P.drain("vector")

            def emit_p3w_batch(b8):
                for T in range(b8 * 8, (b8 + 1) * 8):
                    P.op("vector", lambda e_, T=T: e_.tensor_scalar(
                        out=t_oh1[:, (T % 4) * 8:(T % 4) * 8 + 8], in0=c_io8[:],
                        scalar1=pk_i1[:, T:T + 1], scalar2=None, op0=ALU.is_equal))
                    P.op("vector", lambda e_, T=T: e_.tensor_scalar(
                        out=t_oh2[:, (T % 4) * 8:(T % 4) * 8 + 8], in0=c_io8[:],
                        scalar1=pk_i2[:, T:T + 1], scalar2=None, op0=ALU.is_equal))
                    if T % 4 == 3:
                        P.drain("vector")
                        for Q in range(T - 3, T + 1):
                            P.op("vector", lambda e_, Q=Q: e_.tensor_tensor(
                                out=t_tmp8[:, (Q % 4) * 8:(Q % 4) * 8 + 8], in0=r_opref[:],
                                in1=t_oh1[:, (Q % 4) * 8:(Q % 4) * 8 + 8], op=ALU.mult))
                            P.op("vector", lambda e_, Q=Q: e_.tensor_tensor(
                                out=t_tmp8[:, 32 + (Q % 4) * 8:40 + (Q % 4) * 8], in0=r_opref[:],
                                in1=t_oh2[:, (Q % 4) * 8:(Q % 4) * 8 + 8], op=ALU.mult))
                        P.drain("vector")
                        for Q in range(T - 3, T + 1):
                            P.op("vector", lambda e_, Q=Q: e_.reduce_sum(
                                out=t_osel[:, (Q % 8) * 2:(Q % 8) * 2 + 1],
                                in_=t_tmp8[:, (Q % 4) * 8:(Q % 4) * 8 + 8], axis=AX.X))
                            P.op("vector", lambda e_, Q=Q: e_.reduce_sum(
                                out=t_osel[:, (Q % 8) * 2 + 1:(Q % 8) * 2 + 2],
                                in_=t_tmp8[:, 32 + (Q % 4) * 8:40 + (Q % 4) * 8], axis=AX.X))
                        P.drain("vector")
                        for Q in range(T - 3, T + 1):
                            for kk, pk_p in ((0, pk_p1), (1, pk_p2)):
                                col = (Q % 8) * 2 + kk
                                P.op("vector", lambda e_, Q=Q, col=col, pk_p=pk_p: e_.tensor_tensor(
                                    out=t_iv[:, col:col + 1], in0=pk_p[:, Q:Q + 1],
                                    in1=t_osel[:, col:col + 1], op=ALU.add))
                                P.op("vector", lambda e_, Q=Q, col=col, pk_p=pk_p: e_.tensor_scalar(
                                    out=t_vm[:, col:col + 1], in0=pk_p[:, Q:Q + 1],
                                    scalar1=float(L), scalar2=None, op0=ALU.is_lt))
                        P.drain("vector")
                        for Q in range(T - 3, T + 1):
                            for kk in range(2):
                                col = (Q % 8) * 2 + kk
                                P.op("vector", lambda e_, col=col: e_.tensor_scalar(
                                    out=t_kv[:, col:col + 1], in0=t_iv[:, col:col + 1],
                                    scalar1=CAP, scalar2=None, op0=ALU.is_lt))
                        P.drain("vector")
                        for Q in range(T - 3, T + 1):
                            for kk in range(2):
                                col = (Q % 8) * 2 + kk
                                P.op("vector", lambda e_, col=col: e_.tensor_tensor(
                                    out=t_kv[:, col:col + 1], in0=t_kv[:, col:col + 1],
                                    in1=t_vm[:, col:col + 1], op=ALU.mult))
                        P.drain("vector")
                        for Q in range(T - 3, T + 1):
                            for kk, gcol, pk_w in ((0, t_g1c, pk_w1), (1, t_g2c, pk_w2)):
                                col = (Q % 8) * 2 + kk
                                P.op("vector",
                                     lambda e_, Q=Q, col=col, gcol=gcol, pk_w=pk_w: e_.tensor_tensor(
                                         out=pk_w[:, Q:Q + 1], in0=gcol[:, Q:Q + 1],
                                         in1=t_kv[:, col:col + 1], op=ALU.mult))
                            for kk, pk_i, pk_p in ((0, pk_i1, pk_p1), (1, pk_i2, pk_p2)):
                                col = (Q % 8) * 2 + kk
                                P.op("vector",
                                     lambda e_, Q=Q, col=col, pk_i=pk_i, pk_p=pk_p:
                                     e_.scalar_tensor_tensor(
                                         out=t_iv[:, col:col + 1], in0=pk_i[:, Q:Q + 1],
                                         scalar=float(L), in1=pk_p[:, Q:Q + 1],
                                         op0=ALU.mult, op1=ALU.add))
                        P.drain("vector")
                        for Q in range(T - 3, T + 1):
                            for kk in range(2):
                                col = (Q % 8) * 2 + kk
                                P.op("vector", lambda e_, col=col: e_.select(
                                    out=t_kv[:, col:col + 1], mask=t_vm[:, col:col + 1],
                                    on_true=t_iv[:, col:col + 1], on_false=c_bigf[:]))
                        P.drain("vector")
                        for Q in range(T - 3, T + 1):
                            for kk, pk_g in ((0, pk_g1), (1, pk_g2)):
                                col = (Q % 8) * 2 + kk
                                P.op("vector", lambda e_, Q=Q, col=col, pk_g=pk_g: e_.tensor_copy(
                                    out=pk_g[:, Q:Q + 1], in_=t_kv[:, col:col + 1]))
                        P.drain("vector")

            scan_sst = {}
            slab_ld = {}
            scan_done = {}
            scan_half = {}

            def emit_scan_loads(dm):
                if dm >= 1:
                    P.wait("scalar", s_dv, scan_done[dm - 1])
                P.dma("scalar", s_st, ts_w[:], d_wfm[dm * 128:(dm + 1) * 128, :])
                P.dma("scalar", s_st, ts_g[:], d_gfm[dm * 128:(dm + 1) * 128, :])
                slab_ld[dm] = s_st.n

            def emit_scan_dm(dm):
                P.op("vector", lambda e_, dm=dm: e_.reduce_sum(
                    out=ts_pc[:], in_=v_pref[:, dm * 8:(dm + 1) * 8], axis=AX.X))
                P.drain("vector")
                P.op("vector", lambda e_: e_.tensor_copy(out=ts_cy[:], in_=ts_pc[:]))
                P.drain("vector")
                P.wait("vector", s_st, slab_ld[dm])
                for qq in range(8):
                    if (dm, qq - 2) in scan_sst:
                        P.wait("vector", s_st, scan_sst[(dm, qq - 2)])
                    elif dm >= 1 and (dm - 1, qq + 6) in scan_sst:
                        P.wait("vector", s_st, scan_sst[(dm - 1, qq + 6)])
                    P.op("vector", lambda e_: e_.tensor_copy(out=ts_h[:, 0:1], in_=ts_cy[:]))
                    P.op("vector", lambda e_, qq=qq: e_.tensor_tensor_scan(
                        out=ts_h[:, 1:512], data0=ts_w[:, qq * 512:qq * 512 + 511],
                        data1=ts_w[:, qq * 512:qq * 512 + 511],
                        initial=ts_cy[:], op0=ALU.add, op1=ALU.bypass))
                    P.drain("vector")
                    P.op("vector", lambda e_, qq=qq: e_.tensor_tensor(
                        out=ts_cy[:], in0=ts_h[:, 511:512],
                        in1=ts_w[:, qq * 512 + 511:qq * 512 + 512], op=ALU.add))
                    P.op("vector", lambda e_, qq=qq: e_.tensor_tensor(
                        out=ts_s[qq % 2][:], in0=ts_g[:, qq * 512:(qq + 1) * 512],
                        in1=ts_h[:], op=ALU.mult))
                    sd = P.mark("vector", s_dv)
                    P.wait("scalar", s_dv, sd)
                    P.dma("scalar", s_st,
                          d_sfm[dm * 128:(dm + 1) * 128, qq * 512:(qq + 1) * 512],
                          ts_s[qq % 2][:])
                    scan_sst[(dm, qq)] = s_st.n
                scan_done[dm] = P.mark("vector", s_dv)

            bprep_bcast = {}
            bprep_b1 = {}
            emit_bprep(0)
            emit_xqloads(0)
            for e in range(_NEXP):
                bp = e % 2

                for j in range(_NTIL):
                    base = j * 384
                    if j == 0:
                        emit_scan_loads(e)
                    P.wait("tensor", s_ld, w1_loaded[e])
                    P.wait("tensor", s_ld, xq_loaded[e])
                    hq = {}
                    for m in range(32):
                        if m >= 4:
                            P.wait("tensor", s_ac, hq[m - 4])
                        else:
                            P.wait("tensor", s_ac, prev_hq01[m])
                        for k in range(NE):
                            P.op("tensor", lambda e_, m=m, k=k, base=base: e_.matmul(
                                out=ps_h[m % 4][:],
                                lhsT=t_W1s[:, k * HD + m * 128:k * HD + (m + 1) * 128],
                                rhs=t_xqs[:, k * L + base:k * L + base + 384],
                                start=(k == 0), stop=(k == NE - 1)),
                                inc=(s_pe, 1) if k == NE - 1 else None)
                        mmv = s_pe.n
                        P.wait("scalar", s_pe, mmv)
                        if m == 0 and j == 0:
                            P.wait("scalar", s_st, bprep_b1[e])
                        hq[m] = P.op("scalar", lambda e_, m=m, bp=bp: e_.activation(
                            out=t_hsb[:, m * 384:(m + 1) * 384], in_=ps_h[m % 4][:],
                            func=AF.Gelu_apprx_tanh, bias=t_b1c[bp][:, m:m + 1]),
                            inc=(s_ac, 1))
                    prev_hq01 = [hq[28], hq[29], hq[30], hq[31]]
                    if j == 0 and e < _NEXP - 1:
                        emit_bprep(e + 1)
                    if j == 0:
                        for ee in range(e + 1, min(e + 3, _NEXP)):
                            if ee not in holders["wpc"]:
                                holders["emit_wprecast"](ee)
                    if j == _NTIL - 1:
                        mm1_done[e] = s_pe.n
                        if e < _NEXP - 1:
                            emit_w1loads(e + 1)
                            emit_xqloads(e + 1)
                    P.wait("tensor", s_st, w2_loaded[e])
                    yv_inc = {}
                    yst = {}
                    for g in range(6):
                        m2, dh = divmod(g, 2)
                        if g >= 2:
                            P.wait("tensor", s_dv, yv_inc[g - 2])
                        else:
                            P.wait("tensor", s_dv, prev_yv01[g])
                        for k in range(32):
                            if g == 0:
                                P.wait("tensor", s_ac, hq[k])
                            P.op("tensor", lambda e_, m2=m2, dh=dh, k=k, g=g: e_.matmul(
                                out=ps_y[g % 2][:],
                                lhsT=t_hsb[:, k * 384 + m2 * 128:k * 384 + (m2 + 1) * 128],
                                rhs=t_W2s[:, k * 1024 + dh * 512:k * 1024 + (dh + 1) * 512],
                                start=(k == 0), stop=(k == 31)),
                                inc=(s_pe, 1) if k == 31 else None)
                        mmv = s_pe.n
                        P.wait("vector", s_pe, mmv)
                        if g >= 2:
                            P.wait("vector", s_ld, yst[g - 2])
                        else:
                            P.wait("vector", s_ld, prev_yst01[g])
                        yv_inc[g] = P.op("vector", lambda e_, g=g, dh=dh, bp=bp: e_.tensor_tensor(
                            out=t_ysb[g % 2][:], in0=ps_y[g % 2][:],
                            in1=t_b2rep[bp][:, dh * 512:(dh + 1) * 512], op=ALU.add),
                            inc=(s_dv, 1))
                        P.wait("sync", s_dv, yv_inc[g])
                        P.dma("sync", s_ld,
                              d_yq[e * L + base + m2 * 128: e * L + base + (m2 + 1) * 128,
                                   dh * 512:(dh + 1) * 512],
                              t_ysb[g % 2][:])
                        yst[g] = s_ld.n
                    prev_yv01 = [yv_inc[4], yv_inc[5]]
                    prev_yst01 = [yst[4], yst[5]]
                    if j == _NTIL - 1:
                        mm2_done[e] = s_pe.n
                        if e < _NEXP - 1:
                            emit_w2loads(e + 1)
                bfree_ac[e] = s_ac.n
                bfree_dv[e] = s_dv.n
                if e == 0:
                    holders["v_opref"] = emit_opref_a()
                    emit_p3w_act(holders["v_opref"])
                elif 1 <= e <= 4:
                    emit_p3w_batch(e - 1)
                emit_scan_dm(e)

            barB = {}
            barB["pe"] = P.mark("tensor", s_pe)
            barB["dv"] = P.mark("vector", s_dv)
            barB["ac"] = P.mark("scalar", s_ac)
            barB["gp"] = P.mark("gpsimd", s_gp)
            barB["ld"] = s_ld.n
            barB["st"] = s_st.n
            barB["gq"] = v_scat_all
            holders["v_scat_all"] = v_scat_all
            barB["sc"] = s_sc.n

        with contextlib.ExitStack() as Ck:
            t_stm = [sb(Ck, f"t_stm{p}", [128, D], BF16, is_transpose_dest=True)
                     for p in range(2)]
            t_xr = [sb(Ck, f"t_xr{p}", [128, D], F32) for p in range(2)]
            t_A = [sb(Ck, f"t_A{p}", [128, D], BF16) for p in range(2)]
            t_Bt = [sb(Ck, f"t_Bt{p}", [128, D], BF16) for p in range(2)]
            t_f1 = [sb(Ck, f"t_f1{p}", [128, D], F32) for p in range(2)]
            t_f2 = [sb(Ck, f"t_f2{p}", [128, D], F32) for p in range(2)]
            t_g3 = [sb(Ck, f"t_g3{p}", [128, D], F32) for p in range(2)]
            t_sg3 = [sb(Ck, f"t_sg3{p}", [128, D], F32) for p in range(2)]
            t_o3 = [sb(Ck, f"t_o3{p}", [128, D], F32) for p in range(2)]

            P = Prog()
            progs.append(P)
            for eng in ENGS:
                P.wait(eng, s_pe, barB["pe"])
                P.wait(eng, s_dv, barB["dv"])
                P.wait(eng, s_ac, barB["ac"])
                P.wait(eng, s_gp, barB["gp"])
                P.wait(eng, s_ld, barB["ld"])
                P.wait(eng, s_st, barB["st"])
                P.wait(eng, s_gq, barB["gq"])
                P.wait(eng, s_sc, barB["sc"])

            for tile in (t_A, t_Bt):
                for p in range(2):
                    P.op("vector", lambda e_, tile=tile, p=p: e_.memset(tile[p][:], 0.0))
            v_cz = P.mark("vector", s_dv)
            P.wait("gpsimd", s_dv, v_cz)

            f_done = {}
            g_done = {}
            sg_done = {}
            o_done = {}
            st_done = {}
            for T in range(32):
                par = T % 2
                if T >= 2:
                    P.wait("sync", s_dv, sg_done[T - 2])
                    P.wait("sync", s_gp, o_done[T - 2])
                P.tdma("sync", s_ld, t_stm[par][:], d_sfm[:, T * 128:(T + 1) * 128])
                P.dma("sync", s_ld, t_xr[par][:], d_x[T * 128:(T + 1) * 128, :])
                ld_T = s_ld.n
                if T >= 2:
                    P.wait("gpsimd", s_dv, f_done[T - 2])
                for kk, pkg in ((0, pk_g1), (1, pk_g2)):
                    tile = t_A if kk == 0 else t_Bt
                    P.op("gpsimd", lambda e_, T=T, pkg=pkg, tile=tile, par=par:
                         e_.indirect_dma_start(
                             out=tile[par][:], out_offset=None, in_=d_yq[:],
                             in_offset=IndirectOffsetOnAxis(ap=pkg[:, T:T + 1], axis=0),
                             bounds_check=bc_reg["r"], oob_is_err=False), inc=(s_gq, 16))
                gath = s_gq.n
                P.wait("vector", s_gq, gath)
                if T >= 2:
                    P.wait("vector", s_ac, g_done[T - 2])
                P.op("vector", lambda e_, T=T, par=par: e_.tensor_scalar(
                    out=t_f1[par][:], in0=t_A[par][:], scalar1=pk_w1[:, T:T + 1],
                    scalar2=None, op0=ALU.mult))
                P.drain("vector")
                f_done[T] = P.op("vector", lambda e_, T=T, par=par: e_.scalar_tensor_tensor(
                    out=t_f2[par][:], in0=t_Bt[par][:], scalar=pk_w2[:, T:T + 1],
                    in1=t_f1[par][:], op0=ALU.mult, op1=ALU.add), inc=(s_dv, 1))
                P.wait("scalar", s_dv, f_done[T])
                if T >= 2:
                    P.wait("scalar", s_dv, sg_done[T - 2])
                g_done[T] = P.op("scalar", lambda e_, T=T, par=par: e_.activation(
                    out=t_g3[par][:], in_=t_f2[par][:], func=AF.Silu), inc=(s_ac, 1))
                P.wait("vector", s_ac, g_done[T])
                P.wait("vector", s_ld, ld_T)
                if T >= 2:
                    P.wait("vector", s_gp, o_done[T - 2])
                sg_done[T] = P.op("vector", lambda e_, T=T, par=par: e_.tensor_tensor(
                    out=t_sg3[par][:], in0=t_g3[par][:], in1=t_stm[par][:], op=ALU.mult),
                    inc=(s_dv, 1))
                P.wait("gpsimd", s_dv, sg_done[T])
                if T >= 2:
                    P.wait("gpsimd", s_st, st_done[T - 2])
                o_done[T] = P.op("gpsimd", lambda e_, T=T, par=par: e_.tensor_tensor(
                    out=t_o3[par][:], in0=t_sg3[par][:], in1=t_xr[par][:], op=ALU.add),
                    inc=(s_gp, 1))
                P.wait("scalar", s_gp, o_done[T])
                P.dma("scalar", s_st, d_out[T * 128:(T + 1) * 128, :], t_o3[par][:])
                st_done[T] = s_st.n
            P.wait("scalar", s_st, st_done[31])
    emit_block(nc, progs)

    simulate(progs)
    return nc


_NC_CACHE = [None]


def kernel(L0_attn_x, L0_attn_B_w, L0_attn_C_w, L0_attn_gate_w, L0_attn_gate_b,
           L0_attn_log_decay, L0_ffn_Wg, L0_ffn_W1, L0_ffn_b1, L0_ffn_W2, L0_ffn_b2):
    xf = np.ascontiguousarray(np.asarray(L0_attn_x, np.float32).reshape(TALL, D))
    Bw = np.ascontiguousarray(np.asarray(L0_attn_B_w, np.float32))
    Cw = np.ascontiguousarray(np.asarray(L0_attn_C_w, np.float32).reshape(NE, 128, SDIM).transpose(1, 0, 2).reshape(128, NE * SDIM))
    GW = np.ascontiguousarray(np.asarray(L0_attn_gate_w, np.float32))
    gb = np.ascontiguousarray(np.asarray(L0_attn_gate_b, np.float32).reshape(NE, 128).T)
    ldec = np.ascontiguousarray(np.asarray(L0_attn_log_decay, np.float32).reshape(NE, 128).T)
    Wg = np.ascontiguousarray(np.asarray(L0_ffn_Wg, np.float32).reshape(NE, 128, NE).transpose(1, 0, 2).reshape(128, NE * NE))
    W1 = np.ascontiguousarray(np.asarray(L0_ffn_W1, np.float32).reshape(NE, NE, 128, HD).transpose(0, 2, 1, 3).reshape(NE * 128, NE * HD))
    b1 = np.ascontiguousarray(np.asarray(L0_ffn_b1, np.float32).reshape(NE, 32, 128).transpose(0, 2, 1).reshape(NE * 128, 32))
    W2 = np.ascontiguousarray(np.asarray(L0_ffn_W2, np.float32).reshape(NE, 32, 128, D).transpose(0, 2, 1, 3).reshape(NE * 128, 32 * D))
    b2 = np.ascontiguousarray(np.asarray(L0_ffn_b2, np.float32))

    Lx = np.triu(np.ones((128, 128), np.float32), 1)
    ident = np.eye(128, dtype=np.float32)
    onesr = np.ones((1, 128), np.float32)
    iota8 = np.tile(np.arange(NE, dtype=np.float32), (128, 1))
    iotaP = np.ascontiguousarray(np.tile(np.arange(TOK, dtype=np.float32), (128, 1)))
    bigio = ((1 << 20) + np.arange(128, dtype=np.uint32)).reshape(128, 1)

    if _NC_CACHE[0] is None:
        _NC_CACHE[0] = build_nc()
    nc = _NC_CACHE[0]

    in_maps = []
    for c in range(NCORES):
        xrot = np.ascontiguousarray(np.concatenate([xf[c * TOK:], xf[:c * TOK]], axis=0))
        xrotT = np.ascontiguousarray(xrot.T)
        if c % 2 == 1:
            xpair = np.ascontiguousarray(xf[(c - 1) * TOK: c * TOK])
        else:
            xpair = np.zeros((TOK, D), np.float32)
        half = c % 2
        iotaT = np.ascontiguousarray(
            np.tile(np.arange(TOK, dtype=np.float32) + half * TOK, (128, 1)))
        owner = (c + (np.arange(256) // 32)) % NCORES
        pvm = np.ascontiguousarray(
            np.tile((owner < c).astype(np.float32), (NE, 1)))
        in_maps.append({
            "x": xrot, "xT": xrotT, "xpair": xpair, "Bw": Bw, "Cw": Cw, "GW": GW, "gb": gb,
            "ldec": ldec, "Wg": Wg, "W1": W1, "b1": b1, "W2": W2, "b2": b2,
            "Lx": Lx, "ident": ident, "onesr": onesr, "iota8": iota8,
            "iotaT": iotaT, "iotaP": iotaP, "bigio": bigio, "pvm": pvm,
        })

    res = run_bass_kernel_spmd(nc, in_maps, core_ids=list(range(NCORES)))
    out = np.concatenate([res.results[c]["out"] for c in range(NCORES)], axis=0)
    return out.reshape(4, 8192, 1024).astype(np.float32)
